# version 01
# baseline (speedup 1.0000x reference)
import contextlib

import numpy as np
import concourse.bass as bass
import concourse.mybir as mybir
from concourse.bass_utils import run_bass_kernel_spmd

F32 = mybir.dt.float32
BF16 = mybir.dt.bfloat16
AF = mybir.ActivationFunctionType
ALU = mybir.AluOpType
AX = mybir.AxisListType

MAGIC = 12582912.0
T, C, HID, NH, HD = 2048, 1024, 4096, 16, 64
NCC = 8
NHC = 32
NB, BT, BCH = 8, 256, 2
NE, ET = 8, 256
LN_EPS, GN_EPS = 1e-5, 0.00064
MEAN_W1 = 1.0 / (HID * C)
MEAN_W2 = 1.0 / (C * HID)


def build_nc():
    nc = bass.Bass("TRN2", target_bir_lowering=False)

    def din(name, shape, dt=F32):
        return nc.dram_tensor(name, list(shape), dt, kind="ExternalInput").ap()

    x_h = din("x", (T, C))
    wr_h, wk_h, wv_h, wo_h = (din(n, (C, C)) for n in ("wr", "wk", "wv", "wo"))
    maa_h = {f: din("maa_" + f, (C,)) for f in ("x", "w", "k", "v", "r")}
    mw1_h = din("mw1", (C, 128))
    mw2_h = din("mw2", (4, 32, C))
    dec_h = din("dec", (C,))
    dw1_h = din("dw1", (C, 64))
    dw2_h = din("dw2", (64, C))
    faaaa_h = din("faaaa", (NH, HD))
    lnw_h, lnb_h = din("lnw", (C,)), din("lnb", (C,))
    w1_h = din("w1", (HID, C))
    cw_h = din("cw", (HID, 1, 3))
    cb_h = din("cb", (HID,))
    w2_h = din("w2", (C, HID))

    out_h = nc.dram_tensor("out", [T, C], F32, kind="ExternalOutput").ap()
    n1_h = nc.dram_tensor("n1s", [T, C], BF16, kind="Internal").ap()
    t1q_h = nc.dram_tensor("t1qs", [HID, C], BF16, kind="Internal").ap()
    n2_h = nc.dram_tensor("n2s", [T, HID], BF16, kind="Internal").ap()
    t2q_h = nc.dram_tensor("t2qs", [C, HID], BF16, kind="Internal").ap()
    xp_h = nc.dram_tensor("xps", [T, C], F32, kind="Internal").ap()

    def mk_sb(stack):
        def sb(name, shape, dt=F32):
            return stack.enter_context(nc.sbuf_tensor(name, list(shape), dt)).ap()
        return sb

    perm = contextlib.ExitStack()
    sb = mk_sb(perm)

    IDENT = sb("ident", (128, 128))
    MSK2 = sb("msk2", (128, 256))
    ONEC = sb("onec", (128, 1))
    ONER = sb("oner", (1, 128))
    IND2 = sb("ind2", (128, 2))
    NH5 = sb("nh5", (128, 1))
    PH5 = sb("ph5", (128, 1))
    ZER = sb("zer", (128, 128))
    TAP = [sb("tap%d" % i, (128, NHC)) for i in range(3)]
    CBB = sb("cbpp", (128, NHC))
    S1c = sb("s1c", (128, 1))
    S2c = sb("s2c", (128, 1))
    MUc = sb("muc", (128, 1))
    RSc = sb("rsc", (128, 1))
    NBc = sb("nbc", (128, 1))
    QS1 = sb("qs1", (128, 16))
    QS2 = sb("qs2", (128, 16))
    INVS2 = sb("invs2", (128, 16))
    SC16 = sb("sc16", (128, 16))
    RMX = sb("rmx", (128, 1))
    PL = sb("plx", (128, 128))
    Q1, Q2 = sb("q1", (128, 1)), sb("q2", (128, 1))
    SONE = sb("sone", (1, 1))
    PLv = PL.rearrange("p (a g) -> p a g", a=NCC)

    prog = []

    def op(eng, inc=1):
        def deco(fn):
            prog.append((eng, fn, inc))
        return deco

    def dma(out, in_, transpose=False):
        @op("sync", 16)
        def _(e, out=out, in_=in_, tr=transpose):
            if tr:
                return e.dma_start(out=out, in_=in_, transpose=True)
            return e.dma_start(out=out, in_=in_)

    def vec(fn):
        @op("vector")
        def _(e, fn=fn):
            return fn(e)

    def act(fn):
        @op("scalar")
        def _(e, fn=fn):
            return fn(e)

    def ten(fn):
        @op("tensor")
        def _(e, fn=fn):
            return fn(e)

    def gps(fn):
        @op("gpsimd")
        def _(e, fn=fn):
            return fn(e)

    vec(lambda e: e.memset(MSK2[:, 0:128], 1.0))
    gps(lambda e: e.affine_select(IDENT, MSK2[:, 0:128], [[-1, 128]], ALU.is_equal,
                                  0.0, base=0, channel_multiplier=1))
    gps(lambda e: e.iota(MSK2.rearrange("p (a b) -> p a b", a=2), [[0, 2], [1, 128]],
                         base=0, channel_multiplier=-1,
                         allow_small_or_imprecise_dtypes=True))
    vec(lambda e: e.tensor_single_scalar(MSK2, MSK2, 0.0, ALU.is_gt))
    vec(lambda e: e.memset(ONEC, 1.0))
    vec(lambda e: e.memset(ONER, 1.0))
    vec(lambda e: e.memset(NH5, -0.5))
    vec(lambda e: e.memset(PH5, 0.5))
    vec(lambda e: e.memset(ZER, 0.0))
    vec(lambda e: e.memset(IND2, 0.0))
    vec(lambda e: e.memset(IND2[0:64, 0:1], 1.0))
    vec(lambda e: e.memset(IND2[64:128, 1:2], 1.0))

    PSH = nc.alloc_psum_tensor("ps", [128, 4096], F32)
    PS = PSH.ap()

    def bank(k, w=512):
        return PS[:, 512 * k:512 * k + w]

    cw_stack = contextlib.ExitStack()
    sbc = mk_sb(cw_stack)
    CWS = sbc("cws", (32, 384))
    CBS = sbc("cbs", (32, 128))
    dma(CWS, cw_h.rearrange("(a b) o t -> a (b o t)", a=32))
    dma(CBS, cb_h.rearrange("(a b) -> a b", a=32))
    for tau in range(3):
        ten(lambda e, tau=tau: e.transpose(
            bank(0, 32)[:, 0:32],
            CWS.rearrange("a (b t) -> a b t", t=3)[:, :, tau], IDENT[0:32, 0:32]))
        act(lambda e, tau=tau: e.activation(TAP[tau], bank(0, 32)[:, 0:32], AF.Copy,
                                            bias=0.0, scale=1.0))
    ten(lambda e: e.transpose(bank(0, 32)[:, 0:32], CBS, IDENT[0:32, 0:32]))
    act(lambda e: e.activation(CBB, bank(0, 32)[:, 0:32], AF.Copy, bias=0.0,
                               scale=1.0))
    cw_stack.close()

    wq_stack = contextlib.ExitStack()
    sbw = mk_sb(wq_stack)
    WQT = sbw("wqt", (128, HID))
    WQA = sbw("wqa", (128, HID))
    WQB = sbw("wqb", (128, HID))
    WQO = sbw("wqo", (128, HID), BF16)
    WSUM = sbw("wsum", (128, 32))
    SW = sbw("sw", (128, 1))

    def wquant(w_hbm, ntile, cols, meanmul, qtile, tq_hbm):
        for wt in range(ntile):
            dma(WQT[:, 0:cols], w_hbm[128 * wt:128 * wt + 128, :])
            act(lambda e, wt=wt, cols=cols: e.activation(
                WQA[:, 0:cols], WQT[:, 0:cols], AF.Abs, bias=0.0, scale=1.0,
                accum_out=WSUM[:, wt:wt + 1]))
        vec(lambda e, ntile=ntile: e.reduce_sum(RMX, WSUM[:, 0:ntile], axis=AX.X))
        ten(lambda e: e.matmul(bank(0)[0:1, 0:1], RMX, ONEC, start=True, stop=True))
        act(lambda e: e.activation(SONE, bank(0)[0:1, 0:1], AF.Copy, bias=0.0,
                                   scale=1.0))
        vec(lambda e, mm=meanmul: e.tensor_scalar(SONE, SONE, mm, 1e-5, ALU.mult,
                                                  ALU.max))
        ten(lambda e: e.matmul(bank(0)[:, 0:1], ONER, SONE, start=True, stop=True))
        act(lambda e, qtile=qtile: e.activation(qtile, bank(0)[:, 0:1], AF.Copy,
                                                bias=0.0, scale=1.0))
        vec(lambda e, qtile=qtile: e.reciprocal(SW, qtile))
        for wt in range(ntile):
            dma(WQT[:, 0:cols], w_hbm[128 * wt:128 * wt + 128, :])
            act(lambda e, cols=cols: e.activation(WQA[:, 0:cols], WQT[:, 0:cols],
                                                  AF.Sign, bias=NH5, scale=SW))
            act(lambda e, cols=cols: e.activation(WQB[:, 0:cols], WQT[:, 0:cols],
                                                  AF.Sign, bias=PH5, scale=SW))
            vec(lambda e, cols=cols: e.tensor_add(WQO[:, 0:cols], WQA[:, 0:cols],
                                                  WQB[:, 0:cols]))
            dma(tq_hbm[128 * wt:128 * wt + 128, :], WQO[:, 0:cols])

    wquant(w1_h, 32, C, MEAN_W1, Q1, t1q_h)
    wquant(w2_h, 8, HID, MEAN_W2, Q2, t2q_h)
    wq_stack.close()

    attn_stack = contextlib.ExitStack()
    sba = mk_sb(attn_stack)
    MAA = {f: sba("maa" + f, (128, NCC)) for f in ("x", "w", "k", "v", "r")}
    DEC = sba("decpp", (128, NCC))
    UPP = sba("upp", (128, NCC))
    LNW = sba("lnwpp", (128, NCC))
    LNB = sba("lnbpp", (128, NCC))
    MW1 = sba("mw1t", (128, NCC * 128))
    MW2 = sba("mw2t", (128, NCC * 128))
    DW1 = sba("dw1t", (128, NCC * 64))
    DW2 = sba("dw2t", (128, NCC * 128))
    BIASR = sba("biasr", (1, C))
    XTI = sba("xti", (128, C))
    XNTI = sba("xnti", (128, C))
    QA = sba("qa", (128, C))
    N1B = sba("n1b", (128, C), BF16)
    XN_B = sba("xn_b", (128, NCC * BT))
    XX_B = sba("xx_b", (128, NCC * BT))
    XF_B = sba("xf_b", (128, NCC * BT))
    XPC = sba("xpc", (128, NCC))
    TSC = sba("tsc", (128, BT))
    X1 = sba("x1b", (128, BT))
    XD = sba("xdb", (128, BT))
    CB = sba("cbuf", (128, NCC * BT))
    SCB = sba("scb", (128, NCC * BT))
    PIB = sba("pib", (128, NCC * BT))
    KT_B = sba("kt_b", (128, NCC * BT))
    RT_B = sba("rt_b", (128, NCC * BT))
    ASCR = sba("ascr", (128, BT))
    V_B = sba("v_b", (128, BCH * C))
    KHF = sba("khf", (128, NCC * 128))
    KHTM = sba("khtm", (128, NCC * 128))
    DTMB = sba("dtmb", (128, BCH * 16))
    GM = sba("gm", (128, NH * 128))
    SBF = [sba("s0b", (128, NCC * HD)), sba("s1b", (128, NCC * HD))]
    YS = sba("ys", (128, C))
    YSQ = sba("ysq", (128, C))
    YN = sba("yn", (128, C))
    GN1, GN2 = sba("gn1", (128, 16)), sba("gn2", (128, 16))
    GMU, GRS = sba("gmu", (128, 16)), sba("grs", (128, 16))
    YNFB = sba("ynfb", (128, NCC * BT))
    WST = sba("wstr", (128, C))

    VST8 = sba("vst8", (64, 128))
    vec_targets = [(maa_h["x"], MAA["x"]), (maa_h["w"], MAA["w"]),
                   (maa_h["k"], MAA["k"]), (maa_h["v"], MAA["v"]),
                   (maa_h["r"], MAA["r"]), (dec_h, DEC), (lnw_h, LNW),
                   (lnb_h, LNB)]
    for src_h, tgt in vec_targets:
        dma(VST8[0:8, :], src_h.rearrange("(a b) -> a b", a=8))
        ten(lambda e: e.transpose(bank(2)[:, 0:8], VST8[0:8, :], IDENT[0:8, 0:8]))
        act(lambda e, tgt=tgt: e.activation(tgt, bank(2)[:, 0:8], AF.Copy,
                                            bias=0.0, scale=1.0))
    dma(VST8[0:16, 0:64], faaaa_h)
    ten(lambda e: e.transpose(bank(2)[0:64, 0:16], VST8[0:16, 0:64],
                              IDENT[0:16, 0:16]))
    act(lambda e: e.activation(VST8[0:64, 0:16], bank(2)[0:64, 0:16],
                               AF.Copy, bias=0.0, scale=1.0))
    for hp in range(2):
        vec(lambda e, hp=hp: e.tensor_copy(
            UPP[64 * hp:64 * hp + 64, :],
            VST8[0:64, 0:16].rearrange("p (c two) -> p c two", two=2)[:, :, hp]))
    dma(MW1.rearrange("p (a l) -> p a l", a=NCC), mw1_h.rearrange("(a b) l -> b a l", b=128))
    for f in range(4):
        dma(MW2[32 * f:32 * f + 32, :], mw2_h[f])
    dma(DW1.rearrange("p (a l) -> p a l", a=NCC), dw1_h.rearrange("(a b) l -> b a l", b=128))
    dma(DW2[0:64, :], dw2_h)
    vec(lambda e: e.memset(XPC, 0.0))
    vec(lambda e: e.memset(SBF[0], 0.0))

    for cc in range(NCC):
        dma(WST, wo_h[128 * cc:128 * cc + 128, :])
        for half in range(2):
            ten(lambda e, cc=cc, half=half: e.matmul(
                bank(half)[0:1, :], LNB[:, cc:cc + 1],
                WST[:, 512 * half:512 * half + 512],
                start=(cc == 0), stop=(cc == NCC - 1)))
    for half in range(2):
        act(lambda e, half=half: e.activation(
            BIASR[:, 512 * half:512 * half + 512], bank(half)[0:1, :], AF.Copy,
            bias=0.0, scale=1.0))

    def lora2_mm(f, cc):
        ten(lambda e, f=f, cc=cc: e.matmul(
            bank(6)[:, 0:BT], MW2[32 * f:32 * f + 32, 128 * cc:128 * cc + 128],
            X1[32 * f:32 * f + 32, :], start=True, stop=True,
            tile_position=(32 * f, 0)))

    def build_xf(f, mf):
        for cc in range(NCC):
            lora2_mm(f, cc)
            sl = slice(BT * cc, BT * cc + BT)
            vec(lambda e, cc=cc, sl=sl, mf=mf: e.scalar_tensor_tensor(
                TSC, bank(6)[:, 0:BT], MAA[mf][:, cc:cc + 1], XX_B[:, sl],
                ALU.add, ALU.mult))
            vec(lambda e, sl=sl: e.tensor_add(XF_B[:, sl], TSC, XN_B[:, sl]))

    for b in range(NB):
        for tl in range(BCH):
            gt = b * BCH + tl
            dma(XTI, x_h[128 * gt:128 * gt + 128, :])
            vec(lambda e: e.reduce_sum(S1c, XTI, axis=AX.X))
            act(lambda e: e.activation(QA, XTI, AF.Square, bias=0.0, scale=1.0,
                                       accum_out=S2c))
            vec(lambda e: e.tensor_single_scalar(MUc, S1c, 1.0 / C, ALU.mult))
            vec(lambda e: e.tensor_single_scalar(S2c, S2c, 1.0 / C, ALU.mult))
            vec(lambda e: e.tensor_mul(RSc, MUc, MUc))
            vec(lambda e: e.tensor_sub(S2c, S2c, RSc))
            vec(lambda e: e.tensor_single_scalar(S2c, S2c, LN_EPS, ALU.add))
            act(lambda e: e.activation(S2c, S2c, AF.Sqrt, bias=0.0, scale=1.0))
            vec(lambda e: e.reciprocal(RSc, S2c))
            vec(lambda e: e.scalar_tensor_tensor(NBc, MUc, -1.0, RSc, ALU.mult,
                                                 ALU.mult))
            vec(lambda e: e.tensor_scalar(XNTI, XTI, RSc, NBc, ALU.mult, ALU.add))
            vec(lambda e: e.reduce_max(RMX, XNTI, axis=AX.X,
                                       apply_absolute_value=True))
            vec(lambda e: e.tensor_single_scalar(RMX, RMX, 1e-5, ALU.max))
            vec(lambda e: e.reciprocal(RMX, RMX))
            vec(lambda e, gt=gt: e.tensor_single_scalar(QS1[:, gt:gt + 1], RMX,
                                                        127.0, ALU.mult))
            vec(lambda e, gt=gt: e.tensor_scalar(QA, XNTI, QS1[:, gt:gt + 1],
                                                 MAGIC, ALU.mult, ALU.add))
            vec(lambda e: e.tensor_single_scalar(N1B, QA, MAGIC, ALU.subtract))
            dma(n1_h[128 * gt:128 * gt + 128, :], N1B)
            for cc in range(NCC):
                pb = cc % 2
                ten(lambda e, cc=cc, pb=pb: e.transpose(
                    bank(pb)[:, 0:128], XNTI[:, 128 * cc:128 * cc + 128], IDENT))
                act(lambda e, tl=tl, cc=cc, pb=pb: e.activation(
                    XN_B[:, BT * cc + 128 * tl:BT * cc + 128 * tl + 128],
                    bank(pb)[:, 0:128], AF.Copy, bias=0.0, scale=1.0))
        for cc in range(NCC):
            s0 = BT * cc
            vec(lambda e, s0=s0: e.tensor_sub(XX_B[:, s0 + 1:s0 + BT],
                                              XN_B[:, s0:s0 + BT - 1],
                                              XN_B[:, s0 + 1:s0 + BT]))
            vec(lambda e, s0=s0, cc=cc: e.tensor_sub(XX_B[:, s0:s0 + 1],
                                                     XPC[:, cc:cc + 1],
                                                     XN_B[:, s0:s0 + 1]))
        vec(lambda e: e.tensor_copy(
            XPC, XN_B.rearrange("p (a w) -> p a w", a=NCC)[:, :, BT - 1]))
        for cc in range(NCC):
            sl = slice(BT * cc, BT * cc + BT)
            vec(lambda e, cc=cc, sl=sl: e.scalar_tensor_tensor(
                XF_B[:, sl], XX_B[:, sl], MAA["x"][:, cc:cc + 1], XN_B[:, sl],
                ALU.mult, ALU.add))
        for cc in range(NCC):
            ten(lambda e, cc=cc: e.matmul(bank(0)[:, 0:BT],
                                          MW1[:, 128 * cc:128 * cc + 128],
                                          XF_B[:, BT * cc:BT * cc + BT],
                                          start=(cc == 0), stop=(cc == NCC - 1)))
        act(lambda e: e.activation(X1, bank(0)[:, 0:BT], AF.Tanh, bias=0.0,
                                   scale=1.0))

        build_xf(0, "w")
        for cc in range(NCC):
            ten(lambda e, cc=cc: e.matmul(bank(0)[0:64, 0:BT],
                                          DW1[:, 64 * cc:64 * cc + 64],
                                          XF_B[:, BT * cc:BT * cc + BT],
                                          start=(cc == 0), stop=(cc == NCC - 1)))
        act(lambda e: e.activation(XD[0:64, :], bank(0)[0:64, 0:BT], AF.Tanh,
                                   bias=0.0, scale=1.0))
        for cc in range(NCC):
            ten(lambda e, cc=cc: e.matmul(bank(1)[:, 0:BT],
                                          DW2[0:64, 128 * cc:128 * cc + 128],
                                          XD[0:64, :], start=True, stop=True))
            act(lambda e, cc=cc: e.activation(CB[:, BT * cc:BT * cc + BT],
                                              bank(1)[:, 0:BT], AF.Exp,
                                              bias=DEC[:, cc:cc + 1], scale=1.0))
        for cc in range(NCC):
            for ch in range(BCH):
                sl = slice(BT * cc + 128 * ch, BT * cc + 128 * ch + 128)
                vec(lambda e, sl=sl: e.tensor_tensor_scan(SCB[:, sl], CB[:, sl],
                                                          ZER, 0.0, ALU.add,
                                                          ALU.add))
        act(lambda e: e.activation(CB, SCB, AF.Exp, bias=0.0, scale=-1.0))
        act(lambda e: e.activation(PIB, SCB, AF.Exp, bias=0.0, scale=1.0))
        vec(lambda e, b=b: e.tensor_copy(
            PLv[:, :, b * BCH:b * BCH + BCH],
            CB.rearrange("p (a c s) -> p a c s", a=NCC, c=BCH)[:, :, :, 127]))
        vec(lambda e: e.tensor_copy(SCB[:, 1:NCC * BT], CB[:, 0:NCC * BT - 1]))
        vec(lambda e: e.memset(
            SCB.rearrange("p (a c s) -> p a c s", a=NCC, c=BCH)[:, :, :, 0], 1.0))

        build_xf(1, "k")
        for dchunk in range(NCC):
            dma(WST, wk_h[128 * dchunk:128 * dchunk + 128, :])
            for dout in range(NCC):
                ten(lambda e, dchunk=dchunk, dout=dout: e.matmul(
                    bank(dout)[:, 0:BT], WST[:, 128 * dout:128 * dout + 128],
                    XF_B[:, BT * dchunk:BT * dchunk + BT],
                    start=(dchunk == 0), stop=(dchunk == NCC - 1)))
        for dout in range(NCC):
            sl = slice(BT * dout, BT * dout + BT)
            vec(lambda e, dout=dout, sl=sl: e.tensor_mul(KT_B[:, sl],
                                                         bank(dout)[:, 0:BT],
                                                         PIB[:, sl]))
            vec(lambda e, dout=dout, sl=sl: e.tensor_scalar(
                CB[:, sl], bank(dout)[:, 0:BT], UPP[:, dout:dout + 1], None,
                ALU.mult))

        build_xf(3, "r")
        for dchunk in range(NCC):
            dma(WST, wr_h[128 * dchunk:128 * dchunk + 128, :])
            for dout in range(NCC):
                ten(lambda e, dchunk=dchunk, dout=dout: e.matmul(
                    bank(dout)[:, 0:BT], WST[:, 128 * dout:128 * dout + 128],
                    XF_B[:, BT * dchunk:BT * dchunk + BT],
                    start=(dchunk == 0), stop=(dchunk == NCC - 1)))
        for dout in range(NCC):
            sl = slice(BT * dout, BT * dout + BT)
            vec(lambda e, dout=dout, sl=sl: e.tensor_mul(RT_B[:, sl],
                                                         bank(dout)[:, 0:BT],
                                                         SCB[:, sl]))
            vec(lambda e, dout=dout, sl=sl: e.tensor_mul(ASCR,
                                                         bank(dout)[:, 0:BT],
                                                         CB[:, sl]))
            for chd in range(BCH):
                dsl = slice(512 * dout + 256, 512 * dout + 256 + 2)
                ten(lambda e, chd=chd, dsl=dsl: e.matmul(
                    PS[:, dsl], ASCR[:, 128 * chd:128 * chd + 128], IND2,
                    start=True, stop=True))
                act(lambda e, chd=chd, dout=dout, dsl=dsl: e.activation(
                    DTMB[:, 16 * chd + 2 * dout:16 * chd + 2 * dout + 2],
                    PS[:, dsl], AF.Copy, bias=0.0, scale=1.0))

        build_xf(2, "v")
        for dchunk in range(NCC):
            dma(WST, wv_h[128 * dchunk:128 * dchunk + 128, :])
            for ch in range(BCH):
                for half in range(2):
                    ten(lambda e, dchunk=dchunk, ch=ch, half=half: e.matmul(
                        bank(2 * ch + half),
                        XF_B[:, BT * dchunk + 128 * ch:BT * dchunk + 128 * ch + 128],
                        WST[:, 512 * half:512 * half + 512],
                        start=(dchunk == 0), stop=(dchunk == NCC - 1)))
        for ch in range(BCH):
            for half in range(2):
                act(lambda e, ch=ch, half=half: e.activation(
                    V_B[:, C * ch + 512 * half:C * ch + 512 * half + 512],
                    bank(2 * ch + half), AF.Copy, bias=0.0, scale=1.0))

        for ch in range(BCH):
            g = b * BCH + ch
            Scur, Snxt = SBF[g % 2], SBF[(g + 1) % 2]
            for cc in range(NCC):
                vec(lambda e, cc=cc, ch=ch, g=g: e.tensor_scalar(
                    KHF[:, 128 * cc:128 * cc + 128],
                    KT_B[:, BT * cc + 128 * ch:BT * cc + 128 * ch + 128],
                    PLv[:, cc, g:g + 1], None, ALU.mult))
            for cc in range(NCC):
                pb = cc % 2
                ten(lambda e, cc=cc, pb=pb: e.transpose(
                    bank(pb)[:, 0:128], KHF[:, 128 * cc:128 * cc + 128], IDENT))
                act(lambda e, cc=cc, pb=pb: e.activation(
                    KHTM[:, 128 * cc:128 * cc + 128], bank(pb)[:, 0:128], AF.Copy,
                    bias=0.0, scale=1.0))
            for h in range(NH):
                cc, base = h // 2, 64 * (h % 2)
                q, s4 = 4 + h // 4, 128 * (h % 4)
                sl = slice(BT * cc + 128 * ch, BT * cc + 128 * ch + 128)
                ten(lambda e, cc=cc, base=base, q=q, s4=s4, sl=sl: e.matmul(
                    bank(q)[:, s4:s4 + 128], KT_B[base:base + 64, sl],
                    RT_B[base:base + 64, sl], start=True, stop=True,
                    tile_position=(base, 0)))
            for q in range(4):
                for hp in range(2):
                    vec(lambda e, q=q, hp=hp: e.tensor_mul(
                        GM[:, 512 * q + 256 * hp:512 * q + 256 * hp + 256],
                        bank(4 + q)[:, 256 * hp:256 * hp + 256], MSK2))
            for h in range(NH):
                cc, base = h // 2, 64 * (h % 2)
                sl = slice(BT * cc + 128 * ch, BT * cc + 128 * ch + 128)
                vsl = slice(C * ch + HD * h, C * ch + HD * h + HD)
                ob = bank(h // 8)[:, (64 * h) % 512:(64 * h) % 512 + 64]
                ten(lambda e, h=h, ob=ob, vsl=vsl: e.matmul(
                    ob, GM[:, 128 * h:128 * h + 128], V_B[:, vsl],
                    start=True, stop=False))
                ten(lambda e, cc=cc, base=base, sl=sl, ob=ob, h=h, Scur=Scur:
                    e.matmul(ob, RT_B[base:base + 64, sl],
                             Scur[base:base + 64, HD * (h // 2):HD * (h // 2) + HD],
                             start=False, stop=True, tile_position=(base, 0)))
            vec(lambda e, ch=ch: e.tensor_mul(
                YSQ.rearrange("p (h e) -> p h e", h=NH),
                V_B[:, C * ch:C * ch + C].rearrange("p (h e) -> p h e", h=NH),
                DTMB[:, 16 * ch:16 * ch + 16].unsqueeze(2)
                .broadcast_to([128, NH, HD])))
            vec(lambda e: e.tensor_add(YS[:, 0:512], YSQ[:, 0:512], bank(0)))
            vec(lambda e: e.tensor_add(YS[:, 512:1024], YSQ[:, 512:1024], bank(1)))
            for h in range(NH):
                vsl = slice(C * ch + HD * h, C * ch + HD * h + HD)
                ob = bank(2)[64 * (h % 2):64 * (h % 2) + 64,
                             HD * (h // 2):HD * (h // 2) + HD]
                ten(lambda e, h=h, ob=ob, vsl=vsl: e.matmul(
                    ob, KHTM[:, HD * h:HD * h + HD], V_B[:, vsl],
                    start=True, stop=True, tile_position=(0, 64 * (h % 2))))
            for cc in range(NCC):
                vec(lambda e, cc=cc, g=g, Scur=Scur, Snxt=Snxt:
                    e.scalar_tensor_tensor(
                        Snxt[:, HD * cc:HD * cc + HD], Scur[:, HD * cc:HD * cc + HD],
                        PLv[:, cc, g:g + 1], bank(2)[:, HD * cc:HD * cc + HD],
                        ALU.mult, ALU.add))
            vec(lambda e: e.tensor_mul(YSQ, YS, YS))
            vec(lambda e: e.reduce_sum(GN1, YS.rearrange("p (h e) -> p h e", h=NH),
                                       axis=AX.X))
            vec(lambda e: e.reduce_sum(GN2, YSQ.rearrange("p (h e) -> p h e", h=NH),
                                       axis=AX.X))
            vec(lambda e: e.tensor_single_scalar(GMU, GN1, 1.0 / HD, ALU.mult))
            vec(lambda e: e.tensor_single_scalar(GN2, GN2, 1.0 / HD, ALU.mult))
            vec(lambda e: e.tensor_mul(GRS, GMU, GMU))
            vec(lambda e: e.tensor_sub(GN2, GN2, GRS))
            vec(lambda e: e.tensor_single_scalar(GN2, GN2, GN_EPS, ALU.add))
            act(lambda e: e.activation(GN2, GN2, AF.Sqrt, bias=0.0, scale=1.0))
            vec(lambda e: e.reciprocal(GRS, GN2))
            vec(lambda e: e.tensor_sub(YN.rearrange("p (h e) -> p h e", h=NH),
                                       YS.rearrange("p (h e) -> p h e", h=NH),
                                       GMU.unsqueeze(2).broadcast_to([128, NH, HD])))
            vec(lambda e: e.tensor_mul(YN.rearrange("p (h e) -> p h e", h=NH),
                                       YN.rearrange("p (h e) -> p h e", h=NH),
                                       GRS.unsqueeze(2).broadcast_to([128, NH, HD])))
            for cc in range(NCC):
                pb = cc % 2
                ten(lambda e, cc=cc, pb=pb: e.transpose(
                    bank(pb)[:, 0:128], YN[:, 128 * cc:128 * cc + 128], IDENT))
                act(lambda e, cc=cc, ch=ch, pb=pb: e.activation(
                    YNFB[:, BT * cc + 128 * ch:BT * cc + 128 * ch + 128],
                    bank(pb)[:, 0:128], AF.Copy, bias=0.0, scale=LNW[:, cc:cc + 1]))

        for ch in range(BCH):
            for half in range(2):
                ten(lambda e, ch=ch, half=half: e.matmul(
                    bank(2 * ch + half), ONER, BIASR[:, 512 * half:512 * half + 512],
                    start=True, stop=False))
        for cc in range(NCC):
            dma(WST, wo_h[128 * cc:128 * cc + 128, :])
            for ch in range(BCH):
                for half in range(2):
                    ten(lambda e, cc=cc, ch=ch, half=half: e.matmul(
                        bank(2 * ch + half),
                        YNFB[:, BT * cc + 128 * ch:BT * cc + 128 * ch + 128],
                        WST[:, 512 * half:512 * half + 512],
                        start=False, stop=(cc == NCC - 1)))
        for ch in range(BCH):
            gt = b * BCH + ch
            dma(XTI, x_h[128 * gt:128 * gt + 128, :])
            for half in range(2):
                vec(lambda e, ch=ch, half=half: e.tensor_add(
                    QA[:, 512 * half:512 * half + 512],
                    XTI[:, 512 * half:512 * half + 512], bank(2 * ch + half)))
            dma(xp_h[128 * gt:128 * gt + 128, :], QA)

    attn_stack.close()

    ffn_stack = contextlib.ExitStack()
    sbf = mk_sb(ffn_stack)
    N1T = sbf("n1t", (128, NCC * T), BF16)
    T1SL = sbf("t1sl", (128, 512), BF16)
    INVS1B = sbf("invs1b", (128, T))
    HB0 = sbf("hb0", (128, NHC * 258))
    HB1 = sbf("hb1", (128, NHC * 258))
    GST = sbf("gstash", (128, NHC))
    ACC = sbf("accc", (128, 256))
    SIL = sbf("silc", (128, 256))
    HCT = sbf("hct", (128, 2 * HID))
    N2B = sbf("n2b", (128, HID), BF16)
    QA2 = sbf("qa2", (128, HID))
    IVT = sbf("ivt", (16, 128))
    INVR = sbf("invr", (1, T))
    HBUF = [HB0, HB1]

    for dchunk in range(NCC):
        dma(N1T[:, T * dchunk:T * dchunk + T],
            n1_h[:, 128 * dchunk:128 * dchunk + 128], transpose=True)
    vec(lambda e: e.reciprocal(SC16, QS1))
    vec(lambda e: e.scalar_tensor_tensor(SC16, SC16, 0.5,
                                         Q1.broadcast_to([128, 16]), ALU.mult,
                                         ALU.mult))
    ten(lambda e: e.transpose(bank(0)[0:16, 0:128], SC16, IDENT))
    act(lambda e: e.activation(IVT, bank(0)[0:16, 0:128], AF.Copy, bias=0.0,
                               scale=1.0))
    dma(INVR.rearrange("p (a w) -> p a w", a=16), IVT)
    for qq in range(4):
        ten(lambda e, qq=qq: e.matmul(bank(qq % 2), ONER,
                                      INVR[:, 512 * qq:512 * qq + 512],
                                      start=True, stop=True))
        act(lambda e, qq=qq: e.activation(INVS1B[:, 512 * qq:512 * qq + 512],
                                          bank(qq % 2), AF.Copy, bias=0.0,
                                          scale=1.0))
    vec(lambda e: e.memset(GST, 0.0))

    def conv_eighth(j):
        hb, hn = HBUF[j % 2], HBUF[(j + 1) % 2]
        hbv = hb.rearrange("p (a w) -> p a w", a=NHC)
        vec(lambda e, hbv=hbv: e.tensor_copy(hbv[:, :, 0], GST))
        if j < NE - 1:
            vec(lambda e, hbv=hbv, hn=hn: e.tensor_copy(
                hbv[:, :, 257], hn.rearrange("p (a w) -> p a w", a=NHC)[:, :, 1]))
        else:
            vec(lambda e, hbv=hbv: e.memset(hbv[:, :, 257], 0.0))
        vec(lambda e, hbv=hbv: e.tensor_copy(GST, hbv[:, :, 256]))
        for hc in range(NHC):
            w0 = 258 * hc
            vec(lambda e, w0=w0, hc=hc, hb=hb: e.tensor_scalar(
                ACC, hb[:, w0:w0 + 256], TAP[0][:, hc:hc + 1], None, ALU.mult))
            vec(lambda e, w0=w0, hc=hc, hb=hb: e.scalar_tensor_tensor(
                ACC, hb[:, w0 + 1:w0 + 257], TAP[1][:, hc:hc + 1], ACC,
                ALU.mult, ALU.add))
            vec(lambda e, w0=w0, hc=hc, hb=hb: e.scalar_tensor_tensor(
                ACC, hb[:, w0 + 2:w0 + 258], TAP[2][:, hc:hc + 1], ACC,
                ALU.mult, ALU.add))
            act(lambda e, hc=hc: e.activation(SIL, ACC, AF.Silu,
                                              bias=CBB[:, hc:hc + 1], scale=1.0))
            for chh in range(2):
                pb = (hc * 2 + chh) % 2
                ten(lambda e, chh=chh, pb=pb: e.transpose(
                    bank(pb)[:, 0:128], SIL[:, 128 * chh:128 * chh + 128], IDENT))
                act(lambda e, hc=hc, chh=chh, pb=pb: e.activation(
                    HCT[:, HID * chh + 128 * hc:HID * chh + 128 * hc + 128],
                    bank(pb)[:, 0:128], AF.Copy, bias=0.0, scale=1.0))
        for chh in range(2):
            tc = 2 * j + chh
            hsl = slice(HID * chh, HID * chh + HID)
            vec(lambda e, hsl=hsl: e.reduce_max(RMX, HCT[:, hsl], axis=AX.X,
                                                apply_absolute_value=True))
            vec(lambda e: e.tensor_single_scalar(RMX, RMX, 1e-5, ALU.max))
            vec(lambda e: e.reciprocal(RMX, RMX))
            vec(lambda e, tc=tc: e.tensor_single_scalar(QS2[:, tc:tc + 1], RMX,
                                                        127.0, ALU.mult))
            vec(lambda e, hsl=hsl, tc=tc: e.tensor_scalar(
                QA2, HCT[:, hsl], QS2[:, tc:tc + 1], MAGIC, ALU.mult, ALU.add))
            vec(lambda e: e.tensor_single_scalar(N2B, QA2, MAGIC, ALU.subtract))
            dma(n2_h[128 * tc:128 * tc + 128, :], N2B)

    for j in range(NE):
        for hg in range(8):
            for dchunk in range(NCC):
                dma(T1SL, t1q_h[512 * hg:512 * hg + 512,
                                128 * dchunk:128 * dchunk + 128], transpose=True)
                for h4 in range(4):
                    ten(lambda e, dchunk=dchunk, h4=h4, j=j: e.matmul(
                        bank(h4)[:, 0:256], T1SL[:, 128 * h4:128 * h4 + 128],
                        N1T[:, T * dchunk + ET * j:T * dchunk + ET * j + ET],
                        start=(dchunk == 0), stop=(dchunk == NCC - 1)))
            for h4 in range(4):
                hc = 4 * hg + h4
                w0 = 258 * hc
                vec(lambda e, h4=h4, w0=w0, j=j: e.tensor_mul(
                    HBUF[j % 2][:, w0 + 1:w0 + 257], bank(h4)[:, 0:256],
                    INVS1B[:, ET * j:ET * j + ET]))
        if j >= 1:
            conv_eighth(j - 1)
    conv_eighth(NE - 1)

    vec(lambda e: e.reciprocal(SC16, QS2))
    vec(lambda e: e.scalar_tensor_tensor(INVS2, SC16, 0.5,
                                         Q2.broadcast_to([128, 16]), ALU.mult,
                                         ALU.mult))

    ffn_stack.close()
    ffn2_stack = contextlib.ExitStack()
    sbg = mk_sb(ffn2_stack)
    N2TS = sbg("n2ts", (128, 512), BF16)
    T2SL = sbg("t2sl", (128, C), BF16)
    XRE = sbg("xre", (128, C))
    OST = sbg("ost", (128, C))

    for tg in range(4):
        for hc in range(NHC):
            dma(N2TS, n2_h[512 * tg:512 * tg + 512, 128 * hc:128 * hc + 128],
                transpose=True)
            dma(T2SL, t2q_h[:, 128 * hc:128 * hc + 128], transpose=True)
            for t4 in range(4):
                for half in range(2):
                    ten(lambda e, hc=hc, t4=t4, half=half: e.matmul(
                        bank(2 * t4 + half), N2TS[:, 128 * t4:128 * t4 + 128],
                        T2SL[:, 512 * half:512 * half + 512],
                        start=(hc == 0), stop=(hc == NHC - 1)))
        for t4 in range(4):
            tc = 4 * tg + t4
            dma(XRE, xp_h[128 * tc:128 * tc + 128, :])
            for half in range(2):
                vec(lambda e, t4=t4, half=half, tc=tc: e.scalar_tensor_tensor(
                    OST[:, 512 * half:512 * half + 512], bank(2 * t4 + half),
                    INVS2[:, tc:tc + 1], XRE[:, 512 * half:512 * half + 512],
                    ALU.mult, ALU.add))
            dma(out_h[128 * tc:128 * tc + 128, :], OST)

    ffn2_stack.close()
    perm.close()
    with nc.semaphore("g") as g, nc.Block() as block:
        sched = []
        pos = 0
        for eng, fn, inc in prog:
            sched.append((eng, fn, inc, pos))
            pos += inc
        total = pos

        def emit(name, handle):
            for eng, fn, inc, at in sched:
                if eng != name:
                    continue
                handle.wait_ge(g, at)
                fn(handle).then_inc(g, inc)
            if name == "gpsimd":
                handle.wait_ge(g, total)

        @block.sync
        def _(e):
            emit("sync", e)

        @block.vector
        def _(e):
            emit("vector", e)

        @block.scalar
        def _(e):
            emit("scalar", e)

        @block.tensor
        def _(e):
            emit("tensor", e)

        @block.gpsimd
        def _(e):
            emit("gpsimd", e)

    return nc


_NC_CACHE = []


def kernel(L0_attn_query, L0_attn_key, L0_attn_value, L0_attn_time_maa_x,
           L0_attn_time_maa_w, L0_attn_time_maa_k, L0_attn_time_maa_v,
           L0_attn_time_maa_r, L0_attn_time_maa_w1, L0_attn_time_maa_w2,
           L0_attn_time_decay, L0_attn_time_decay_w1, L0_attn_time_decay_w2,
           L0_attn_time_faaaa, L0_attn_W_r, L0_attn_W_k, L0_attn_W_v, L0_attn_W_o,
           L0_attn_ln_w, L0_attn_ln_b, L0_ffn_w1, L0_ffn_conv_w, L0_ffn_conv_b,
           L0_ffn_w2):
    x = np.ascontiguousarray(np.asarray(L0_attn_query, dtype=np.float32))
    w = {
        "wr": L0_attn_W_r, "wk": L0_attn_W_k, "wv": L0_attn_W_v, "wo": L0_attn_W_o,
        "maa_x": L0_attn_time_maa_x, "maa_w": L0_attn_time_maa_w,
        "maa_k": L0_attn_time_maa_k, "maa_v": L0_attn_time_maa_v,
        "maa_r": L0_attn_time_maa_r, "mw1": L0_attn_time_maa_w1,
        "mw2": L0_attn_time_maa_w2, "dec": L0_attn_time_decay,
        "dw1": L0_attn_time_decay_w1, "dw2": L0_attn_time_decay_w2,
        "faaaa": L0_attn_time_faaaa, "lnw": L0_attn_ln_w, "lnb": L0_attn_ln_b,
        "w1": L0_ffn_w1, "cw": L0_ffn_conv_w, "cb": L0_ffn_conv_b, "w2": L0_ffn_w2,
    }
    w = {k: np.ascontiguousarray(np.asarray(v, dtype=np.float32))
         for k, v in w.items()}
    if not _NC_CACHE:
        _NC_CACHE.append(build_nc())
    nc = _NC_CACHE[0]
    in_maps = [dict(w, x=np.ascontiguousarray(x[i])) for i in range(8)]
    res = run_bass_kernel_spmd(nc, in_maps, core_ids=list(range(8)))
    return np.stack([np.asarray(r["out"], dtype=np.float32)
                     for r in res.results], axis=0)


# version 04
# speedup vs baseline: 1.3611x; 1.3611x over previous
import contextlib
import os

import numpy as np
import concourse.bass as bass
import concourse.mybir as mybir
from concourse.bass_utils import run_bass_kernel_spmd

F32 = mybir.dt.float32
BF16 = mybir.dt.bfloat16
AF = mybir.ActivationFunctionType
ALU = mybir.AluOpType
AX = mybir.AxisListType

MAGIC = 12582912.0
T, C, HID, NH, HD = 2048, 1024, 4096, 16, 64
NCC = 8
NHC = 32
NB, BT, BCH = 8, 256, 2
NE, ET = 8, 256
LN_EPS, GN_EPS = 1e-5, 0.00064
MEAN_W1 = 1.0 / (HID * C)
MEAN_W2 = 1.0 / (C * HID)
_KPH = int(os.environ.get("KPHASE", "99"))
_KBLK = int(os.environ.get("KBLK", "99"))
_KPART = int(os.environ.get("KPART", "99"))
_KWKV = int(os.environ.get("KWKV", "99"))


ENGS = ("sync", "vector", "scalar", "tensor", "gpsimd")


def build_prog(nc):

    def din(name, shape, dt=F32):
        return nc.dram_tensor(name, list(shape), dt, kind="ExternalInput").ap()

    x_h = din("x", (T, C))
    wr_h, wk_h, wv_h, wo_h = (din(n, (C, C)) for n in ("wr", "wk", "wv", "wo"))
    maa_h = {f: din("maa_" + f, (C,)) for f in ("x", "w", "k", "v", "r")}
    mw1_h = din("mw1", (C, 128))
    mw2_h = din("mw2", (4, 32, C))
    dec_h = din("dec", (C,))
    dw1_h = din("dw1", (C, 64))
    dw2_h = din("dw2", (64, C))
    faaaa_h = din("faaaa", (NH, HD))
    lnw_h, lnb_h = din("lnw", (C,)), din("lnb", (C,))
    w1_h = din("w1", (HID, C))
    cw_h = din("cw", (HID, 1, 3))
    cb_h = din("cb", (HID,))
    w2_h = din("w2", (C, HID))

    out_h = nc.dram_tensor("out", [T, C], F32, kind="ExternalOutput").ap()
    n1_h = nc.dram_tensor("n1s", [T, C], BF16, kind="Internal").ap()
    t1q_h = nc.dram_tensor("t1qs", [HID, C], BF16, kind="Internal").ap()
    n2_h = nc.dram_tensor("n2s", [T, HID], BF16, kind="Internal").ap()
    t2q_h = nc.dram_tensor("t2qs", [C, HID], BF16, kind="Internal").ap()
    xp_h = nc.dram_tensor("xps", [T, C], F32, kind="Internal").ap()

    def mk_sb(stack):
        def sb(name, shape, dt=F32):
            return stack.enter_context(nc.sbuf_tensor(name, list(shape), dt)).ap()
        return sb

    perm = contextlib.ExitStack()
    sb = mk_sb(perm)

    IDENT = sb("ident", (128, 128))
    MSK2 = sb("msk2", (128, 256))
    ONEC = sb("onec", (128, 1))
    ONER = sb("oner", (1, 128))
    IND2 = sb("ind2", (128, 2))
    NH5 = sb("nh5", (128, 1))
    PH5 = sb("ph5", (128, 1))
    ZER = sb("zer", (128, 128))
    TAP = [sb("tap%d" % i, (128, NHC)) for i in range(3)]
    CBB = sb("cbpp", (128, NHC))
    S1c = sb("s1c", (128, 1))
    S2c = sb("s2c", (128, 1))
    MUc = sb("muc", (128, 1))
    RSc = sb("rsc", (128, 1))
    NBc = sb("nbc", (128, 1))
    QS1 = sb("qs1", (128, 16))
    QS2 = sb("qs2", (128, 16))
    INVS2 = sb("invs2", (128, 16))
    SC16 = sb("sc16", (128, 16))
    RMX = sb("rmx", (128, 1))
    PL = sb("plx", (128, 128))
    Q1, Q2 = sb("q1", (128, 1)), sb("q2", (128, 1))
    SONE = sb("sone", (1, 1))
    PLv = PL.rearrange("p (a g) -> p a g", a=NCC)

    prog = []

    def fence():
        prog.append(("FENCE", None, 0))

    def op(eng, inc=1):
        def deco(fn):
            prog.append((eng, fn, inc))
        return deco

    def dma(out, in_, transpose=False):
        @op("sync", 16)
        def _(e, out=out, in_=in_, tr=transpose):
            if tr:
                return e.dma_start(out=out, in_=in_, transpose=True)
            return e.dma_start(out=out, in_=in_)

    def vec(fn):
        @op("vector")
        def _(e, fn=fn):
            return fn(e)

    def act(fn):
        @op("scalar")
        def _(e, fn=fn):
            return fn(e)

    def ten(fn):
        @op("tensor")
        def _(e, fn=fn):
            return fn(e)

    def gps(fn):
        @op("gpsimd")
        def _(e, fn=fn):
            return fn(e)

    vec(lambda e: e.memset(ZER, 0.0))
    vec(lambda e: e.memset(IDENT, 1.0))
    gps(lambda e: e.affine_select(IDENT, IDENT, [[-1, 128]], ALU.is_equal,
                                  0.0, base=0, channel_multiplier=1))
    gps(lambda e: e.iota(MSK2.rearrange("p (a b) -> p a b", a=2), [[0, 2], [1, 128]],
                         base=0, channel_multiplier=-1,
                         allow_small_or_imprecise_dtypes=True))
    vec(lambda e: e.tensor_single_scalar(MSK2, MSK2, 0.0, ALU.is_gt))
    vec(lambda e: e.memset(ONEC, 1.0))
    vec(lambda e: e.memset(ONER, 1.0))
    vec(lambda e: e.memset(NH5, -0.5))
    vec(lambda e: e.memset(PH5, 0.5))
    vec(lambda e: e.memset(IND2, 0.0))
    vec(lambda e: e.memset(IND2[0:64, 0:1], 1.0))
    vec(lambda e: e.memset(IND2[64:128, 1:2], 1.0))

    PSH = nc.alloc_psum_tensor("ps", [128, 4096], F32)
    PS = PSH.ap()

    def bank(k, w=512):
        return PS[:, 512 * k:512 * k + w]

    cw_stack = contextlib.ExitStack()
    sbc = mk_sb(cw_stack)
    CWS = sbc("cws", (32, 384))
    CBS = sbc("cbs", (32, 128))
    dma(CWS, cw_h.rearrange("(a b) o t -> a (b o t)", a=32))
    dma(CBS, cb_h.rearrange("(a b) -> a b", a=32))
    for tau in range(3):
        ten(lambda e, tau=tau: e.transpose(
            bank(0, 32)[:, 0:32],
            CWS.rearrange("a (b t) -> a b t", t=3)[:, :, tau], IDENT[0:32, 0:32]))
        act(lambda e, tau=tau: e.activation(TAP[tau], bank(0, 32)[:, 0:32], AF.Copy,
                                            bias=0.0, scale=1.0))
    ten(lambda e: e.transpose(bank(0, 32)[:, 0:32], CBS, IDENT[0:32, 0:32]))
    act(lambda e: e.activation(CBB, bank(0, 32)[:, 0:32], AF.Copy, bias=0.0,
                               scale=1.0))
    fence()
    cw_stack.close()

    wq_stack = contextlib.ExitStack()
    sbw = mk_sb(wq_stack)
    WQT = sbw("wqt", (128, HID))
    WQA = sbw("wqa", (128, HID))
    WQB = sbw("wqb", (128, HID))
    WQO = sbw("wqo", (128, HID), BF16)
    WSUM = sbw("wsum", (128, 32))
    SW = sbw("sw", (128, 1))

    def wquant(w_hbm, ntile, cols, meanmul, qtile, tq_hbm):
        for wt in range(ntile):
            dma(WQT[:, 0:cols], w_hbm[128 * wt:128 * wt + 128, :])
            act(lambda e, wt=wt, cols=cols: e.activation(
                WQA[:, 0:cols], WQT[:, 0:cols], AF.Abs, bias=0.0, scale=1.0,
                accum_out=WSUM[:, wt:wt + 1]))
        vec(lambda e, ntile=ntile: e.reduce_sum(RMX, WSUM[:, 0:ntile], axis=AX.X))
        ten(lambda e: e.matmul(bank(0)[0:1, 0:1], RMX, ONEC, start=True, stop=True))
        act(lambda e: e.activation(SONE, bank(0)[0:1, 0:1], AF.Copy, bias=0.0,
                                   scale=1.0))
        vec(lambda e, mm=meanmul: e.tensor_scalar(SONE, SONE, mm, 1e-5, ALU.mult,
                                                  ALU.max))
        ten(lambda e: e.matmul(bank(0)[:, 0:1], ONER, SONE, start=True, stop=True))
        act(lambda e, qtile=qtile: e.activation(qtile, bank(0)[:, 0:1], AF.Copy,
                                                bias=0.0, scale=1.0))
        vec(lambda e, qtile=qtile: e.reciprocal(SW, qtile))
        for wt in range(ntile):
            dma(WQT[:, 0:cols], w_hbm[128 * wt:128 * wt + 128, :])
            act(lambda e, cols=cols: e.activation(WQA[:, 0:cols], WQT[:, 0:cols],
                                                  AF.Sign, bias=NH5, scale=SW))
            act(lambda e, cols=cols: e.activation(WQB[:, 0:cols], WQT[:, 0:cols],
                                                  AF.Sign, bias=PH5, scale=SW))
            vec(lambda e, cols=cols: e.tensor_add(WQO[:, 0:cols], WQA[:, 0:cols],
                                                  WQB[:, 0:cols]))
            dma(tq_hbm[128 * wt:128 * wt + 128, :], WQO[:, 0:cols])

    wquant(w1_h, 32, C, MEAN_W1, Q1, t1q_h)
    wquant(w2_h, 8, HID, MEAN_W2, Q2, t2q_h)
    fence()
    wq_stack.close()

    attn_stack = contextlib.ExitStack()
    sba = mk_sb(attn_stack)
    MAA = {f: sba("maa" + f, (128, NCC)) for f in ("x", "w", "k", "v", "r")}
    DEC = sba("decpp", (128, NCC))
    UPP = sba("upp", (128, NCC))
    LNW = sba("lnwpp", (128, NCC))
    LNB = sba("lnbpp", (128, NCC))
    MW1 = sba("mw1t", (128, NCC * 128))
    MW2 = sba("mw2t", (128, NCC * 128))
    DW1 = sba("dw1t", (128, NCC * 64))
    DW2 = sba("dw2t", (128, NCC * 128))
    BIASR = sba("biasr", (1, C))
    XTI = sba("xti", (128, C))
    XNTI = sba("xnti", (128, C))
    QA = sba("qa", (128, C))
    N1B = sba("n1b", (128, C), BF16)
    XN_B = sba("xn_b", (128, NCC * BT))
    XX_B = sba("xx_b", (128, NCC * BT))
    XF_B = sba("xf_b", (128, NCC * BT))
    XPC = sba("xpc", (128, NCC))
    TSC = sba("tsc", (128, BT))
    X1 = sba("x1b", (128, BT))
    XD = sba("xdb", (128, BT))
    CB = sba("cbuf", (128, NCC * BT))
    SCB = sba("scb", (128, NCC * BT))
    PIB = sba("pib", (128, NCC * BT))
    KT_B = sba("kt_b", (128, NCC * BT))
    RT_B = sba("rt_b", (128, NCC * BT))
    ASCR = sba("ascr", (128, BT))
    V_B = sba("v_b", (128, BCH * C))
    KHF = sba("khf", (128, NCC * 128))
    KHTM = sba("khtm", (128, NCC * 128))
    DTMB = sba("dtmb", (128, BCH * 16))
    GM = sba("gm", (128, NH * 128))
    SBF = [sba("s0b", (128, NCC * HD)), sba("s1b", (128, NCC * HD))]
    YS = sba("ys", (128, C))
    YSQ = sba("ysq", (128, C))
    YN = sba("yn", (128, C))
    GN1, GN2 = sba("gn1", (128, 16)), sba("gn2", (128, 16))
    GMU, GRS = sba("gmu", (128, 16)), sba("grs", (128, 16))
    YNFB = sba("ynfb", (128, NCC * BT))
    WST = sba("wstr", (128, C))

    VST8 = sba("vst8", (64, 128))
    vec_targets = [(maa_h["x"], MAA["x"]), (maa_h["w"], MAA["w"]),
                   (maa_h["k"], MAA["k"]), (maa_h["v"], MAA["v"]),
                   (maa_h["r"], MAA["r"]), (dec_h, DEC), (lnw_h, LNW),
                   (lnb_h, LNB)]
    for src_h, tgt in vec_targets:
        dma(VST8[0:8, :], src_h.rearrange("(a b) -> a b", a=8))
        ten(lambda e: e.transpose(bank(2)[:, 0:8], VST8[0:8, :], IDENT[0:8, 0:8]))
        act(lambda e, tgt=tgt: e.activation(tgt, bank(2)[:, 0:8], AF.Copy,
                                            bias=0.0, scale=1.0))
    dma(VST8[0:16, 0:64], faaaa_h)
    ten(lambda e: e.transpose(bank(2)[0:64, 0:16], VST8[0:16, 0:64],
                              IDENT[0:16, 0:16]))
    act(lambda e: e.activation(VST8[0:64, 0:16], bank(2)[0:64, 0:16],
                               AF.Copy, bias=0.0, scale=1.0))
    for hp in range(2):
        vec(lambda e, hp=hp: e.tensor_copy(
            UPP[64 * hp:64 * hp + 64, :],
            VST8[0:64, 0:16].rearrange("p (c two) -> p c two", two=2)[:, :, hp]))
    dma(MW1.rearrange("p (a l) -> p a l", a=NCC), mw1_h.rearrange("(a b) l -> b a l", b=128))
    for f in range(4):
        dma(MW2[32 * f:32 * f + 32, :], mw2_h[f])
    dma(DW1.rearrange("p (a l) -> p a l", a=NCC), dw1_h.rearrange("(a b) l -> b a l", b=128))
    dma(DW2[0:64, :], dw2_h)
    vec(lambda e: e.memset(XPC, 0.0))
    vec(lambda e: e.memset(SBF[0], 0.0))

    for cc in range(NCC):
        dma(WST, wo_h[128 * cc:128 * cc + 128, :])
        for half in range(2):
            ten(lambda e, cc=cc, half=half: e.matmul(
                bank(half)[0:1, :], LNB[:, cc:cc + 1],
                WST[:, 512 * half:512 * half + 512],
                start=(cc == 0), stop=(cc == NCC - 1)))
    for half in range(2):
        act(lambda e, half=half: e.activation(
            BIASR[:, 512 * half:512 * half + 512], bank(half)[0:1, :], AF.Copy,
            bias=0.0, scale=1.0))

    def lora2_mm(f, cc):
        ten(lambda e, f=f, cc=cc: e.matmul(
            bank(6)[:, 0:BT], MW2[32 * f:32 * f + 32, 128 * cc:128 * cc + 128],
            X1[32 * f:32 * f + 32, :], start=True, stop=True,
            tile_position=(32 * f, 0)))

    def build_xf(f, mf):
        for cc in range(NCC):
            lora2_mm(f, cc)
            sl = slice(BT * cc, BT * cc + BT)
            vec(lambda e, cc=cc, sl=sl, mf=mf: e.scalar_tensor_tensor(
                TSC, bank(6)[:, 0:BT], MAA[mf][:, cc:cc + 1], XX_B[:, sl],
                ALU.add, ALU.mult))
            vec(lambda e, sl=sl: e.tensor_add(XF_B[:, sl], TSC, XN_B[:, sl]))

    for b in range(min(NB if _KPH >= 2 else 0, _KBLK)):
        for tl in range(BCH):
            gt = b * BCH + tl
            dma(XTI, x_h[128 * gt:128 * gt + 128, :])
            vec(lambda e: e.reduce_sum(S1c, XTI, axis=AX.X))
            act(lambda e: e.activation(QA, XTI, AF.Square, bias=0.0, scale=1.0,
                                       accum_out=S2c))
            vec(lambda e: e.tensor_single_scalar(MUc, S1c, 1.0 / C, ALU.mult))
            vec(lambda e: e.tensor_single_scalar(S2c, S2c, 1.0 / C, ALU.mult))
            vec(lambda e: e.tensor_mul(RSc, MUc, MUc))
            vec(lambda e: e.tensor_sub(S2c, S2c, RSc))
            vec(lambda e: e.tensor_single_scalar(S2c, S2c, LN_EPS, ALU.add))
            act(lambda e: e.activation(S2c, S2c, AF.Sqrt, bias=0.0, scale=1.0))
            vec(lambda e: e.reciprocal(RSc, S2c))
            vec(lambda e: e.scalar_tensor_tensor(NBc, MUc, -1.0, RSc, ALU.mult,
                                                 ALU.mult))
            vec(lambda e: e.tensor_scalar(XNTI, XTI, RSc, NBc, ALU.mult, ALU.add))
            vec(lambda e: e.reduce_max(RMX, XNTI, axis=AX.X,
                                       apply_absolute_value=True))
            vec(lambda e: e.tensor_single_scalar(RMX, RMX, 1e-5, ALU.max))
            vec(lambda e: e.reciprocal(RMX, RMX))
            vec(lambda e, gt=gt: e.tensor_single_scalar(QS1[:, gt:gt + 1], RMX,
                                                        127.0, ALU.mult))
            vec(lambda e, gt=gt: e.tensor_scalar(QA, XNTI, QS1[:, gt:gt + 1],
                                                 MAGIC, ALU.mult, ALU.add))
            vec(lambda e: e.tensor_single_scalar(N1B, QA, MAGIC, ALU.subtract))
            dma(n1_h[128 * gt:128 * gt + 128, :], N1B)
            for cc in range(NCC):
                pb = cc % 2
                ten(lambda e, cc=cc, pb=pb: e.transpose(
                    bank(pb)[:, 0:128], XNTI[:, 128 * cc:128 * cc + 128], IDENT))
                act(lambda e, tl=tl, cc=cc, pb=pb: e.activation(
                    XN_B[:, BT * cc + 128 * tl:BT * cc + 128 * tl + 128],
                    bank(pb)[:, 0:128], AF.Copy, bias=0.0, scale=1.0))
        if _KPART < 2:
            continue
        for cc in range(NCC):
            s0 = BT * cc
            vec(lambda e, s0=s0: e.tensor_sub(XX_B[:, s0 + 1:s0 + BT],
                                              XN_B[:, s0:s0 + BT - 1],
                                              XN_B[:, s0 + 1:s0 + BT]))
            vec(lambda e, s0=s0, cc=cc: e.tensor_sub(XX_B[:, s0:s0 + 1],
                                                     XPC[:, cc:cc + 1],
                                                     XN_B[:, s0:s0 + 1]))
        vec(lambda e: e.tensor_copy(
            XPC, XN_B.rearrange("p (a w) -> p a w", a=NCC)[:, :, BT - 1]))
        for cc in range(NCC):
            sl = slice(BT * cc, BT * cc + BT)
            vec(lambda e, cc=cc, sl=sl: e.scalar_tensor_tensor(
                XF_B[:, sl], XX_B[:, sl], MAA["x"][:, cc:cc + 1], XN_B[:, sl],
                ALU.mult, ALU.add))
        for cc in range(NCC):
            ten(lambda e, cc=cc: e.matmul(bank(0)[:, 0:BT],
                                          MW1[:, 128 * cc:128 * cc + 128],
                                          XF_B[:, BT * cc:BT * cc + BT],
                                          start=(cc == 0), stop=(cc == NCC - 1)))
        act(lambda e: e.activation(X1, bank(0)[:, 0:BT], AF.Tanh, bias=0.0,
                                   scale=1.0))

        if _KPART < 3:
            continue
        build_xf(0, "w")
        for cc in range(NCC):
            ten(lambda e, cc=cc: e.matmul(bank(0)[0:64, 0:BT],
                                          DW1[:, 64 * cc:64 * cc + 64],
                                          XF_B[:, BT * cc:BT * cc + BT],
                                          start=(cc == 0), stop=(cc == NCC - 1)))
        act(lambda e: e.activation(XD[0:64, :], bank(0)[0:64, 0:BT], AF.Tanh,
                                   bias=0.0, scale=1.0))
        for cc in range(NCC):
            ten(lambda e, cc=cc: e.matmul(bank(1)[:, 0:BT],
                                          DW2[0:64, 128 * cc:128 * cc + 128],
                                          XD[0:64, :], start=True, stop=True))
            act(lambda e, cc=cc: e.activation(CB[:, BT * cc:BT * cc + BT],
                                              bank(1)[:, 0:BT], AF.Exp,
                                              bias=DEC[:, cc:cc + 1], scale=1.0))
        for cc in range(NCC):
            for ch in range(BCH):
                sl = slice(BT * cc + 128 * ch, BT * cc + 128 * ch + 128)
                vec(lambda e, sl=sl: e.tensor_tensor_scan(SCB[:, sl], CB[:, sl],
                                                          ZER, 0.0, ALU.add,
                                                          ALU.add))
        act(lambda e: e.activation(CB, SCB, AF.Exp, bias=0.0, scale=-1.0))
        act(lambda e: e.activation(PIB, SCB, AF.Exp, bias=0.0, scale=1.0))
        vec(lambda e, b=b: e.tensor_copy(
            PLv[:, :, b * BCH:b * BCH + BCH],
            CB.rearrange("p (a c s) -> p a c s", a=NCC, c=BCH)[:, :, :, 127]))
        vec(lambda e: e.tensor_copy(SCB[:, 1:NCC * BT], CB[:, 0:NCC * BT - 1]))
        vec(lambda e: e.memset(
            SCB.rearrange("p (a c s) -> p a c s", a=NCC, c=BCH)[:, :, :, 0], 1.0))

        if _KPART < 4:
            continue
        build_xf(1, "k")
        for dchunk in range(NCC):
            dma(WST, wk_h[128 * dchunk:128 * dchunk + 128, :])
            for dout in range(NCC):
                ten(lambda e, dchunk=dchunk, dout=dout: e.matmul(
                    bank(dout)[:, 0:BT], WST[:, 128 * dout:128 * dout + 128],
                    XF_B[:, BT * dchunk:BT * dchunk + BT],
                    start=(dchunk == 0), stop=(dchunk == NCC - 1)))
        for dout in range(NCC):
            sl = slice(BT * dout, BT * dout + BT)
            vec(lambda e, dout=dout, sl=sl: e.tensor_mul(KT_B[:, sl],
                                                         bank(dout)[:, 0:BT],
                                                         PIB[:, sl]))
            vec(lambda e, dout=dout, sl=sl: e.tensor_scalar(
                CB[:, sl], bank(dout)[:, 0:BT], UPP[:, dout:dout + 1], None,
                ALU.mult))

        if _KPART < 5:
            continue
        build_xf(3, "r")
        for dchunk in range(NCC):
            dma(WST, wr_h[128 * dchunk:128 * dchunk + 128, :])
            for dout in range(NCC):
                ten(lambda e, dchunk=dchunk, dout=dout: e.matmul(
                    bank(dout)[:, 0:BT], WST[:, 128 * dout:128 * dout + 128],
                    XF_B[:, BT * dchunk:BT * dchunk + BT],
                    start=(dchunk == 0), stop=(dchunk == NCC - 1)))
        for dout in range(NCC):
            sl = slice(BT * dout, BT * dout + BT)
            vec(lambda e, dout=dout, sl=sl: e.tensor_mul(RT_B[:, sl],
                                                         bank(dout)[:, 0:BT],
                                                         SCB[:, sl]))
            vec(lambda e, dout=dout, sl=sl: e.tensor_mul(ASCR,
                                                         bank(dout)[:, 0:BT],
                                                         CB[:, sl]))
            for chd in range(BCH):
                dsl = slice(512 * dout + 256, 512 * dout + 256 + 2)
                ten(lambda e, chd=chd, dsl=dsl: e.matmul(
                    PS[:, dsl], ASCR[:, 128 * chd:128 * chd + 128], IND2,
                    start=True, stop=True))
                act(lambda e, chd=chd, dout=dout, dsl=dsl: e.activation(
                    DTMB[:, 16 * chd + 2 * dout:16 * chd + 2 * dout + 2],
                    PS[:, dsl], AF.Copy, bias=0.0, scale=1.0))

        if _KPART < 6:
            continue
        build_xf(2, "v")
        for dchunk in range(NCC):
            dma(WST, wv_h[128 * dchunk:128 * dchunk + 128, :])
            for ch in range(BCH):
                for half in range(2):
                    ten(lambda e, dchunk=dchunk, ch=ch, half=half: e.matmul(
                        bank(2 * ch + half),
                        XF_B[:, BT * dchunk + 128 * ch:BT * dchunk + 128 * ch + 128],
                        WST[:, 512 * half:512 * half + 512],
                        start=(dchunk == 0), stop=(dchunk == NCC - 1)))
        for ch in range(BCH):
            for half in range(2):
                act(lambda e, ch=ch, half=half: e.activation(
                    V_B[:, C * ch + 512 * half:C * ch + 512 * half + 512],
                    bank(2 * ch + half), AF.Copy, bias=0.0, scale=1.0))

        if _KPART < 7:
            continue
        for ch in range(BCH):
            g = b * BCH + ch
            Scur, Snxt = SBF[g % 2], SBF[(g + 1) % 2]
            for cc in range(NCC):
                vec(lambda e, cc=cc, ch=ch, g=g: e.tensor_scalar(
                    KHF[:, 128 * cc:128 * cc + 128],
                    KT_B[:, BT * cc + 128 * ch:BT * cc + 128 * ch + 128],
                    PLv[:, cc, g:g + 1], None, ALU.mult))
            for cc in range(NCC):
                pb = cc % 2
                ten(lambda e, cc=cc, pb=pb: e.transpose(
                    bank(pb)[:, 0:128], KHF[:, 128 * cc:128 * cc + 128], IDENT))
                act(lambda e, cc=cc, pb=pb: e.activation(
                    KHTM[:, 128 * cc:128 * cc + 128], bank(pb)[:, 0:128], AF.Copy,
                    bias=0.0, scale=1.0))
            if _KWKV < 2:
                continue
            for h in [x for x in range(NH) if x % 2 == 0] + \
                     [x for x in range(NH) if x % 2 == 1]:
                cc, base = h // 2, 64 * (h % 2)
                q, s4 = 4 + h // 4, 128 * (h % 4)
                sl = slice(BT * cc + 128 * ch, BT * cc + 128 * ch + 128)
                ten(lambda e, cc=cc, base=base, q=q, s4=s4, sl=sl: e.matmul(
                    bank(q)[:, s4:s4 + 128], KT_B[base:base + 64, sl],
                    RT_B[base:base + 64, sl], start=True, stop=True,
                    tile_position=(base, 0)))
            for q in range(4):
                for hp in range(2):
                    vec(lambda e, q=q, hp=hp: e.tensor_mul(
                        GM[:, 512 * q + 256 * hp:512 * q + 256 * hp + 256],
                        bank(4 + q)[:, 256 * hp:256 * hp + 256], MSK2))
            if _KWKV < 3:
                continue
            for h in range(NH):
                vsl = slice(C * ch + HD * h, C * ch + HD * h + HD)
                ob = bank(h // 8)[:, (64 * h) % 512:(64 * h) % 512 + 64]
                ten(lambda e, h=h, ob=ob, vsl=vsl: e.matmul(
                    ob, GM[:, 128 * h:128 * h + 128], V_B[:, vsl],
                    start=(h % 8 == 0), stop=False))
            for h in [x for x in range(NH) if x % 2 == 0] + \
                     [x for x in range(NH) if x % 2 == 1]:
                cc, base = h // 2, 64 * (h % 2)
                sl = slice(BT * cc + 128 * ch, BT * cc + 128 * ch + 128)
                ob = bank(h // 8)[:, (64 * h) % 512:(64 * h) % 512 + 64]
                ten(lambda e, cc=cc, base=base, sl=sl, ob=ob, h=h, Scur=Scur:
                    e.matmul(ob, RT_B[base:base + 64, sl],
                             Scur[base:base + 64, HD * (h // 2):HD * (h // 2) + HD],
                             start=False, stop=(h in (7, 15)),
                             tile_position=(base, 0)))
            vec(lambda e, ch=ch: e.tensor_mul(
                YSQ.rearrange("p (h e) -> p h e", h=NH),
                V_B[:, C * ch:C * ch + C].rearrange("p (h e) -> p h e", h=NH),
                DTMB[:, 16 * ch:16 * ch + 16].unsqueeze(2)
                .broadcast_to([128, NH, HD])))
            vec(lambda e: e.tensor_add(YS[:, 0:512], YSQ[:, 0:512], bank(0)))
            vec(lambda e: e.tensor_add(YS[:, 512:1024], YSQ[:, 512:1024], bank(1)))
            if _KWKV < 4:
                continue
            for h in [x for x in range(NH) if x % 2 == 0] + \
                     [x for x in range(NH) if x % 2 == 1]:
                vsl = slice(C * ch + HD * h, C * ch + HD * h + HD)
                ob = bank(2)[64 * (h % 2):64 * (h % 2) + 64,
                             HD * (h // 2):HD * (h // 2) + HD]
                ten(lambda e, h=h, ob=ob, vsl=vsl: e.matmul(
                    ob, KHTM[:, HD * h:HD * h + HD], V_B[:, vsl],
                    start=True, stop=True, tile_position=(0, 64 * (h % 2))))
            for cc in range(NCC):
                vec(lambda e, cc=cc, g=g, Scur=Scur, Snxt=Snxt:
                    e.scalar_tensor_tensor(
                        Snxt[:, HD * cc:HD * cc + HD], Scur[:, HD * cc:HD * cc + HD],
                        PLv[:, cc, g:g + 1], bank(2)[:, HD * cc:HD * cc + HD],
                        ALU.mult, ALU.add))
            if _KWKV < 5:
                continue
            vec(lambda e: e.tensor_mul(YSQ, YS, YS))
            vec(lambda e: e.reduce_sum(GN1, YS.rearrange("p (h e) -> p h e", h=NH),
                                       axis=AX.X))
            vec(lambda e: e.reduce_sum(GN2, YSQ.rearrange("p (h e) -> p h e", h=NH),
                                       axis=AX.X))
            vec(lambda e: e.tensor_single_scalar(GMU, GN1, 1.0 / HD, ALU.mult))
            vec(lambda e: e.tensor_single_scalar(GN2, GN2, 1.0 / HD, ALU.mult))
            vec(lambda e: e.tensor_mul(GRS, GMU, GMU))
            vec(lambda e: e.tensor_sub(GN2, GN2, GRS))
            vec(lambda e: e.tensor_single_scalar(GN2, GN2, GN_EPS, ALU.add))
            act(lambda e: e.activation(GN2, GN2, AF.Sqrt, bias=0.0, scale=1.0))
            vec(lambda e: e.reciprocal(GRS, GN2))
            vec(lambda e: e.tensor_sub(YN.rearrange("p (h e) -> p h e", h=NH),
                                       YS.rearrange("p (h e) -> p h e", h=NH),
                                       GMU.unsqueeze(2).broadcast_to([128, NH, HD])))
            vec(lambda e: e.tensor_mul(YN.rearrange("p (h e) -> p h e", h=NH),
                                       YN.rearrange("p (h e) -> p h e", h=NH),
                                       GRS.unsqueeze(2).broadcast_to([128, NH, HD])))
            if _KWKV < 6:
                continue
            for cc in range(NCC):
                pb = cc % 2
                ten(lambda e, cc=cc, pb=pb: e.transpose(
                    bank(pb)[:, 0:128], YN[:, 128 * cc:128 * cc + 128], IDENT))
                act(lambda e, cc=cc, ch=ch, pb=pb: e.activation(
                    YNFB[:, BT * cc + 128 * ch:BT * cc + 128 * ch + 128],
                    bank(pb)[:, 0:128], AF.Copy, bias=0.0, scale=LNW[:, cc:cc + 1]))

        if _KPART < 8:
            continue
        for ch in range(BCH):
            for half in range(2):
                ten(lambda e, ch=ch, half=half: e.matmul(
                    bank(2 * ch + half), ONER, BIASR[:, 512 * half:512 * half + 512],
                    start=True, stop=False))
        for cc in range(NCC):
            dma(WST, wo_h[128 * cc:128 * cc + 128, :])
            for ch in range(BCH):
                for half in range(2):
                    ten(lambda e, cc=cc, ch=ch, half=half: e.matmul(
                        bank(2 * ch + half),
                        YNFB[:, BT * cc + 128 * ch:BT * cc + 128 * ch + 128],
                        WST[:, 512 * half:512 * half + 512],
                        start=False, stop=(cc == NCC - 1)))
        for ch in range(BCH):
            gt = b * BCH + ch
            dma(XTI, x_h[128 * gt:128 * gt + 128, :])
            for half in range(2):
                vec(lambda e, ch=ch, half=half: e.tensor_add(
                    QA[:, 512 * half:512 * half + 512],
                    XTI[:, 512 * half:512 * half + 512], bank(2 * ch + half)))
            dma(xp_h[128 * gt:128 * gt + 128, :], QA)

    fence()
    attn_stack.close()

    ffn_stack = contextlib.ExitStack()
    sbf = mk_sb(ffn_stack)
    N1T = sbf("n1t", (128, NCC * T), BF16)
    T1SL = sbf("t1sl", (128, 512), BF16)
    INVS1B = sbf("invs1b", (128, T))
    HB0 = sbf("hb0", (128, NHC * 258))
    HB1 = sbf("hb1", (128, NHC * 258))
    GST = sbf("gstash", (128, NHC))
    ACC = sbf("accc", (128, 256))
    SIL = sbf("silc", (128, 256))
    HCT = sbf("hct", (128, 2 * HID))
    N2B = sbf("n2b", (128, HID), BF16)
    QA2 = sbf("qa2", (128, HID))
    IVT = sbf("ivt", (16, 128))
    INVR = sbf("invr", (1, T))
    HBUF = [HB0, HB1]

    for dchunk in range(NCC):
        dma(N1T[:, T * dchunk:T * dchunk + T],
            n1_h[:, 128 * dchunk:128 * dchunk + 128], transpose=True)
    vec(lambda e: e.reciprocal(SC16, QS1))
    vec(lambda e: e.scalar_tensor_tensor(SC16, SC16, 0.5,
                                         Q1.broadcast_to([128, 16]), ALU.mult,
                                         ALU.mult))
    ten(lambda e: e.transpose(bank(0)[0:16, 0:128], SC16, IDENT))
    act(lambda e: e.activation(IVT, bank(0)[0:16, 0:128], AF.Copy, bias=0.0,
                               scale=1.0))
    dma(INVR.rearrange("p (a w) -> p a w", a=16), IVT)
    for qq in range(4):
        ten(lambda e, qq=qq: e.matmul(bank(qq % 2), ONER,
                                      INVR[:, 512 * qq:512 * qq + 512],
                                      start=True, stop=True))
        act(lambda e, qq=qq: e.activation(INVS1B[:, 512 * qq:512 * qq + 512],
                                          bank(qq % 2), AF.Copy, bias=0.0,
                                          scale=1.0))
    vec(lambda e: e.memset(GST, 0.0))

    def conv_eighth(j):
        hb, hn = HBUF[j % 2], HBUF[(j + 1) % 2]
        hbv = hb.rearrange("p (a w) -> p a w", a=NHC)
        vec(lambda e, hbv=hbv: e.tensor_copy(hbv[:, :, 0], GST))
        if j < NE - 1:
            vec(lambda e, hbv=hbv, hn=hn: e.tensor_copy(
                hbv[:, :, 257], hn.rearrange("p (a w) -> p a w", a=NHC)[:, :, 1]))
        else:
            vec(lambda e, hbv=hbv: e.memset(hbv[:, :, 257], 0.0))
        vec(lambda e, hbv=hbv: e.tensor_copy(GST, hbv[:, :, 256]))
        for hc in range(NHC):
            w0 = 258 * hc
            vec(lambda e, w0=w0, hc=hc, hb=hb: e.tensor_scalar(
                ACC, hb[:, w0:w0 + 256], TAP[0][:, hc:hc + 1], None, ALU.mult))
            vec(lambda e, w0=w0, hc=hc, hb=hb: e.scalar_tensor_tensor(
                ACC, hb[:, w0 + 1:w0 + 257], TAP[1][:, hc:hc + 1], ACC,
                ALU.mult, ALU.add))
            vec(lambda e, w0=w0, hc=hc, hb=hb: e.scalar_tensor_tensor(
                ACC, hb[:, w0 + 2:w0 + 258], TAP[2][:, hc:hc + 1], ACC,
                ALU.mult, ALU.add))
            act(lambda e, hc=hc: e.activation(SIL, ACC, AF.Silu,
                                              bias=CBB[:, hc:hc + 1], scale=1.0))
            for chh in range(2):
                pb = (hc * 2 + chh) % 2
                ten(lambda e, chh=chh, pb=pb: e.transpose(
                    bank(pb)[:, 0:128], SIL[:, 128 * chh:128 * chh + 128], IDENT))
                act(lambda e, hc=hc, chh=chh, pb=pb: e.activation(
                    HCT[:, HID * chh + 128 * hc:HID * chh + 128 * hc + 128],
                    bank(pb)[:, 0:128], AF.Copy, bias=0.0, scale=1.0))
        for chh in range(2):
            tc = 2 * j + chh
            hsl = slice(HID * chh, HID * chh + HID)
            vec(lambda e, hsl=hsl: e.reduce_max(RMX, HCT[:, hsl], axis=AX.X,
                                                apply_absolute_value=True))
            vec(lambda e: e.tensor_single_scalar(RMX, RMX, 1e-5, ALU.max))
            vec(lambda e: e.reciprocal(RMX, RMX))
            vec(lambda e, tc=tc: e.tensor_single_scalar(QS2[:, tc:tc + 1], RMX,
                                                        127.0, ALU.mult))
            vec(lambda e, hsl=hsl, tc=tc: e.tensor_scalar(
                QA2, HCT[:, hsl], QS2[:, tc:tc + 1], MAGIC, ALU.mult, ALU.add))
            vec(lambda e: e.tensor_single_scalar(N2B, QA2, MAGIC, ALU.subtract))
            dma(n2_h[128 * tc:128 * tc + 128, :], N2B)

    for j in range(NE if _KPH >= 3 else 0):
        for hg in range(8):
            for dchunk in range(NCC):
                dma(T1SL, t1q_h[512 * hg:512 * hg + 512,
                                128 * dchunk:128 * dchunk + 128], transpose=True)
                for h4 in range(4):
                    ten(lambda e, dchunk=dchunk, h4=h4, j=j: e.matmul(
                        bank(h4)[:, 0:256], T1SL[:, 128 * h4:128 * h4 + 128],
                        N1T[:, T * dchunk + ET * j:T * dchunk + ET * j + ET],
                        start=(dchunk == 0), stop=(dchunk == NCC - 1)))
            for h4 in range(4):
                hc = 4 * hg + h4
                w0 = 258 * hc
                vec(lambda e, h4=h4, w0=w0, j=j: e.tensor_mul(
                    HBUF[j % 2][:, w0 + 1:w0 + 257], bank(h4)[:, 0:256],
                    INVS1B[:, ET * j:ET * j + ET]))
        if j >= 1:
            conv_eighth(j - 1)
    if _KPH >= 3:
        conv_eighth(NE - 1)

    vec(lambda e: e.reciprocal(SC16, QS2))
    vec(lambda e: e.scalar_tensor_tensor(INVS2, SC16, 0.5,
                                         Q2.broadcast_to([128, 16]), ALU.mult,
                                         ALU.mult))

    fence()
    ffn_stack.close()
    ffn2_stack = contextlib.ExitStack()
    sbg = mk_sb(ffn2_stack)
    N2TS = sbg("n2ts", (128, 512), BF16)
    T2SL = sbg("t2sl", (128, C), BF16)
    XRE = sbg("xre", (128, C))
    OST = sbg("ost", (128, C))

    for tg in range(4 if _KPH >= 4 else 0):
        for hc in range(NHC):
            dma(N2TS, n2_h[512 * tg:512 * tg + 512, 128 * hc:128 * hc + 128],
                transpose=True)
            dma(T2SL, t2q_h[:, 128 * hc:128 * hc + 128], transpose=True)
            for t4 in range(4):
                for half in range(2):
                    ten(lambda e, hc=hc, t4=t4, half=half: e.matmul(
                        bank(2 * t4 + half), N2TS[:, 128 * t4:128 * t4 + 128],
                        T2SL[:, 512 * half:512 * half + 512],
                        start=(hc == 0), stop=(hc == NHC - 1)))
        for t4 in range(4):
            tc = 4 * tg + t4
            dma(XRE, xp_h[128 * tc:128 * tc + 128, :])
            for half in range(2):
                vec(lambda e, t4=t4, half=half, tc=tc: e.scalar_tensor_tensor(
                    OST[:, 512 * half:512 * half + 512], bank(2 * t4 + half),
                    INVS2[:, tc:tc + 1], XRE[:, 512 * half:512 * half + 512],
                    ALU.mult, ALU.add))
            dma(out_h[128 * tc:128 * tc + 128, :], OST)

    ffn2_stack.close()
    perm.close()
    return prog


def _extract_rw(nc, prog):
    with nc.Block() as blk:
        for eng in ENGS:
            ops = [p for p in prog if p[0] == eng]
            if not ops:
                continue
            getattr(blk, eng)(lambda e, ops=ops: [(p[1](e), e.nop(nofuse=True))
                                                  for p in ops])

    def keys_of(args):
        out = set()
        for a in args:
            memref = getattr(a, "memref", None)
            if memref is None:
                continue
            off = getattr(a, "offset", 0) or 0
            ap = getattr(a, "ap", None) or []
            span = 1
            for step, count in ap:
                span += abs(step) * (count - 1)
            if memref == "ps":
                b0, b1 = off // 512, (off + span - 1) // 512
                for bk in range(b0, b1 + 1):
                    out.add("ps:%d" % bk)
            else:
                out.add(memref)
        return out

    groups = {eng: [] for eng in ENGS}
    cur = {eng: ([], []) for eng in ENGS}
    for inst in nc.all_instructions():
        eng = {"PE": "tensor", "Activation": "scalar", "DVE": "vector",
               "Pool": "gpsimd", "SP": "sync"}.get(str(inst.engine).split(".")[-1])
        if eng is None:
            continue
        if type(inst).__name__ == "InstNoOp":
            r, w = cur[eng]
            groups[eng].append((frozenset(r), frozenset(w)))
            cur[eng] = ([], [])
        else:
            r, w = cur[eng]
            r.extend(keys_of(inst.ins))
            w.extend(keys_of(inst.outs))
    idx = {eng: 0 for eng in ENGS}
    rw = []
    for p in prog:
        if p[0] == "FENCE":
            rw.append(None)
            continue
        eng = p[0]
        rw.append(groups[eng][idx[eng]])
        idx[eng] += 1
    return rw


def build_nc():
    nc1 = bass.Bass("TRN2", target_bir_lowering=False)
    prog1 = build_prog(nc1)
    rw = _extract_rw(nc1, prog1)

    nc = bass.Bass("TRN2", target_bir_lowering=False)
    prog = build_prog(nc)
    assert len(prog) == len(prog1)

    KP = 10
    tokens = []
    cnt = {e: 0 for e in ENGS}
    dmai = {e: 0 for e in ENGS}
    for p in prog:
        if p[0] == "FENCE":
            tokens.append(None)
            continue
        eng, fn, inc = p
        if inc == 16:
            j = dmai[eng]
            dmai[eng] += 1
            key = ("dma", eng, j % KP)
            val = 16 * (j // KP + 1)
            selfneed = ((key, val - 16) if j >= KP else None)
        else:
            cnt[eng] += 1
            key = ("eng", eng)
            val = cnt[eng]
            selfneed = None
        tokens.append((key, val, selfneed))

    writer = {}
    readers = {}
    fence_need = {}
    sched = []
    ti = 0
    for p, k in zip(prog, rw):
        if p[0] == "FENCE":
            sched.append(("FENCE", None, 0, None, dict(fence_need)))
            ti += 1
            continue
        eng, fn, inc = p
        tok = tokens[ti]
        ti += 1
        key, val, selfneed = tok
        rds, wrs = k
        need = {}

        def add(tk):
            if tk is None:
                return
            tkey, tval = tk
            if tkey == ("eng", eng) and eng == "tensor":
                return
            if need.get(tkey, 0) < tval:
                need[tkey] = tval
        add(selfneed)
        for rk in rds:
            add(writer.get(rk))
        for rk in wrs:
            add(writer.get(rk))
            for tkey, tval in readers.get(rk, {}).items():
                add((tkey, tval))
        sched.append((eng, fn, inc, (key, val), need))
        for rk in rds:
            readers.setdefault(rk, {})[key] = val
        for rk in wrs:
            writer[rk] = (key, val)
            readers[rk] = {key: val}
        fence_need[key] = val

    finals = dict(fence_need)

    semkeys = sorted(set(fence_need.keys()), key=str)
    with contextlib.ExitStack() as es:
        sems = {sk: es.enter_context(nc.semaphore("s_" + "_".join(map(str, sk))))
                for sk in semkeys}
        with nc.Block() as block:
            def emit(name, handle):
                water = {sk: 0 for sk in semkeys}
                for eng, fn, inc, tok, need in sched:
                    if eng == "FENCE":
                        for sk, v in need.items():
                            if sk == ("eng", name):
                                continue
                            if water[sk] < v:
                                handle.wait_ge(sems[sk], v)
                                water[sk] = v
                        continue
                    if eng != name:
                        continue
                    for sk, v in sorted(need.items(), key=str):
                        if water[sk] < v:
                            handle.wait_ge(sems[sk], v)
                            if sk != ("eng", name):
                                water[sk] = v
                    fn(handle).then_inc(sems[tok[0]], inc)
                if name == "gpsimd":
                    for sk, v in sorted(finals.items(), key=str):
                        if sk != ("eng", name) and water[sk] < v:
                            handle.wait_ge(sems[sk], v)

            block.sync(lambda e: emit("sync", e))
            block.vector(lambda e: emit("vector", e))
            block.scalar(lambda e: emit("scalar", e))
            block.tensor(lambda e: emit("tensor", e))
            block.gpsimd(lambda e: emit("gpsimd", e))

    return nc


_NC_CACHE = []


def kernel(L0_attn_query, L0_attn_key, L0_attn_value, L0_attn_time_maa_x,
           L0_attn_time_maa_w, L0_attn_time_maa_k, L0_attn_time_maa_v,
           L0_attn_time_maa_r, L0_attn_time_maa_w1, L0_attn_time_maa_w2,
           L0_attn_time_decay, L0_attn_time_decay_w1, L0_attn_time_decay_w2,
           L0_attn_time_faaaa, L0_attn_W_r, L0_attn_W_k, L0_attn_W_v, L0_attn_W_o,
           L0_attn_ln_w, L0_attn_ln_b, L0_ffn_w1, L0_ffn_conv_w, L0_ffn_conv_b,
           L0_ffn_w2):
    x = np.ascontiguousarray(np.asarray(L0_attn_query, dtype=np.float32))
    w = {
        "wr": L0_attn_W_r, "wk": L0_attn_W_k, "wv": L0_attn_W_v, "wo": L0_attn_W_o,
        "maa_x": L0_attn_time_maa_x, "maa_w": L0_attn_time_maa_w,
        "maa_k": L0_attn_time_maa_k, "maa_v": L0_attn_time_maa_v,
        "maa_r": L0_attn_time_maa_r, "mw1": L0_attn_time_maa_w1,
        "mw2": L0_attn_time_maa_w2, "dec": L0_attn_time_decay,
        "dw1": L0_attn_time_decay_w1, "dw2": L0_attn_time_decay_w2,
        "faaaa": L0_attn_time_faaaa, "lnw": L0_attn_ln_w, "lnb": L0_attn_ln_b,
        "w1": L0_ffn_w1, "cw": L0_ffn_conv_w, "cb": L0_ffn_conv_b, "w2": L0_ffn_w2,
    }
    w = {k: np.ascontiguousarray(np.asarray(v, dtype=np.float32))
         for k, v in w.items()}
    if not _NC_CACHE:
        _NC_CACHE.append(build_nc())
    nc = _NC_CACHE[0]
    in_maps = [dict(w, x=np.ascontiguousarray(x[i])) for i in range(8)]
    res = run_bass_kernel_spmd(nc, in_maps, core_ids=list(range(8)))
    return np.stack([np.asarray(r["out"], dtype=np.float32)
                     for r in res.results], axis=0)


# version 05
# speedup vs baseline: 2.1534x; 1.5822x over previous
import contextlib
import os

import numpy as np
import concourse.bass as bass
import concourse.mybir as mybir
from concourse.bass_utils import run_bass_kernel_spmd

F32 = mybir.dt.float32
BF16 = mybir.dt.bfloat16
AF = mybir.ActivationFunctionType
ALU = mybir.AluOpType
AX = mybir.AxisListType

MAGIC = 12582912.0
T, C, HID, NH, HD = 2048, 1024, 4096, 16, 64
NCC = 8
NHC = 32
NB, BT, BCH = 8, 256, 2
NE, ET = 8, 256
LN_EPS, GN_EPS = 1e-5, 0.00064
MEAN_W1 = 1.0 / (HID * C)
MEAN_W2 = 1.0 / (C * HID)
_KPH = int(os.environ.get("KPHASE", "99"))
_KBLK = int(os.environ.get("KBLK", "99"))
_KPART = int(os.environ.get("KPART", "99"))
_KWKV = int(os.environ.get("KWKV", "99"))


ENGS = ("sync", "vector", "scalar", "tensor", "gpsimd")


def build_prog(nc):

    def din(name, shape, dt=F32):
        return nc.dram_tensor(name, list(shape), dt, kind="ExternalInput").ap()

    x_h = din("x", (T, C))
    wr_h, wk_h, wv_h, wo_h = (din(n, (C, C)) for n in ("wr", "wk", "wv", "wo"))
    maa_h = {f: din("maa_" + f, (C,)) for f in ("x", "w", "k", "v", "r")}
    mw1_h = din("mw1", (C, 128))
    mw2_h = din("mw2", (4, 32, C))
    dec_h = din("dec", (C,))
    dw1_h = din("dw1", (C, 64))
    dw2_h = din("dw2", (64, C))
    faaaa_h = din("faaaa", (NH, HD))
    lnw_h, lnb_h = din("lnw", (C,)), din("lnb", (C,))
    w1_h = din("w1", (HID, C))
    cw_h = din("cw", (HID, 1, 3))
    cb_h = din("cb", (HID,))
    w2_h = din("w2", (C, HID))

    out_h = nc.dram_tensor("out", [T, C], F32, kind="ExternalOutput").ap()
    n1_h = nc.dram_tensor("n1s", [T, C], BF16, kind="Internal").ap()
    t1q_h = nc.dram_tensor("t1qs", [HID, C], BF16, kind="Internal").ap()
    n2_h = nc.dram_tensor("n2s", [T, HID], BF16, kind="Internal").ap()
    t2q_h = nc.dram_tensor("t2qs", [C, HID], BF16, kind="Internal").ap()
    xp_h = nc.dram_tensor("xps", [T, C], F32, kind="Internal").ap()

    def mk_sb(stack):
        def sb(name, shape, dt=F32):
            return stack.enter_context(nc.sbuf_tensor(name, list(shape), dt)).ap()
        return sb

    perm = contextlib.ExitStack()
    sb = mk_sb(perm)

    IDENT = sb("ident", (128, 128))
    MSK2 = sb("msk2", (128, 256))
    ONEC = sb("onec", (128, 1))
    ONER = sb("oner", (1, 128))
    IND2 = sb("ind2", (128, 2))
    NH5 = sb("nh5", (128, 1))
    PH5 = sb("ph5", (128, 1))
    ZER = sb("zer", (128, 128))
    TAP = [sb("tap%d" % i, (128, NHC)) for i in range(3)]
    CBB = sb("cbpp", (128, NHC))
    S1c = sb("s1c", (128, 1))
    S2c = sb("s2c", (128, 1))
    MUc = sb("muc", (128, 1))
    RSc = sb("rsc", (128, 1))
    NBc = sb("nbc", (128, 1))
    QS1 = sb("qs1", (128, 16))
    QS2 = sb("qs2", (128, 16))
    INVS2 = sb("invs2", (128, 16))
    SC16 = sb("sc16", (128, 16))
    RMX = sb("rmx", (128, 1))
    PL = sb("plx", (128, 128))
    Q1, Q2 = sb("q1", (128, 1)), sb("q2", (128, 1))
    SONE = sb("sone", (1, 1))
    PLv = PL.rearrange("p (a g) -> p a g", a=NCC)

    prog = []

    def fence():
        prog.append(("FENCE", None, 0))

    def op(eng, inc=1):
        def deco(fn):
            prog.append((eng, fn, inc))
        return deco

    def dma(out, in_, transpose=False, q="sync"):
        @op(q, 16)
        def _(e, out=out, in_=in_, tr=transpose):
            if tr:
                return e.dma_start(out=out, in_=in_, transpose=True)
            return e.dma_start(out=out, in_=in_)

    def vec(fn):
        @op("vector")
        def _(e, fn=fn):
            return fn(e)

    def act(fn):
        @op("scalar")
        def _(e, fn=fn):
            return fn(e)

    def ten(fn):
        @op("tensor")
        def _(e, fn=fn):
            return fn(e)

    def gps(fn):
        @op("gpsimd")
        def _(e, fn=fn):
            return fn(e)

    vec(lambda e: e.memset(ZER, 0.0))
    vec(lambda e: e.memset(IDENT, 1.0))
    gps(lambda e: e.affine_select(IDENT, IDENT, [[-1, 128]], ALU.is_equal,
                                  0.0, base=0, channel_multiplier=1))
    gps(lambda e: e.iota(MSK2.rearrange("p (a b) -> p a b", a=2), [[0, 2], [1, 128]],
                         base=0, channel_multiplier=-1,
                         allow_small_or_imprecise_dtypes=True))
    vec(lambda e: e.tensor_single_scalar(MSK2, MSK2, 0.0, ALU.is_gt))
    vec(lambda e: e.memset(ONEC, 1.0))
    vec(lambda e: e.memset(ONER, 1.0))
    vec(lambda e: e.memset(NH5, -0.5))
    vec(lambda e: e.memset(PH5, 0.5))
    vec(lambda e: e.memset(IND2, 0.0))
    vec(lambda e: e.memset(IND2[0:64, 0:1], 1.0))
    vec(lambda e: e.memset(IND2[64:128, 1:2], 1.0))

    PSH = nc.alloc_psum_tensor("ps", [128, 4096], F32)
    PS = PSH.ap()

    def bank(k, w=512):
        return PS[:, 512 * k:512 * k + w]

    cw_stack = contextlib.ExitStack()
    sbc = mk_sb(cw_stack)
    CWS = sbc("cws", (32, 384))
    CBS = sbc("cbs", (32, 128))
    dma(CWS, cw_h.rearrange("(a b) o t -> a (b o t)", a=32))
    dma(CBS, cb_h.rearrange("(a b) -> a b", a=32))
    for tau in range(3):
        ten(lambda e, tau=tau: e.transpose(
            bank(0, 32)[:, 0:32],
            CWS.rearrange("a (b t) -> a b t", t=3)[:, :, tau], IDENT[0:32, 0:32]))
        act(lambda e, tau=tau: e.activation(TAP[tau], bank(0, 32)[:, 0:32], AF.Copy,
                                            bias=0.0, scale=1.0))
    ten(lambda e: e.transpose(bank(0, 32)[:, 0:32], CBS, IDENT[0:32, 0:32]))
    act(lambda e: e.activation(CBB, bank(0, 32)[:, 0:32], AF.Copy, bias=0.0,
                               scale=1.0))
    fence()
    cw_stack.close()

    wq_stack = contextlib.ExitStack()
    sbw = mk_sb(wq_stack)
    WQT = sbw("wqt", (128, HID))
    WQA = sbw("wqa", (128, HID))
    WQB = sbw("wqb", (128, HID))
    WQO = sbw("wqo", (128, HID), BF16)
    WSUM = sbw("wsum", (128, 32))
    SW = sbw("sw", (128, 1))

    def wquant(w_hbm, ntile, cols, meanmul, qtile, tq_hbm):
        for wt in range(ntile):
            dma(WQT[:, 0:cols], w_hbm[128 * wt:128 * wt + 128, :])
            act(lambda e, wt=wt, cols=cols: e.activation(
                WQA[:, 0:cols], WQT[:, 0:cols], AF.Abs, bias=0.0, scale=1.0,
                accum_out=WSUM[:, wt:wt + 1]))
        vec(lambda e, ntile=ntile: e.reduce_sum(RMX, WSUM[:, 0:ntile], axis=AX.X))
        ten(lambda e: e.matmul(bank(0)[0:1, 0:1], RMX, ONEC, start=True, stop=True))
        act(lambda e: e.activation(SONE, bank(0)[0:1, 0:1], AF.Copy, bias=0.0,
                                   scale=1.0))
        vec(lambda e, mm=meanmul: e.tensor_scalar(SONE, SONE, mm, 1e-5, ALU.mult,
                                                  ALU.max))
        ten(lambda e: e.matmul(bank(0)[:, 0:1], ONER, SONE, start=True, stop=True))
        act(lambda e, qtile=qtile: e.activation(qtile, bank(0)[:, 0:1], AF.Copy,
                                                bias=0.0, scale=1.0))
        vec(lambda e, qtile=qtile: e.reciprocal(SW, qtile))
        for wt in range(ntile):
            dma(WQT[:, 0:cols], w_hbm[128 * wt:128 * wt + 128, :])
            act(lambda e, cols=cols: e.activation(WQA[:, 0:cols], WQT[:, 0:cols],
                                                  AF.Sign, bias=NH5, scale=SW))
            act(lambda e, cols=cols: e.activation(WQB[:, 0:cols], WQT[:, 0:cols],
                                                  AF.Sign, bias=PH5, scale=SW))
            vec(lambda e, cols=cols: e.tensor_add(WQO[:, 0:cols], WQA[:, 0:cols],
                                                  WQB[:, 0:cols]))
            dma(tq_hbm[128 * wt:128 * wt + 128, :], WQO[:, 0:cols], q="gpsimd")

    wquant(w1_h, 32, C, MEAN_W1, Q1, t1q_h)
    wquant(w2_h, 8, HID, MEAN_W2, Q2, t2q_h)
    fence()
    wq_stack.close()

    attn_stack = contextlib.ExitStack()
    sba = mk_sb(attn_stack)
    MAA = {f: sba("maa" + f, (128, NCC)) for f in ("x", "w", "k", "v", "r")}
    DEC = sba("decpp", (128, NCC))
    UPP = sba("upp", (128, NCC))
    LNW = sba("lnwpp", (128, NCC))
    LNB = sba("lnbpp", (128, NCC))
    MW1 = sba("mw1t", (128, NCC * 128))
    MW2 = sba("mw2t", (128, NCC * 128))
    DW1 = sba("dw1t", (128, NCC * 64))
    DW2 = sba("dw2t", (128, NCC * 128))
    BIASR = sba("biasr", (1, C))
    XTI = sba("xti", (128, C))
    XNTI = sba("xnti", (128, C))
    QA = sba("qa", (128, C))
    N1B = sba("n1b", (128, C), BF16)
    XN_B = sba("xn_b", (128, NCC * BT))
    XX_B = sba("xx_b", (128, NCC * BT))
    XF_B = sba("xf_b", (128, NCC * BT))
    XPC = sba("xpc", (128, NCC))
    TSC = sba("tsc", (128, BT))
    X1 = sba("x1b", (128, BT))
    XD = sba("xdb", (128, BT))
    CB = sba("cbuf", (128, NCC * BT))
    SCB = sba("scb", (128, NCC * BT))
    PIB = sba("pib", (128, NCC * BT))
    KT_B = sba("kt_b", (128, NCC * BT))
    RT_B = sba("rt_b", (128, NCC * BT))
    ASCR = sba("ascr", (128, BT))
    V_B = sba("v_b", (128, BCH * C))
    KHF = sba("khf", (128, NCC * 128))
    KHTM = sba("khtm", (128, NCC * 128))
    DTMB = sba("dtmb", (128, BCH * 16))
    GM = sba("gm", (128, NH * 128))
    SBF = [sba("s0b", (128, NCC * HD)), sba("s1b", (128, NCC * HD))]
    YS = sba("ys", (128, C))
    YSQ = sba("ysq", (128, C))
    YN = sba("yn", (128, C))
    GN1, GN2 = sba("gn1", (128, 16)), sba("gn2", (128, 16))
    GMU, GRS = sba("gmu", (128, 16)), sba("grs", (128, 16))
    YNFB = sba("ynfb", (128, NCC * BT))
    WST = [sba("wstr0", (128, C)), sba("wstr1", (128, C))]

    VST8 = sba("vst8", (64, 128))
    vec_targets = [(maa_h["x"], MAA["x"]), (maa_h["w"], MAA["w"]),
                   (maa_h["k"], MAA["k"]), (maa_h["v"], MAA["v"]),
                   (maa_h["r"], MAA["r"]), (dec_h, DEC), (lnw_h, LNW),
                   (lnb_h, LNB)]
    for src_h, tgt in vec_targets:
        dma(VST8[0:8, :], src_h.rearrange("(a b) -> a b", a=8))
        ten(lambda e: e.transpose(bank(2)[:, 0:8], VST8[0:8, :], IDENT[0:8, 0:8]))
        act(lambda e, tgt=tgt: e.activation(tgt, bank(2)[:, 0:8], AF.Copy,
                                            bias=0.0, scale=1.0))
    dma(VST8[0:16, 0:64], faaaa_h)
    ten(lambda e: e.transpose(bank(2)[0:64, 0:16], VST8[0:16, 0:64],
                              IDENT[0:16, 0:16]))
    act(lambda e: e.activation(VST8[0:64, 0:16], bank(2)[0:64, 0:16],
                               AF.Copy, bias=0.0, scale=1.0))
    for hp in range(2):
        vec(lambda e, hp=hp: e.tensor_copy(
            UPP[64 * hp:64 * hp + 64, :],
            VST8[0:64, 0:16].rearrange("p (c two) -> p c two", two=2)[:, :, hp]))
    dma(MW1.rearrange("p (a l) -> p a l", a=NCC), mw1_h.rearrange("(a b) l -> b a l", b=128))
    for f in range(4):
        dma(MW2[32 * f:32 * f + 32, :], mw2_h[f])
    dma(DW1.rearrange("p (a l) -> p a l", a=NCC), dw1_h.rearrange("(a b) l -> b a l", b=128))
    dma(DW2[0:64, :], dw2_h)
    vec(lambda e: e.memset(XPC, 0.0))
    vec(lambda e: e.memset(SBF[0], 0.0))

    for cc in range(NCC):
        wst = WST[cc % 2]
        dma(wst, wo_h[128 * cc:128 * cc + 128, :])
        for half in range(2):
            ten(lambda e, cc=cc, half=half, wst=wst: e.matmul(
                bank(half)[0:1, :], LNB[:, cc:cc + 1],
                wst[:, 512 * half:512 * half + 512],
                start=(cc == 0), stop=(cc == NCC - 1)))
    for half in range(2):
        act(lambda e, half=half: e.activation(
            BIASR[:, 512 * half:512 * half + 512], bank(half)[0:1, :], AF.Copy,
            bias=0.0, scale=1.0))

    def lora2_mm(f, cc):
        ten(lambda e, f=f, cc=cc: e.matmul(
            bank(6)[:, 0:BT], MW2[32 * f:32 * f + 32, 128 * cc:128 * cc + 128],
            X1[32 * f:32 * f + 32, :], start=True, stop=True,
            tile_position=(32 * f, 0)))

    def build_xf(f, mf):
        for cc in range(NCC):
            lora2_mm(f, cc)
            sl = slice(BT * cc, BT * cc + BT)
            vec(lambda e, cc=cc, sl=sl, mf=mf: e.scalar_tensor_tensor(
                TSC, bank(6)[:, 0:BT], MAA[mf][:, cc:cc + 1], XX_B[:, sl],
                ALU.add, ALU.mult))
            vec(lambda e, sl=sl: e.tensor_add(XF_B[:, sl], TSC, XN_B[:, sl]))

    for b in range(min(NB if _KPH >= 2 else 0, _KBLK)):
        for tl in range(BCH):
            gt = b * BCH + tl
            dma(XTI, x_h[128 * gt:128 * gt + 128, :])
            vec(lambda e: e.reduce_sum(S1c, XTI, axis=AX.X))
            act(lambda e: e.activation(QA, XTI, AF.Square, bias=0.0, scale=1.0,
                                       accum_out=S2c))
            vec(lambda e: e.tensor_single_scalar(MUc, S1c, 1.0 / C, ALU.mult))
            vec(lambda e: e.tensor_single_scalar(S2c, S2c, 1.0 / C, ALU.mult))
            vec(lambda e: e.tensor_mul(RSc, MUc, MUc))
            vec(lambda e: e.tensor_sub(S2c, S2c, RSc))
            vec(lambda e: e.tensor_single_scalar(S2c, S2c, LN_EPS, ALU.add))
            act(lambda e: e.activation(S2c, S2c, AF.Sqrt, bias=0.0, scale=1.0))
            vec(lambda e: e.reciprocal(RSc, S2c))
            vec(lambda e: e.scalar_tensor_tensor(NBc, MUc, -1.0, RSc, ALU.mult,
                                                 ALU.mult))
            vec(lambda e: e.tensor_scalar(XNTI, XTI, RSc, NBc, ALU.mult, ALU.add))
            vec(lambda e: e.reduce_max(RMX, XNTI, axis=AX.X,
                                       apply_absolute_value=True))
            vec(lambda e: e.tensor_single_scalar(RMX, RMX, 1e-5, ALU.max))
            vec(lambda e: e.reciprocal(RMX, RMX))
            vec(lambda e, gt=gt: e.tensor_single_scalar(QS1[:, gt:gt + 1], RMX,
                                                        127.0, ALU.mult))
            vec(lambda e, gt=gt: e.tensor_scalar(QA, XNTI, QS1[:, gt:gt + 1],
                                                 MAGIC, ALU.mult, ALU.add))
            vec(lambda e: e.tensor_single_scalar(N1B, QA, MAGIC, ALU.subtract))
            dma(n1_h[128 * gt:128 * gt + 128, :], N1B, q="gpsimd")
            for cc in range(NCC):
                pb = cc % 2
                ten(lambda e, cc=cc, pb=pb: e.transpose(
                    bank(pb)[:, 0:128], XNTI[:, 128 * cc:128 * cc + 128], IDENT))
                act(lambda e, tl=tl, cc=cc, pb=pb: e.activation(
                    XN_B[:, BT * cc + 128 * tl:BT * cc + 128 * tl + 128],
                    bank(pb)[:, 0:128], AF.Copy, bias=0.0, scale=1.0))
        if _KPART < 2:
            continue
        for cc in range(NCC):
            s0 = BT * cc
            vec(lambda e, s0=s0: e.tensor_sub(XX_B[:, s0 + 1:s0 + BT],
                                              XN_B[:, s0:s0 + BT - 1],
                                              XN_B[:, s0 + 1:s0 + BT]))
            vec(lambda e, s0=s0, cc=cc: e.tensor_sub(XX_B[:, s0:s0 + 1],
                                                     XPC[:, cc:cc + 1],
                                                     XN_B[:, s0:s0 + 1]))
        vec(lambda e: e.tensor_copy(
            XPC, XN_B.rearrange("p (a w) -> p a w", a=NCC)[:, :, BT - 1]))
        for cc in range(NCC):
            sl = slice(BT * cc, BT * cc + BT)
            vec(lambda e, cc=cc, sl=sl: e.scalar_tensor_tensor(
                XF_B[:, sl], XX_B[:, sl], MAA["x"][:, cc:cc + 1], XN_B[:, sl],
                ALU.mult, ALU.add))
        for cc in range(NCC):
            ten(lambda e, cc=cc: e.matmul(bank(0)[:, 0:BT],
                                          MW1[:, 128 * cc:128 * cc + 128],
                                          XF_B[:, BT * cc:BT * cc + BT],
                                          start=(cc == 0), stop=(cc == NCC - 1)))
        act(lambda e: e.activation(X1, bank(0)[:, 0:BT], AF.Tanh, bias=0.0,
                                   scale=1.0))

        if _KPART < 3:
            continue
        build_xf(0, "w")
        for cc in range(NCC):
            ten(lambda e, cc=cc: e.matmul(bank(0)[0:64, 0:BT],
                                          DW1[:, 64 * cc:64 * cc + 64],
                                          XF_B[:, BT * cc:BT * cc + BT],
                                          start=(cc == 0), stop=(cc == NCC - 1)))
        act(lambda e: e.activation(XD[0:64, :], bank(0)[0:64, 0:BT], AF.Tanh,
                                   bias=0.0, scale=1.0))
        for cc in range(NCC):
            ten(lambda e, cc=cc: e.matmul(bank(1)[:, 0:BT],
                                          DW2[0:64, 128 * cc:128 * cc + 128],
                                          XD[0:64, :], start=True, stop=True))
            act(lambda e, cc=cc: e.activation(CB[:, BT * cc:BT * cc + BT],
                                              bank(1)[:, 0:BT], AF.Exp,
                                              bias=DEC[:, cc:cc + 1], scale=1.0))
        for cc in range(NCC):
            for ch in range(BCH):
                sl = slice(BT * cc + 128 * ch, BT * cc + 128 * ch + 128)
                vec(lambda e, sl=sl: e.tensor_tensor_scan(SCB[:, sl], CB[:, sl],
                                                          ZER, 0.0, ALU.add,
                                                          ALU.add))
        act(lambda e: e.activation(CB, SCB, AF.Exp, bias=0.0, scale=-1.0))
        act(lambda e: e.activation(PIB, SCB, AF.Exp, bias=0.0, scale=1.0))
        vec(lambda e, b=b: e.tensor_copy(
            PLv[:, :, b * BCH:b * BCH + BCH],
            CB.rearrange("p (a c s) -> p a c s", a=NCC, c=BCH)[:, :, :, 127]))
        vec(lambda e: e.tensor_copy(SCB[:, 1:NCC * BT], CB[:, 0:NCC * BT - 1]))
        vec(lambda e: e.memset(
            SCB.rearrange("p (a c s) -> p a c s", a=NCC, c=BCH)[:, :, :, 0], 1.0))

        if _KPART < 4:
            continue
        build_xf(1, "k")
        for dchunk in range(NCC):
            wst = WST[dchunk % 2]
            dma(wst, wk_h[128 * dchunk:128 * dchunk + 128, :])
            for dout in range(NCC):
                ten(lambda e, dchunk=dchunk, dout=dout, wst=wst: e.matmul(
                    bank(dout)[:, 0:BT], wst[:, 128 * dout:128 * dout + 128],
                    XF_B[:, BT * dchunk:BT * dchunk + BT],
                    start=(dchunk == 0), stop=(dchunk == NCC - 1)))
        for dout in range(NCC):
            sl = slice(BT * dout, BT * dout + BT)
            vec(lambda e, dout=dout, sl=sl: e.tensor_mul(KT_B[:, sl],
                                                         bank(dout)[:, 0:BT],
                                                         PIB[:, sl]))
            vec(lambda e, dout=dout, sl=sl: e.tensor_scalar(
                CB[:, sl], bank(dout)[:, 0:BT], UPP[:, dout:dout + 1], None,
                ALU.mult))

        if _KPART < 5:
            continue
        build_xf(3, "r")
        for dchunk in range(NCC):
            wst = WST[dchunk % 2]
            dma(wst, wr_h[128 * dchunk:128 * dchunk + 128, :])
            for dout in range(NCC):
                ten(lambda e, dchunk=dchunk, dout=dout, wst=wst: e.matmul(
                    bank(dout)[:, 0:BT], wst[:, 128 * dout:128 * dout + 128],
                    XF_B[:, BT * dchunk:BT * dchunk + BT],
                    start=(dchunk == 0), stop=(dchunk == NCC - 1)))
        for dout in range(NCC):
            sl = slice(BT * dout, BT * dout + BT)
            vec(lambda e, dout=dout, sl=sl: e.tensor_mul(RT_B[:, sl],
                                                         bank(dout)[:, 0:BT],
                                                         SCB[:, sl]))
            vec(lambda e, dout=dout, sl=sl: e.tensor_mul(ASCR,
                                                         bank(dout)[:, 0:BT],
                                                         CB[:, sl]))
            for chd in range(BCH):
                dsl = slice(512 * dout + 256, 512 * dout + 256 + 2)
                ten(lambda e, chd=chd, dsl=dsl: e.matmul(
                    PS[:, dsl], ASCR[:, 128 * chd:128 * chd + 128], IND2,
                    start=True, stop=True))
                act(lambda e, chd=chd, dout=dout, dsl=dsl: e.activation(
                    DTMB[:, 16 * chd + 2 * dout:16 * chd + 2 * dout + 2],
                    PS[:, dsl], AF.Copy, bias=0.0, scale=1.0))

        if _KPART < 6:
            continue
        build_xf(2, "v")
        for dchunk in range(NCC):
            wst = WST[dchunk % 2]
            dma(wst, wv_h[128 * dchunk:128 * dchunk + 128, :])
            for ch in range(BCH):
                for half in range(2):
                    ten(lambda e, dchunk=dchunk, ch=ch, half=half, wst=wst:
                        e.matmul(
                            bank(2 * ch + half),
                            XF_B[:, BT * dchunk + 128 * ch:BT * dchunk + 128 * ch + 128],
                            wst[:, 512 * half:512 * half + 512],
                            start=(dchunk == 0), stop=(dchunk == NCC - 1)))
        for ch in range(BCH):
            for half in range(2):
                act(lambda e, ch=ch, half=half: e.activation(
                    V_B[:, C * ch + 512 * half:C * ch + 512 * half + 512],
                    bank(2 * ch + half), AF.Copy, bias=0.0, scale=1.0))

        if _KPART < 7:
            continue
        for ch in range(BCH):
            g = b * BCH + ch
            Scur, Snxt = SBF[g % 2], SBF[(g + 1) % 2]
            for cc in range(NCC):
                vec(lambda e, cc=cc, ch=ch, g=g: e.tensor_scalar(
                    KHF[:, 128 * cc:128 * cc + 128],
                    KT_B[:, BT * cc + 128 * ch:BT * cc + 128 * ch + 128],
                    PLv[:, cc, g:g + 1], None, ALU.mult))
            for cc in range(NCC):
                pb = cc % 2
                ten(lambda e, cc=cc, pb=pb: e.transpose(
                    bank(pb)[:, 0:128], KHF[:, 128 * cc:128 * cc + 128], IDENT))
                act(lambda e, cc=cc, pb=pb: e.activation(
                    KHTM[:, 128 * cc:128 * cc + 128], bank(pb)[:, 0:128], AF.Copy,
                    bias=0.0, scale=1.0))
            if _KWKV < 2:
                continue
            for h in [x for x in range(NH) if x % 2 == 0] + \
                     [x for x in range(NH) if x % 2 == 1]:
                cc, base = h // 2, 64 * (h % 2)
                q, s4 = 4 + h // 4, 128 * (h % 4)
                sl = slice(BT * cc + 128 * ch, BT * cc + 128 * ch + 128)
                ten(lambda e, cc=cc, base=base, q=q, s4=s4, sl=sl: e.matmul(
                    bank(q)[:, s4:s4 + 128], KT_B[base:base + 64, sl],
                    RT_B[base:base + 64, sl], start=True, stop=True,
                    tile_position=(base, 0)))
            for q in range(4):
                for hp in range(2):
                    vec(lambda e, q=q, hp=hp: e.tensor_mul(
                        GM[:, 512 * q + 256 * hp:512 * q + 256 * hp + 256],
                        bank(4 + q)[:, 256 * hp:256 * hp + 256], MSK2))
            if _KWKV < 3:
                continue
            for h in range(NH):
                vsl = slice(C * ch + HD * h, C * ch + HD * h + HD)
                ob = bank(h // 8)[:, (64 * h) % 512:(64 * h) % 512 + 64]
                ten(lambda e, h=h, ob=ob, vsl=vsl: e.matmul(
                    ob, GM[:, 128 * h:128 * h + 128], V_B[:, vsl],
                    start=(h % 8 == 0), stop=False))
            for h in [x for x in range(NH) if x % 2 == 0] + \
                     [x for x in range(NH) if x % 2 == 1]:
                cc, base = h // 2, 64 * (h % 2)
                sl = slice(BT * cc + 128 * ch, BT * cc + 128 * ch + 128)
                ob = bank(h // 8)[:, (64 * h) % 512:(64 * h) % 512 + 64]
                ten(lambda e, cc=cc, base=base, sl=sl, ob=ob, h=h, Scur=Scur:
                    e.matmul(ob, RT_B[base:base + 64, sl],
                             Scur[base:base + 64, HD * (h // 2):HD * (h // 2) + HD],
                             start=False, stop=(h in (7, 15)),
                             tile_position=(base, 0)))
            vec(lambda e, ch=ch: e.tensor_mul(
                YSQ.rearrange("p (h e) -> p h e", h=NH),
                V_B[:, C * ch:C * ch + C].rearrange("p (h e) -> p h e", h=NH),
                DTMB[:, 16 * ch:16 * ch + 16].unsqueeze(2)
                .broadcast_to([128, NH, HD])))
            vec(lambda e: e.tensor_add(YS[:, 0:512], YSQ[:, 0:512], bank(0)))
            vec(lambda e: e.tensor_add(YS[:, 512:1024], YSQ[:, 512:1024], bank(1)))
            if _KWKV < 4:
                continue
            for h in [x for x in range(NH) if x % 2 == 0] + \
                     [x for x in range(NH) if x % 2 == 1]:
                vsl = slice(C * ch + HD * h, C * ch + HD * h + HD)
                ob = bank(2)[64 * (h % 2):64 * (h % 2) + 64,
                             HD * (h // 2):HD * (h // 2) + HD]
                ten(lambda e, h=h, ob=ob, vsl=vsl: e.matmul(
                    ob, KHTM[:, HD * h:HD * h + HD], V_B[:, vsl],
                    start=True, stop=True, tile_position=(0, 64 * (h % 2))))
            for cc in range(NCC):
                vec(lambda e, cc=cc, g=g, Scur=Scur, Snxt=Snxt:
                    e.scalar_tensor_tensor(
                        Snxt[:, HD * cc:HD * cc + HD], Scur[:, HD * cc:HD * cc + HD],
                        PLv[:, cc, g:g + 1], bank(2)[:, HD * cc:HD * cc + HD],
                        ALU.mult, ALU.add))
            if _KWKV < 5:
                continue
            vec(lambda e: e.tensor_mul(YSQ, YS, YS))
            vec(lambda e: e.reduce_sum(GN1, YS.rearrange("p (h e) -> p h e", h=NH),
                                       axis=AX.X))
            vec(lambda e: e.reduce_sum(GN2, YSQ.rearrange("p (h e) -> p h e", h=NH),
                                       axis=AX.X))
            vec(lambda e: e.tensor_single_scalar(GMU, GN1, 1.0 / HD, ALU.mult))
            vec(lambda e: e.tensor_single_scalar(GN2, GN2, 1.0 / HD, ALU.mult))
            vec(lambda e: e.tensor_mul(GRS, GMU, GMU))
            vec(lambda e: e.tensor_sub(GN2, GN2, GRS))
            vec(lambda e: e.tensor_single_scalar(GN2, GN2, GN_EPS, ALU.add))
            act(lambda e: e.activation(GN2, GN2, AF.Sqrt, bias=0.0, scale=1.0))
            vec(lambda e: e.reciprocal(GRS, GN2))
            vec(lambda e: e.tensor_sub(YN.rearrange("p (h e) -> p h e", h=NH),
                                       YS.rearrange("p (h e) -> p h e", h=NH),
                                       GMU.unsqueeze(2).broadcast_to([128, NH, HD])))
            vec(lambda e: e.tensor_mul(YN.rearrange("p (h e) -> p h e", h=NH),
                                       YN.rearrange("p (h e) -> p h e", h=NH),
                                       GRS.unsqueeze(2).broadcast_to([128, NH, HD])))
            if _KWKV < 6:
                continue
            for cc in range(NCC):
                pb = cc % 2
                ten(lambda e, cc=cc, pb=pb: e.transpose(
                    bank(pb)[:, 0:128], YN[:, 128 * cc:128 * cc + 128], IDENT))
                act(lambda e, cc=cc, ch=ch, pb=pb: e.activation(
                    YNFB[:, BT * cc + 128 * ch:BT * cc + 128 * ch + 128],
                    bank(pb)[:, 0:128], AF.Copy, bias=0.0, scale=LNW[:, cc:cc + 1]))

        if _KPART < 8:
            continue
        for ch in range(BCH):
            for half in range(2):
                ten(lambda e, ch=ch, half=half: e.matmul(
                    bank(2 * ch + half), ONER, BIASR[:, 512 * half:512 * half + 512],
                    start=True, stop=False))
        for cc in range(NCC):
            wst = WST[cc % 2]
            dma(wst, wo_h[128 * cc:128 * cc + 128, :])
            for ch in range(BCH):
                for half in range(2):
                    ten(lambda e, cc=cc, ch=ch, half=half, wst=wst: e.matmul(
                        bank(2 * ch + half),
                        YNFB[:, BT * cc + 128 * ch:BT * cc + 128 * ch + 128],
                        wst[:, 512 * half:512 * half + 512],
                        start=False, stop=(cc == NCC - 1)))
        for ch in range(BCH):
            gt = b * BCH + ch
            dma(XTI, x_h[128 * gt:128 * gt + 128, :])
            for half in range(2):
                vec(lambda e, ch=ch, half=half: e.tensor_add(
                    QA[:, 512 * half:512 * half + 512],
                    XTI[:, 512 * half:512 * half + 512], bank(2 * ch + half)))
            dma(xp_h[128 * gt:128 * gt + 128, :], QA, q="gpsimd")

    fence()
    attn_stack.close()

    ffn_stack = contextlib.ExitStack()
    sbf = mk_sb(ffn_stack)
    N1T = sbf("n1t", (128, NCC * T), BF16)
    T1SL = [sbf("t1sl0", (128, 512), BF16), sbf("t1sl1", (128, 512), BF16)]
    INVS1B = sbf("invs1b", (128, T))
    HB0 = sbf("hb0", (128, NHC * 258))
    HB1 = sbf("hb1", (128, NHC * 258))
    HB2 = sbf("hb2", (128, NHC * 258))
    GST = sbf("gstash", (128, NHC))
    ACC = sbf("accc", (128, 256))
    SIL = sbf("silc", (128, 256))
    HCT = sbf("hct", (128, 2 * HID))
    N2B = sbf("n2b", (128, HID), BF16)
    IVT = sbf("ivt", (16, 128))
    INVR = sbf("invr", (1, T))
    HBUF = [HB0, HB1, HB2]

    for dchunk in range(NCC):
        dma(N1T[:, T * dchunk:T * dchunk + T],
            n1_h[:, 128 * dchunk:128 * dchunk + 128], transpose=True)
    vec(lambda e: e.reciprocal(SC16, QS1))
    vec(lambda e: e.scalar_tensor_tensor(SC16, SC16, 0.5,
                                         Q1.broadcast_to([128, 16]), ALU.mult,
                                         ALU.mult))
    ten(lambda e: e.transpose(bank(0)[0:16, 0:128], SC16, IDENT))
    act(lambda e: e.activation(IVT, bank(0)[0:16, 0:128], AF.Copy, bias=0.0,
                               scale=1.0))
    dma(INVR.rearrange("p (a w) -> p a w", a=16), IVT)
    for qq in range(4):
        ten(lambda e, qq=qq: e.matmul(bank(qq % 2), ONER,
                                      INVR[:, 512 * qq:512 * qq + 512],
                                      start=True, stop=True))
        act(lambda e, qq=qq: e.activation(INVS1B[:, 512 * qq:512 * qq + 512],
                                          bank(qq % 2), AF.Copy, bias=0.0,
                                          scale=1.0))
    vec(lambda e: e.memset(GST, 0.0))

    def conv_eighth(j):
        hb, hn = HBUF[j % 3], HBUF[(j + 1) % 3]
        hbv = hb.rearrange("p (a w) -> p a w", a=NHC)
        vec(lambda e, hbv=hbv: e.tensor_copy(hbv[:, :, 0], GST))
        if j < NE - 1:
            vec(lambda e, hbv=hbv, hn=hn: e.tensor_copy(
                hbv[:, :, 257], hn.rearrange("p (a w) -> p a w", a=NHC)[:, :, 1]))
        else:
            vec(lambda e, hbv=hbv: e.memset(hbv[:, :, 257], 0.0))
        vec(lambda e, hbv=hbv: e.tensor_copy(GST, hbv[:, :, 256]))
        for hc in range(NHC):
            w0 = 258 * hc
            vec(lambda e, w0=w0, hc=hc, hb=hb: e.tensor_scalar(
                ACC, hb[:, w0:w0 + 256], TAP[0][:, hc:hc + 1], None, ALU.mult))
            vec(lambda e, w0=w0, hc=hc, hb=hb: e.scalar_tensor_tensor(
                ACC, hb[:, w0 + 1:w0 + 257], TAP[1][:, hc:hc + 1], ACC,
                ALU.mult, ALU.add))
            vec(lambda e, w0=w0, hc=hc, hb=hb: e.scalar_tensor_tensor(
                ACC, hb[:, w0 + 2:w0 + 258], TAP[2][:, hc:hc + 1], ACC,
                ALU.mult, ALU.add))
            act(lambda e, hc=hc: e.activation(SIL, ACC, AF.Silu,
                                              bias=CBB[:, hc:hc + 1], scale=1.0))
            for chh in range(2):
                pb = (hc * 2 + chh) % 2
                ten(lambda e, chh=chh, pb=pb: e.transpose(
                    bank(pb)[:, 0:128], SIL[:, 128 * chh:128 * chh + 128], IDENT))
                act(lambda e, hc=hc, chh=chh, pb=pb: e.activation(
                    HCT[:, HID * chh + 128 * hc:HID * chh + 128 * hc + 128],
                    bank(pb)[:, 0:128], AF.Copy, bias=0.0, scale=1.0))
        for chh in range(2):
            tc = 2 * j + chh
            hsl = slice(HID * chh, HID * chh + HID)
            vec(lambda e, hsl=hsl: e.reduce_max(RMX, HCT[:, hsl], axis=AX.X,
                                                apply_absolute_value=True))
            vec(lambda e: e.tensor_single_scalar(RMX, RMX, 1e-5, ALU.max))
            vec(lambda e: e.reciprocal(RMX, RMX))
            vec(lambda e, tc=tc: e.tensor_single_scalar(QS2[:, tc:tc + 1], RMX,
                                                        127.0, ALU.mult))
            vec(lambda e, hsl=hsl, tc=tc: e.tensor_scalar(
                HCT[:, hsl], HCT[:, hsl], QS2[:, tc:tc + 1], MAGIC, ALU.mult,
                ALU.add))
            vec(lambda e, hsl=hsl: e.tensor_single_scalar(N2B, HCT[:, hsl],
                                                          MAGIC, ALU.subtract))
            dma(n2_h[128 * tc:128 * tc + 128, :], N2B, q="gpsimd")

    for q in range(4 if _KPH >= 3 else 0):
        for hg in range(8):
            for dchunk in range(NCC):
                t1 = T1SL[dchunk % 2]
                dma(t1, t1q_h[512 * hg:512 * hg + 512,
                              128 * dchunk:128 * dchunk + 128], transpose=True)
                for h4 in range(4):
                    ten(lambda e, dchunk=dchunk, h4=h4, q=q, t1=t1: e.matmul(
                        bank(h4), t1[:, 128 * h4:128 * h4 + 128],
                        N1T[:, T * dchunk + 512 * q:T * dchunk + 512 * q + 512],
                        start=(dchunk == 0), stop=(dchunk == NCC - 1)))
            for h4 in range(4):
                hc = 4 * hg + h4
                w0 = 258 * hc
                for e2 in range(2):
                    e8 = 2 * q + e2
                    vec(lambda e, h4=h4, w0=w0, e2=e2, e8=e8: e.tensor_mul(
                        HBUF[e8 % 3][:, w0 + 1:w0 + 257],
                        bank(h4)[:, 256 * e2:256 * e2 + 256],
                        INVS1B[:, ET * e8:ET * e8 + ET]))
        if q >= 1:
            conv_eighth(2 * q - 1)
        conv_eighth(2 * q)
    if _KPH >= 3:
        conv_eighth(NE - 1)

    vec(lambda e: e.reciprocal(SC16, QS2))
    vec(lambda e: e.scalar_tensor_tensor(INVS2, SC16, 0.5,
                                         Q2.broadcast_to([128, 16]), ALU.mult,
                                         ALU.mult))

    fence()
    ffn_stack.close()
    ffn2_stack = contextlib.ExitStack()
    sbg = mk_sb(ffn2_stack)
    N2TS = [sbg("n2ts0", (128, 512), BF16), sbg("n2ts1", (128, 512), BF16)]
    T2R = sbg("t2r", (128, NHC * 1024), BF16)
    XRE = [sbg("xre0", (128, C)), sbg("xre1", (128, C))]
    OST = [sbg("ost0", (128, C)), sbg("ost1", (128, C))]

    for hc in range(NHC):
        dma(T2R[:, 1024 * hc:1024 * hc + 1024],
            t2q_h[:, 128 * hc:128 * hc + 128], transpose=True)
    for tg in range(4 if _KPH >= 4 else 0):
        for hc in range(NHC):
            n2t = N2TS[hc % 2]
            dma(n2t, n2_h[512 * tg:512 * tg + 512, 128 * hc:128 * hc + 128],
                transpose=True)
            for t4 in range(4):
                for half in range(2):
                    ten(lambda e, hc=hc, t4=t4, half=half, n2t=n2t: e.matmul(
                        bank(2 * t4 + half), n2t[:, 128 * t4:128 * t4 + 128],
                        T2R[:, 1024 * hc + 512 * half:1024 * hc + 512 * half + 512],
                        start=(hc == 0), stop=(hc == NHC - 1)))
        for t4 in range(4):
            tc = 4 * tg + t4
            xre, ost = XRE[t4 % 2], OST[t4 % 2]
            dma(xre, xp_h[128 * tc:128 * tc + 128, :])
            for half in range(2):
                vec(lambda e, t4=t4, half=half, tc=tc, xre=xre, ost=ost:
                    e.scalar_tensor_tensor(
                        ost[:, 512 * half:512 * half + 512], bank(2 * t4 + half),
                        INVS2[:, tc:tc + 1], xre[:, 512 * half:512 * half + 512],
                        ALU.mult, ALU.add))
            dma(out_h[128 * tc:128 * tc + 128, :], ost, q="gpsimd")

    ffn2_stack.close()
    perm.close()
    return prog


def _extract_rw(nc, prog):
    with nc.Block() as blk:
        for eng in ENGS:
            ops = [p for p in prog if p[0] == eng]
            if not ops:
                continue
            getattr(blk, eng)(lambda e, ops=ops: [(p[1](e), e.nop(nofuse=True))
                                                  for p in ops])

    def keys_of(args):
        out = set()
        for a in args:
            memref = getattr(a, "memref", None)
            if memref is None:
                continue
            off = getattr(a, "offset", 0) or 0
            ap = getattr(a, "ap", None) or []
            span = 1
            for step, count in ap:
                span += abs(step) * (count - 1)
            if memref == "ps":
                b0, b1 = off // 512, (off + span - 1) // 512
                for bk in range(b0, b1 + 1):
                    out.add("ps:%d" % bk)
            else:
                out.add(memref)
        return out

    groups = {eng: [] for eng in ENGS}
    cur = {eng: ([], []) for eng in ENGS}
    for inst in nc.all_instructions():
        eng = {"PE": "tensor", "Activation": "scalar", "DVE": "vector",
               "Pool": "gpsimd", "SP": "sync"}.get(str(inst.engine).split(".")[-1])
        if eng is None:
            continue
        if type(inst).__name__ == "InstNoOp":
            r, w = cur[eng]
            groups[eng].append((frozenset(r), frozenset(w)))
            cur[eng] = ([], [])
        else:
            r, w = cur[eng]
            r.extend(keys_of(inst.ins))
            w.extend(keys_of(inst.outs))
    idx = {eng: 0 for eng in ENGS}
    rw = []
    for p in prog:
        if p[0] == "FENCE":
            rw.append(None)
            continue
        eng = p[0]
        rw.append(groups[eng][idx[eng]])
        idx[eng] += 1
    return rw


def build_nc():
    nc1 = bass.Bass("TRN2", target_bir_lowering=False)
    prog1 = build_prog(nc1)
    rw = _extract_rw(nc1, prog1)

    nc = bass.Bass("TRN2", target_bir_lowering=False)
    prog = build_prog(nc)
    assert len(prog) == len(prog1)

    KP = 10
    tokens = []
    cnt = {e: 0 for e in ENGS}
    dmai = {e: 0 for e in ENGS}
    for p in prog:
        if p[0] == "FENCE":
            tokens.append(None)
            continue
        eng, fn, inc = p
        if inc == 16:
            j = dmai[eng]
            dmai[eng] += 1
            key = ("dma", eng, j % KP)
            val = 16 * (j // KP + 1)
            selfneed = ((key, val - 16) if j >= KP else None)
        else:
            cnt[eng] += 1
            key = ("eng", eng)
            val = cnt[eng]
            selfneed = None
        tokens.append((key, val, selfneed))

    writer = {}
    readers = {}
    fence_need = {}
    sched = []
    ti = 0
    for p, k in zip(prog, rw):
        if p[0] == "FENCE":
            sched.append(("FENCE", None, 0, None, dict(fence_need)))
            ti += 1
            continue
        eng, fn, inc = p
        tok = tokens[ti]
        ti += 1
        key, val, selfneed = tok
        rds, wrs = k
        need = {}

        def add(tk):
            if tk is None:
                return
            tkey, tval = tk
            if tkey == ("eng", eng) and eng == "tensor":
                return
            if need.get(tkey, 0) < tval:
                need[tkey] = tval
        add(selfneed)
        for rk in rds:
            add(writer.get(rk))
        for rk in wrs:
            add(writer.get(rk))
            for tkey, tval in readers.get(rk, {}).items():
                add((tkey, tval))
        sched.append((eng, fn, inc, (key, val), need))
        for rk in rds:
            readers.setdefault(rk, {})[key] = val
        for rk in wrs:
            writer[rk] = (key, val)
            readers[rk] = {key: val}
        fence_need[key] = val

    finals = dict(fence_need)

    semkeys = sorted(set(fence_need.keys()), key=str)
    with contextlib.ExitStack() as es:
        sems = {sk: es.enter_context(nc.semaphore("s_" + "_".join(map(str, sk))))
                for sk in semkeys}
        with nc.Block() as block:
            def emit(name, handle):
                water = {sk: 0 for sk in semkeys}
                for eng, fn, inc, tok, need in sched:
                    if eng == "FENCE":
                        for sk, v in need.items():
                            if sk == ("eng", name):
                                continue
                            if water[sk] < v:
                                handle.wait_ge(sems[sk], v)
                                water[sk] = v
                        continue
                    if eng != name:
                        continue
                    for sk, v in sorted(need.items(), key=str):
                        if water[sk] < v:
                            handle.wait_ge(sems[sk], v)
                            if sk != ("eng", name):
                                water[sk] = v
                    fn(handle).then_inc(sems[tok[0]], inc)
                if name == "gpsimd":
                    for sk, v in sorted(finals.items(), key=str):
                        if sk != ("eng", name) and water[sk] < v:
                            handle.wait_ge(sems[sk], v)

            block.sync(lambda e: emit("sync", e))
            block.vector(lambda e: emit("vector", e))
            block.scalar(lambda e: emit("scalar", e))
            block.tensor(lambda e: emit("tensor", e))
            block.gpsimd(lambda e: emit("gpsimd", e))

    return nc


_NC_CACHE = []


def kernel(L0_attn_query, L0_attn_key, L0_attn_value, L0_attn_time_maa_x,
           L0_attn_time_maa_w, L0_attn_time_maa_k, L0_attn_time_maa_v,
           L0_attn_time_maa_r, L0_attn_time_maa_w1, L0_attn_time_maa_w2,
           L0_attn_time_decay, L0_attn_time_decay_w1, L0_attn_time_decay_w2,
           L0_attn_time_faaaa, L0_attn_W_r, L0_attn_W_k, L0_attn_W_v, L0_attn_W_o,
           L0_attn_ln_w, L0_attn_ln_b, L0_ffn_w1, L0_ffn_conv_w, L0_ffn_conv_b,
           L0_ffn_w2):
    x = np.ascontiguousarray(np.asarray(L0_attn_query, dtype=np.float32))
    w = {
        "wr": L0_attn_W_r, "wk": L0_attn_W_k, "wv": L0_attn_W_v, "wo": L0_attn_W_o,
        "maa_x": L0_attn_time_maa_x, "maa_w": L0_attn_time_maa_w,
        "maa_k": L0_attn_time_maa_k, "maa_v": L0_attn_time_maa_v,
        "maa_r": L0_attn_time_maa_r, "mw1": L0_attn_time_maa_w1,
        "mw2": L0_attn_time_maa_w2, "dec": L0_attn_time_decay,
        "dw1": L0_attn_time_decay_w1, "dw2": L0_attn_time_decay_w2,
        "faaaa": L0_attn_time_faaaa, "lnw": L0_attn_ln_w, "lnb": L0_attn_ln_b,
        "w1": L0_ffn_w1, "cw": L0_ffn_conv_w, "cb": L0_ffn_conv_b, "w2": L0_ffn_w2,
    }
    w = {k: np.ascontiguousarray(np.asarray(v, dtype=np.float32))
         for k, v in w.items()}
    if not _NC_CACHE:
        _NC_CACHE.append(build_nc())
    nc = _NC_CACHE[0]
    in_maps = [dict(w, x=np.ascontiguousarray(x[i])) for i in range(8)]
    res = run_bass_kernel_spmd(nc, in_maps, core_ids=list(range(8)))
    return np.stack([np.asarray(r["out"], dtype=np.float32)
                     for r in res.results], axis=0)


# version 06
# speedup vs baseline: 2.1772x; 1.0110x over previous
import contextlib
import os

import numpy as np
import concourse.bass as bass
import concourse.mybir as mybir
from concourse.bass_utils import run_bass_kernel_spmd

F32 = mybir.dt.float32
BF16 = mybir.dt.bfloat16
AF = mybir.ActivationFunctionType
ALU = mybir.AluOpType
AX = mybir.AxisListType

MAGIC = 12582912.0
T, C, HID, NH, HD = 2048, 1024, 4096, 16, 64
NCC = 8
NHC = 32
NB, BT, BCH = 4, 512, 4
NE, ET = 8, 256
LN_EPS, GN_EPS = 1e-5, 0.00064
MEAN_W1 = 1.0 / (HID * C)
MEAN_W2 = 1.0 / (C * HID)
_KPH = int(os.environ.get("KPHASE", "99"))
_KBLK = int(os.environ.get("KBLK", "99"))
_KPART = int(os.environ.get("KPART", "99"))
_KWKV = int(os.environ.get("KWKV", "99"))


ENGS = ("sync", "vector", "scalar", "tensor", "gpsimd")


def build_prog(nc):

    def din(name, shape, dt=F32):
        return nc.dram_tensor(name, list(shape), dt, kind="ExternalInput").ap()

    x_h = din("x", (T, C))
    wr_h, wk_h, wv_h, wo_h = (din(n, (C, C)) for n in ("wr", "wk", "wv", "wo"))
    maa_h = {f: din("maa_" + f, (C,)) for f in ("x", "w", "k", "v", "r")}
    mw1_h = din("mw1", (C, 128))
    mw2_h = din("mw2", (4, 32, C))
    dec_h = din("dec", (C,))
    dw1_h = din("dw1", (C, 64))
    dw2_h = din("dw2", (64, C))
    faaaa_h = din("faaaa", (NH, HD))
    lnw_h, lnb_h = din("lnw", (C,)), din("lnb", (C,))
    w1_h = din("w1", (HID, C))
    cw_h = din("cw", (HID, 1, 3))
    cb_h = din("cb", (HID,))
    w2_h = din("w2", (C, HID))

    out_h = nc.dram_tensor("out", [T, C], F32, kind="ExternalOutput").ap()
    n1_h = nc.dram_tensor("n1s", [T, C], BF16, kind="Internal").ap()
    t1q_h = nc.dram_tensor("t1qs", [HID, C], BF16, kind="Internal").ap()
    n2_h = nc.dram_tensor("n2s", [T, HID], BF16, kind="Internal").ap()
    t2q_h = nc.dram_tensor("t2qs", [C, HID], BF16, kind="Internal").ap()
    xp_h = nc.dram_tensor("xps", [T, C], F32, kind="Internal").ap()

    def mk_sb(stack):
        def sb(name, shape, dt=F32):
            return stack.enter_context(nc.sbuf_tensor(name, list(shape), dt)).ap()
        return sb

    perm = contextlib.ExitStack()
    sb = mk_sb(perm)

    IDENT = sb("ident", (128, 128))
    MSK2 = sb("msk2", (128, 256))
    ONEC = sb("onec", (128, 1))
    ONER = sb("oner", (1, 128))
    IND2 = sb("ind2", (128, 2))
    NH5 = sb("nh5", (128, 1))
    PH5 = sb("ph5", (128, 1))
    ZER = sb("zer", (128, 128))
    TAP = [sb("tap%d" % i, (128, NHC)) for i in range(3)]
    CBB = sb("cbpp", (128, NHC))
    S1c = sb("s1c", (128, 1))
    S2c = sb("s2c", (128, 1))
    MUc = sb("muc", (128, 1))
    RSc = sb("rsc", (128, 1))
    NBc = sb("nbc", (128, 1))
    QS1 = sb("qs1", (128, 16))
    QS2 = sb("qs2", (128, 16))
    INVS2 = sb("invs2", (128, 16))
    SC16 = sb("sc16", (128, 16))
    RMX = sb("rmx", (128, 1))
    PL = sb("plx", (128, 128))
    Q1, Q2 = sb("q1", (128, 1)), sb("q2", (128, 1))
    SONE = sb("sone", (1, 1))
    PLv = PL.rearrange("p (a g) -> p a g", a=NCC)

    prog = []

    def fence():
        prog.append(("FENCE", None, 0))

    def op(eng, inc=1):
        def deco(fn):
            prog.append((eng, fn, inc))
        return deco

    def dma(out, in_, transpose=False, q="sync"):
        @op(q, 16)
        def _(e, out=out, in_=in_, tr=transpose):
            if tr:
                return e.dma_start(out=out, in_=in_, transpose=True)
            return e.dma_start(out=out, in_=in_)

    def vec(fn):
        @op("vector")
        def _(e, fn=fn):
            return fn(e)

    def act(fn):
        @op("scalar")
        def _(e, fn=fn):
            return fn(e)

    def ten(fn):
        @op("tensor")
        def _(e, fn=fn):
            return fn(e)

    def gps(fn):
        @op("gpsimd")
        def _(e, fn=fn):
            return fn(e)

    vec(lambda e: e.memset(ZER, 0.0))
    vec(lambda e: e.memset(IDENT, 1.0))
    gps(lambda e: e.affine_select(IDENT, IDENT, [[-1, 128]], ALU.is_equal,
                                  0.0, base=0, channel_multiplier=1))
    gps(lambda e: e.iota(MSK2.rearrange("p (a b) -> p a b", a=2), [[0, 2], [1, 128]],
                         base=0, channel_multiplier=-1,
                         allow_small_or_imprecise_dtypes=True))
    vec(lambda e: e.tensor_single_scalar(MSK2, MSK2, 0.0, ALU.is_gt))
    vec(lambda e: e.memset(ONEC, 1.0))
    vec(lambda e: e.memset(ONER, 1.0))
    vec(lambda e: e.memset(NH5, -0.5))
    vec(lambda e: e.memset(PH5, 0.5))
    vec(lambda e: e.memset(IND2, 0.0))
    vec(lambda e: e.memset(IND2[0:64, 0:1], 1.0))
    vec(lambda e: e.memset(IND2[64:128, 1:2], 1.0))

    PSH = nc.alloc_psum_tensor("ps", [128, 4096], F32)
    PS = PSH.ap()

    def bank(k, w=512):
        return PS[:, 512 * k:512 * k + w]

    cw_stack = contextlib.ExitStack()
    sbc = mk_sb(cw_stack)
    CWS = sbc("cws", (32, 384))
    CBS = sbc("cbs", (32, 128))
    dma(CWS, cw_h.rearrange("(a b) o t -> a (b o t)", a=32))
    dma(CBS, cb_h.rearrange("(a b) -> a b", a=32))
    for tau in range(3):
        ten(lambda e, tau=tau: e.transpose(
            bank(0, 32)[:, 0:32],
            CWS.rearrange("a (b t) -> a b t", t=3)[:, :, tau], IDENT[0:32, 0:32]))
        act(lambda e, tau=tau: e.activation(TAP[tau], bank(0, 32)[:, 0:32], AF.Copy,
                                            bias=0.0, scale=1.0))
    ten(lambda e: e.transpose(bank(0, 32)[:, 0:32], CBS, IDENT[0:32, 0:32]))
    act(lambda e: e.activation(CBB, bank(0, 32)[:, 0:32], AF.Copy, bias=0.0,
                               scale=1.0))
    fence()
    cw_stack.close()

    wq_stack = contextlib.ExitStack()
    sbw = mk_sb(wq_stack)
    WQT = sbw("wqt", (128, C))
    WQA = sbw("wqa", (128, C))
    WQB = sbw("wqb", (128, C))
    WQO = sbw("wqo", (128, C), BF16)
    WSUM = sbw("wsum", (128, 32))
    SW = sbw("sw", (128, 1))

    def wq_p1_job(w_hbm, rt, ct, si):
        def go():
            dma(WQT, w_hbm[128 * rt:128 * rt + 128, 1024 * ct:1024 * ct + 1024])
            act(lambda e, si=si: e.activation(WQA, WQT, AF.Abs, bias=0.0,
                                              scale=1.0,
                                              accum_out=WSUM[:, si:si + 1]))
        return go

    def wq_mid_job(nsub, meanmul, qtile):
        def go():
            vec(lambda e, nsub=nsub: e.reduce_sum(RMX, WSUM[:, 0:nsub], axis=AX.X))
            ten(lambda e: e.matmul(bank(0)[0:1, 0:1], RMX, ONEC, start=True,
                                   stop=True))
            act(lambda e: e.activation(SONE, bank(0)[0:1, 0:1], AF.Copy, bias=0.0,
                                       scale=1.0))
            vec(lambda e, mm=meanmul: e.tensor_scalar(SONE, SONE, mm, 1e-5,
                                                      ALU.mult, ALU.max))
            ten(lambda e: e.matmul(bank(0)[:, 0:1], ONER, SONE, start=True,
                                   stop=True))
            act(lambda e, qtile=qtile: e.activation(qtile, bank(0)[:, 0:1],
                                                    AF.Copy, bias=0.0, scale=1.0))
            vec(lambda e, qtile=qtile: e.reciprocal(SW, qtile))
        return go

    def wq_p2_job(w_hbm, tq_hbm, rt, ct):
        def go():
            dma(WQT, w_hbm[128 * rt:128 * rt + 128, 1024 * ct:1024 * ct + 1024])
            act(lambda e: e.activation(WQA, WQT, AF.Sign, bias=NH5, scale=SW))
            act(lambda e: e.activation(WQB, WQT, AF.Sign, bias=PH5, scale=SW))
            vec(lambda e: e.tensor_add(WQO, WQA, WQB))
            dma(tq_hbm[128 * rt:128 * rt + 128, 1024 * ct:1024 * ct + 1024],
                WQO, q="gpsimd")
        return go

    wq_jobs = []
    for rt in range(32):
        wq_jobs.append(wq_p1_job(w1_h, rt, 0, rt))
    wq_jobs.append(wq_mid_job(32, MEAN_W1, Q1))
    for rt in range(32):
        wq_jobs.append(wq_p2_job(w1_h, t1q_h, rt, 0))
    for rt in range(8):
        for ct in range(4):
            wq_jobs.append(wq_p1_job(w2_h, rt, ct, 4 * rt + ct))
    wq_jobs.append(wq_mid_job(32, MEAN_W2, Q2))
    for rt in range(8):
        for ct in range(4):
            wq_jobs.append(wq_p2_job(w2_h, t2q_h, rt, ct))

    attn_stack = contextlib.ExitStack()
    sba = mk_sb(attn_stack)
    MAA = {f: sba("maa" + f, (128, NCC)) for f in ("x", "w", "k", "v", "r")}
    DEC = sba("decpp", (128, NCC))
    UPP = sba("upp", (128, NCC))
    LNW = sba("lnwpp", (128, NCC))
    LNB = sba("lnbpp", (128, NCC))
    MW1 = sba("mw1t", (128, NCC * 128))
    MW2 = sba("mw2t", (128, NCC * 128))
    DW1 = sba("dw1t", (128, NCC * 64))
    DW2 = sba("dw2t", (128, NCC * 128))
    BIASR = sba("biasr", (1, C))
    XTI = sba("xti", (128, C))
    QA = sba("qa", (128, C))
    N1B = sba("n1b", (128, C), BF16)
    XN_B = sba("xn_b", (128, NCC * BT))
    XX_B = sba("xx_b", (128, NCC * BT))
    XF_B = sba("xf_b", (128, NCC * BT))
    XPC = sba("xpc", (128, NCC))
    XD = sba("xdb", (128, BT))
    CB = sba("cbuf", (128, NCC * BT))
    SCB = sba("scb", (128, NCC * BT))
    KT_B = sba("kt_b", (128, NCC * BT))
    RT_B = sba("rt_b", (128, NCC * BT))
    V_B = sba("v_b", (128, BCH * C))
    KHTM = sba("khtm", (128, NCC * 128))
    DTMB = sba("dtmb", (128, BCH * 16))
    GM = sba("gm", (128, NH * 128))
    SBF = [sba("s0b", (128, NCC * HD)), sba("s1b", (128, NCC * HD))]
    X1 = V_B[:, 0:BT]
    KHF = QA
    YS = sba("ys", (128, C))
    GN1, GN2 = sba("gn1", (128, 16)), sba("gn2", (128, 16))
    GMU, GRS = sba("gmu", (128, 16)), sba("grs", (128, 16))
    YNFB = XX_B
    WST = [sba("wstr0", (128, C)), sba("wstr1", (128, C))]

    VST8 = sba("vst8", (64, 128))
    vec_targets = [(maa_h["x"], MAA["x"]), (maa_h["w"], MAA["w"]),
                   (maa_h["k"], MAA["k"]), (maa_h["v"], MAA["v"]),
                   (maa_h["r"], MAA["r"]), (dec_h, DEC), (lnw_h, LNW),
                   (lnb_h, LNB)]
    for src_h, tgt in vec_targets:
        dma(VST8[0:8, :], src_h.rearrange("(a b) -> a b", a=8))
        ten(lambda e: e.transpose(bank(2)[:, 0:8], VST8[0:8, :], IDENT[0:8, 0:8]))
        act(lambda e, tgt=tgt: e.activation(tgt, bank(2)[:, 0:8], AF.Copy,
                                            bias=0.0, scale=1.0))
    dma(VST8[0:16, 0:64], faaaa_h)
    ten(lambda e: e.transpose(bank(2)[0:64, 0:16], VST8[0:16, 0:64],
                              IDENT[0:16, 0:16]))
    act(lambda e: e.activation(VST8[0:64, 0:16], bank(2)[0:64, 0:16],
                               AF.Copy, bias=0.0, scale=1.0))
    for hp in range(2):
        vec(lambda e, hp=hp: e.tensor_copy(
            UPP[64 * hp:64 * hp + 64, :],
            VST8[0:64, 0:16].rearrange("p (c two) -> p c two", two=2)[:, :, hp]))
    dma(MW1.rearrange("p (a l) -> p a l", a=NCC),
        mw1_h.rearrange("(a b) l -> b a l", b=128))
    for f in range(4):
        dma(MW2[32 * f:32 * f + 32, :], mw2_h[f])
    dma(DW1.rearrange("p (a l) -> p a l", a=NCC),
        dw1_h.rearrange("(a b) l -> b a l", b=128))
    dma(DW2[0:64, :], dw2_h)
    vec(lambda e: e.memset(XPC, 0.0))
    vec(lambda e: e.memset(SBF[0], 0.0))

    for cc in range(NCC):
        wst = WST[cc % 2]
        dma(wst, wo_h[128 * cc:128 * cc + 128, :])
        for half in range(2):
            ten(lambda e, cc=cc, half=half, wst=wst: e.matmul(
                bank(half)[0:1, :], LNB[:, cc:cc + 1],
                wst[:, 512 * half:512 * half + 512],
                start=(cc == 0), stop=(cc == NCC - 1)))
    for half in range(2):
        act(lambda e, half=half: e.activation(
            BIASR[:, 512 * half:512 * half + 512], bank(half)[0:1, :], AF.Copy,
            bias=0.0, scale=1.0))

    def lora2_mm(f, cc):
        ten(lambda e, f=f, cc=cc: e.matmul(
            bank(6)[:, 0:BT], MW2[32 * f:32 * f + 32, 128 * cc:128 * cc + 128],
            X1[32 * f:32 * f + 32, :], start=True, stop=True,
            tile_position=(32 * f, 0)))

    def build_xf(f, mf):
        for cc in range(NCC):
            lora2_mm(f, cc)
            sl = slice(BT * cc, BT * cc + BT)
            vec(lambda e, cc=cc, sl=sl, mf=mf: e.scalar_tensor_tensor(
                QA[:, 0:BT], bank(6)[:, 0:BT], MAA[mf][:, cc:cc + 1], XX_B[:, sl],
                ALU.add, ALU.mult))
            vec(lambda e, sl=sl: e.tensor_add(XF_B[:, sl], QA[:, 0:BT],
                                              XN_B[:, sl]))

    for b in range(min(NB if _KPH >= 2 else 0, _KBLK)):
        for tl in range(BCH):
            gt = b * BCH + tl
            dma(XTI, x_h[128 * gt:128 * gt + 128, :])
            vec(lambda e: e.reduce_sum(S1c, XTI, axis=AX.X))
            act(lambda e: e.activation(QA, XTI, AF.Square, bias=0.0, scale=1.0,
                                       accum_out=S2c))
            vec(lambda e: e.tensor_single_scalar(MUc, S1c, 1.0 / C, ALU.mult))
            vec(lambda e: e.tensor_single_scalar(S2c, S2c, 1.0 / C, ALU.mult))
            vec(lambda e: e.tensor_mul(RSc, MUc, MUc))
            vec(lambda e: e.tensor_sub(S2c, S2c, RSc))
            vec(lambda e: e.tensor_single_scalar(S2c, S2c, LN_EPS, ALU.add))
            act(lambda e: e.activation(S2c, S2c, AF.Sqrt, bias=0.0, scale=1.0))
            vec(lambda e: e.reciprocal(RSc, S2c))
            vec(lambda e: e.scalar_tensor_tensor(NBc, MUc, -1.0, RSc, ALU.mult,
                                                 ALU.mult))
            vec(lambda e: e.tensor_scalar(XTI, XTI, RSc, NBc, ALU.mult, ALU.add))
            vec(lambda e: e.reduce_max(RMX, XTI, axis=AX.X,
                                       apply_absolute_value=True))
            vec(lambda e: e.tensor_single_scalar(RMX, RMX, 1e-5, ALU.max))
            vec(lambda e: e.reciprocal(RMX, RMX))
            vec(lambda e, gt=gt: e.tensor_single_scalar(QS1[:, gt:gt + 1], RMX,
                                                        127.0, ALU.mult))
            vec(lambda e, gt=gt: e.tensor_scalar(QA, XTI, QS1[:, gt:gt + 1],
                                                 MAGIC, ALU.mult, ALU.add))
            vec(lambda e: e.tensor_single_scalar(N1B, QA, MAGIC, ALU.subtract))
            dma(n1_h[128 * gt:128 * gt + 128, :], N1B, q="gpsimd")
            for cc in range(NCC):
                pb = cc % 2
                ten(lambda e, cc=cc, pb=pb: e.transpose(
                    bank(pb)[:, 0:128], XTI[:, 128 * cc:128 * cc + 128], IDENT))
                act(lambda e, tl=tl, cc=cc, pb=pb: e.activation(
                    XN_B[:, BT * cc + 128 * tl:BT * cc + 128 * tl + 128],
                    bank(pb)[:, 0:128], AF.Copy, bias=0.0, scale=1.0))
        if _KPART < 2:
            continue
        for cc in range(NCC):
            s0 = BT * cc
            vec(lambda e, s0=s0: e.tensor_sub(XX_B[:, s0 + 1:s0 + BT],
                                              XN_B[:, s0:s0 + BT - 1],
                                              XN_B[:, s0 + 1:s0 + BT]))
            vec(lambda e, s0=s0, cc=cc: e.tensor_sub(XX_B[:, s0:s0 + 1],
                                                     XPC[:, cc:cc + 1],
                                                     XN_B[:, s0:s0 + 1]))
        vec(lambda e: e.tensor_copy(
            XPC, XN_B.rearrange("p (a w) -> p a w", a=NCC)[:, :, BT - 1]))
        for cc in range(NCC):
            sl = slice(BT * cc, BT * cc + BT)
            vec(lambda e, cc=cc, sl=sl: e.scalar_tensor_tensor(
                XF_B[:, sl], XX_B[:, sl], MAA["x"][:, cc:cc + 1], XN_B[:, sl],
                ALU.mult, ALU.add))
        for cc in range(NCC):
            ten(lambda e, cc=cc: e.matmul(bank(0)[:, 0:BT],
                                          MW1[:, 128 * cc:128 * cc + 128],
                                          XF_B[:, BT * cc:BT * cc + BT],
                                          start=(cc == 0), stop=(cc == NCC - 1)))
        act(lambda e: e.activation(X1, bank(0)[:, 0:BT], AF.Tanh, bias=0.0,
                                   scale=1.0))
        if _KPART < 3:
            continue
        build_xf(0, "w")
        for cc in range(NCC):
            ten(lambda e, cc=cc: e.matmul(bank(0)[0:64, 0:BT],
                                          DW1[:, 64 * cc:64 * cc + 64],
                                          XF_B[:, BT * cc:BT * cc + BT],
                                          start=(cc == 0), stop=(cc == NCC - 1)))
        act(lambda e: e.activation(XD[0:64, :], bank(0)[0:64, 0:BT], AF.Tanh,
                                   bias=0.0, scale=1.0))
        for cc in range(NCC):
            ten(lambda e, cc=cc: e.matmul(bank(1)[:, 0:BT],
                                          DW2[0:64, 128 * cc:128 * cc + 128],
                                          XD[0:64, :], start=True, stop=True))
            act(lambda e, cc=cc: e.activation(CB[:, BT * cc:BT * cc + BT],
                                              bank(1)[:, 0:BT], AF.Exp,
                                              bias=DEC[:, cc:cc + 1], scale=1.0))
        for cc in range(NCC):
            for ch in range(BCH):
                sl = slice(BT * cc + 128 * ch, BT * cc + 128 * ch + 128)
                vec(lambda e, sl=sl: e.tensor_tensor_scan(SCB[:, sl], CB[:, sl],
                                                          ZER, 0.0, ALU.add,
                                                          ALU.add))
        act(lambda e: e.activation(CB, SCB, AF.Exp, bias=0.0, scale=-1.0))
        vec(lambda e, b=b: e.tensor_copy(
            PLv[:, :, b * BCH:b * BCH + BCH],
            CB.rearrange("p (a c s) -> p a c s", a=NCC, c=BCH)[:, :, :, 127]))
        if _KPART < 4:
            continue
        build_xf(1, "k")
        for dchunk in range(NCC):
            wst = WST[dchunk % 2]
            dma(wst, wk_h[128 * dchunk:128 * dchunk + 128, :])
            for dout in range(NCC):
                ten(lambda e, dchunk=dchunk, dout=dout, wst=wst: e.matmul(
                    bank(dout), wst[:, 128 * dout:128 * dout + 128],
                    XF_B[:, BT * dchunk:BT * dchunk + BT],
                    start=(dchunk == 0), stop=(dchunk == NCC - 1)))
        for dout in range(NCC):
            sl = slice(BT * dout, BT * dout + BT)
            act(lambda e, sl=sl: e.activation(QA[:, 0:BT], SCB[:, sl], AF.Exp,
                                              bias=0.0, scale=1.0))
            vec(lambda e, dout=dout, sl=sl: e.tensor_mul(KT_B[:, sl],
                                                         bank(dout), QA[:, 0:BT]))
            vec(lambda e, dout=dout, sl=sl: e.tensor_scalar(
                SCB[:, sl], bank(dout), UPP[:, dout:dout + 1], None, ALU.mult))
        if _KPART < 5:
            continue
        build_xf(3, "r")
        for dchunk in range(NCC):
            wst = WST[dchunk % 2]
            dma(wst, wr_h[128 * dchunk:128 * dchunk + 128, :])
            for dout in range(NCC):
                ten(lambda e, dchunk=dchunk, dout=dout, wst=wst: e.matmul(
                    bank(dout), wst[:, 128 * dout:128 * dout + 128],
                    XF_B[:, BT * dchunk:BT * dchunk + BT],
                    start=(dchunk == 0), stop=(dchunk == NCC - 1)))
        for dout in range(NCC):
            s0 = BT * dout
            vec(lambda e, dout=dout, s0=s0: e.tensor_mul(
                RT_B[:, s0 + 1:s0 + BT], bank(dout)[:, 1:BT],
                CB[:, s0:s0 + BT - 1]))
            vec(lambda e, dout=dout, s0=s0: e.tensor_copy(
                RT_B[:, s0:s0 + BT].rearrange("p (c s) -> p c s", c=BCH)[:, :, 0],
                bank(dout).rearrange("p (c s) -> p c s", c=BCH)[:, :, 0]))
            vec(lambda e, dout=dout, s0=s0: e.tensor_mul(XD, bank(dout),
                                                         SCB[:, s0:s0 + BT]))
            for chd in range(BCH):
                dsl = slice(512 * dout, 512 * dout + 2)
                ten(lambda e, chd=chd, dsl=dsl: e.matmul(
                    PS[:, dsl], XD[:, 128 * chd:128 * chd + 128], IND2,
                    start=True, stop=True))
                act(lambda e, chd=chd, dout=dout, dsl=dsl: e.activation(
                    DTMB[:, 16 * chd + 2 * dout:16 * chd + 2 * dout + 2],
                    PS[:, dsl], AF.Copy, bias=0.0, scale=1.0))
        if _KPART < 6:
            continue
        build_xf(2, "v")
        for dchunk in range(NCC):
            wst = WST[dchunk % 2]
            dma(wst, wv_h[128 * dchunk:128 * dchunk + 128, :])
            for ch in range(BCH):
                for half in range(2):
                    ten(lambda e, dchunk=dchunk, ch=ch, half=half, wst=wst:
                        e.matmul(
                            bank(2 * ch + half),
                            XF_B[:, BT * dchunk + 128 * ch:BT * dchunk + 128 * ch + 128],
                            wst[:, 512 * half:512 * half + 512],
                            start=(dchunk == 0), stop=(dchunk == NCC - 1)))
        for ch in range(BCH):
            for half in range(2):
                act(lambda e, ch=ch, half=half: e.activation(
                    V_B[:, C * ch + 512 * half:C * ch + 512 * half + 512],
                    bank(2 * ch + half), AF.Copy, bias=0.0, scale=1.0))
        if _KPART < 7:
            continue
        for ch in range(BCH):
            g = b * BCH + ch
            Scur, Snxt = SBF[g % 2], SBF[(g + 1) % 2]
            for cc in range(NCC):
                vec(lambda e, cc=cc, ch=ch, g=g: e.tensor_scalar(
                    KHF[:, 128 * cc:128 * cc + 128],
                    KT_B[:, BT * cc + 128 * ch:BT * cc + 128 * ch + 128],
                    PLv[:, cc, g:g + 1], None, ALU.mult))
            for cc in range(NCC):
                pb = cc % 2
                ten(lambda e, cc=cc, pb=pb: e.transpose(
                    bank(pb)[:, 0:128], KHF[:, 128 * cc:128 * cc + 128], IDENT))
                act(lambda e, cc=cc, pb=pb: e.activation(
                    KHTM[:, 128 * cc:128 * cc + 128], bank(pb)[:, 0:128], AF.Copy,
                    bias=0.0, scale=1.0))
            if _KWKV < 2:
                continue
            for h in [x for x in range(NH) if x % 2 == 0] + \
                     [x for x in range(NH) if x % 2 == 1]:
                cc, base = h // 2, 64 * (h % 2)
                q, s4 = 4 + h // 4, 128 * (h % 4)
                sl = slice(BT * cc + 128 * ch, BT * cc + 128 * ch + 128)
                ten(lambda e, cc=cc, base=base, q=q, s4=s4, sl=sl: e.matmul(
                    bank(q)[:, s4:s4 + 128], KT_B[base:base + 64, sl],
                    RT_B[base:base + 64, sl], start=True, stop=True,
                    tile_position=(base, 0)))
            for q in range(4):
                for hp in range(2):
                    vec(lambda e, q=q, hp=hp: e.tensor_mul(
                        GM[:, 512 * q + 256 * hp:512 * q + 256 * hp + 256],
                        bank(4 + q)[:, 256 * hp:256 * hp + 256], MSK2))
            if _KWKV < 3:
                continue
            for h in range(NH):
                vsl = slice(C * ch + HD * h, C * ch + HD * h + HD)
                ob = bank(h // 8)[:, (64 * h) % 512:(64 * h) % 512 + 64]
                ten(lambda e, h=h, ob=ob, vsl=vsl: e.matmul(
                    ob, GM[:, 128 * h:128 * h + 128], V_B[:, vsl],
                    start=(h % 8 == 0), stop=False))
            for h in [x for x in range(NH) if x % 2 == 0] + \
                     [x for x in range(NH) if x % 2 == 1]:
                cc, base = h // 2, 64 * (h % 2)
                sl = slice(BT * cc + 128 * ch, BT * cc + 128 * ch + 128)
                ob = bank(h // 8)[:, (64 * h) % 512:(64 * h) % 512 + 64]
                ten(lambda e, cc=cc, base=base, sl=sl, ob=ob, h=h, Scur=Scur:
                    e.matmul(ob, RT_B[base:base + 64, sl],
                             Scur[base:base + 64, HD * (h // 2):HD * (h // 2) + HD],
                             start=False, stop=(h in (7, 15)),
                             tile_position=(base, 0)))
            vec(lambda e, ch=ch: e.tensor_mul(
                GM[:, 0:C].rearrange("p (h e) -> p h e", h=NH),
                V_B[:, C * ch:C * ch + C].rearrange("p (h e) -> p h e", h=NH),
                DTMB[:, 16 * ch:16 * ch + 16].unsqueeze(2)
                .broadcast_to([128, NH, HD])))
            vec(lambda e: e.tensor_add(YS[:, 0:512], GM[:, 0:512], bank(0)))
            vec(lambda e: e.tensor_add(YS[:, 512:1024], GM[:, 512:1024], bank(1)))
            if _KWKV < 4:
                continue
            for h in [x for x in range(NH) if x % 2 == 0] + \
                     [x for x in range(NH) if x % 2 == 1]:
                vsl = slice(C * ch + HD * h, C * ch + HD * h + HD)
                ob = bank(2)[64 * (h % 2):64 * (h % 2) + 64,
                             HD * (h // 2):HD * (h // 2) + HD]
                ten(lambda e, h=h, ob=ob, vsl=vsl: e.matmul(
                    ob, KHTM[:, HD * h:HD * h + HD], V_B[:, vsl],
                    start=True, stop=True, tile_position=(0, 64 * (h % 2))))
            for cc in range(NCC):
                vec(lambda e, cc=cc, g=g, Scur=Scur, Snxt=Snxt:
                    e.scalar_tensor_tensor(
                        Snxt[:, HD * cc:HD * cc + HD], Scur[:, HD * cc:HD * cc + HD],
                        PLv[:, cc, g:g + 1], bank(2)[:, HD * cc:HD * cc + HD],
                        ALU.mult, ALU.add))
            if _KWKV < 5:
                continue
            vec(lambda e: e.tensor_mul(GM[:, 0:C], YS, YS))
            vec(lambda e: e.reduce_sum(GN1, YS.rearrange("p (h e) -> p h e", h=NH),
                                       axis=AX.X))
            vec(lambda e: e.reduce_sum(GN2,
                                       GM[:, 0:C].rearrange("p (h e) -> p h e",
                                                            h=NH), axis=AX.X))
            vec(lambda e: e.tensor_single_scalar(GMU, GN1, 1.0 / HD, ALU.mult))
            vec(lambda e: e.tensor_single_scalar(GN2, GN2, 1.0 / HD, ALU.mult))
            vec(lambda e: e.tensor_mul(GRS, GMU, GMU))
            vec(lambda e: e.tensor_sub(GN2, GN2, GRS))
            vec(lambda e: e.tensor_single_scalar(GN2, GN2, GN_EPS, ALU.add))
            act(lambda e: e.activation(GN2, GN2, AF.Sqrt, bias=0.0, scale=1.0))
            vec(lambda e: e.reciprocal(GRS, GN2))
            vec(lambda e: e.tensor_sub(YS.rearrange("p (h e) -> p h e", h=NH),
                                       YS.rearrange("p (h e) -> p h e", h=NH),
                                       GMU.unsqueeze(2).broadcast_to([128, NH, HD])))
            vec(lambda e: e.tensor_mul(YS.rearrange("p (h e) -> p h e", h=NH),
                                       YS.rearrange("p (h e) -> p h e", h=NH),
                                       GRS.unsqueeze(2).broadcast_to([128, NH, HD])))
            if _KWKV < 6:
                continue
            for cc in range(NCC):
                pb = cc % 2
                ten(lambda e, cc=cc, pb=pb: e.transpose(
                    bank(pb)[:, 0:128], YS[:, 128 * cc:128 * cc + 128], IDENT))
                act(lambda e, cc=cc, ch=ch, pb=pb: e.activation(
                    YNFB[:, BT * cc + 128 * ch:BT * cc + 128 * ch + 128],
                    bank(pb)[:, 0:128], AF.Copy, bias=0.0, scale=LNW[:, cc:cc + 1]))
        if _KPART < 8:
            continue
        for ch in range(BCH):
            for half in range(2):
                ten(lambda e, ch=ch, half=half: e.matmul(
                    bank(2 * ch + half), ONER, BIASR[:, 512 * half:512 * half + 512],
                    start=True, stop=False))
        for cc in range(NCC):
            wst = WST[cc % 2]
            dma(wst, wo_h[128 * cc:128 * cc + 128, :])
            for ch in range(BCH):
                for half in range(2):
                    ten(lambda e, cc=cc, ch=ch, half=half, wst=wst: e.matmul(
                        bank(2 * ch + half),
                        YNFB[:, BT * cc + 128 * ch:BT * cc + 128 * ch + 128],
                        wst[:, 512 * half:512 * half + 512],
                        start=False, stop=(cc == NCC - 1)))
        for ch in range(BCH):
            gt = b * BCH + ch
            dma(XTI, x_h[128 * gt:128 * gt + 128, :])
            for half in range(2):
                vec(lambda e, ch=ch, half=half: e.tensor_add(
                    QA[:, 512 * half:512 * half + 512],
                    XTI[:, 512 * half:512 * half + 512], bank(2 * ch + half)))
            dma(xp_h[128 * gt:128 * gt + 128, :], QA, q="gpsimd")
        lo = b * len(wq_jobs) // NB
        hi = (b + 1) * len(wq_jobs) // NB
        for job in wq_jobs[lo:hi]:
            job()
        wq_jobs[lo:hi] = [None] * (hi - lo)

    for job in wq_jobs:
        if job is not None:
            job()

    fence()
    attn_stack.close()
    wq_stack.close()

    ffn_stack = contextlib.ExitStack()
    sbf = mk_sb(ffn_stack)
    N1T = sbf("n1t", (128, NCC * T), BF16)
    T1SL = [sbf("t1sl0", (128, 512), BF16), sbf("t1sl1", (128, 512), BF16)]
    INVS1B = sbf("invs1b", (128, T))
    HB0 = sbf("hb0", (128, NHC * 258))
    HB1 = sbf("hb1", (128, NHC * 258))
    HB2 = sbf("hb2", (128, NHC * 258))
    GST = sbf("gstash", (128, NHC))
    ACC = sbf("accc", (128, 256))
    SIL = sbf("silc", (128, 256))
    HCT = sbf("hct", (128, 2 * HID))
    N2B = sbf("n2b", (128, HID), BF16)
    IVT = sbf("ivt", (16, 128))
    INVR = sbf("invr", (1, T))
    HBUF = [HB0, HB1, HB2]

    for dchunk in range(NCC):
        dma(N1T[:, T * dchunk:T * dchunk + T],
            n1_h[:, 128 * dchunk:128 * dchunk + 128], transpose=True)
    vec(lambda e: e.reciprocal(SC16, QS1))
    vec(lambda e: e.scalar_tensor_tensor(SC16, SC16, 0.5,
                                         Q1.broadcast_to([128, 16]), ALU.mult,
                                         ALU.mult))
    ten(lambda e: e.transpose(bank(0)[0:16, 0:128], SC16, IDENT))
    act(lambda e: e.activation(IVT, bank(0)[0:16, 0:128], AF.Copy, bias=0.0,
                               scale=1.0))
    dma(INVR.rearrange("p (a w) -> p a w", a=16), IVT)
    for qq in range(4):
        ten(lambda e, qq=qq: e.matmul(bank(qq % 2), ONER,
                                      INVR[:, 512 * qq:512 * qq + 512],
                                      start=True, stop=True))
        act(lambda e, qq=qq: e.activation(INVS1B[:, 512 * qq:512 * qq + 512],
                                          bank(qq % 2), AF.Copy, bias=0.0,
                                          scale=1.0))
    vec(lambda e: e.memset(GST, 0.0))

    def conv_eighth(j):
        hb, hn = HBUF[j % 3], HBUF[(j + 1) % 3]
        hbv = hb.rearrange("p (a w) -> p a w", a=NHC)
        vec(lambda e, hbv=hbv: e.tensor_copy(hbv[:, :, 0], GST))
        if j < NE - 1:
            vec(lambda e, hbv=hbv, hn=hn: e.tensor_copy(
                hbv[:, :, 257], hn.rearrange("p (a w) -> p a w", a=NHC)[:, :, 1]))
        else:
            vec(lambda e, hbv=hbv: e.memset(hbv[:, :, 257], 0.0))
        vec(lambda e, hbv=hbv: e.tensor_copy(GST, hbv[:, :, 256]))
        for hc in range(NHC):
            w0 = 258 * hc
            vec(lambda e, w0=w0, hc=hc, hb=hb: e.tensor_scalar(
                ACC, hb[:, w0:w0 + 256], TAP[0][:, hc:hc + 1], None, ALU.mult))
            vec(lambda e, w0=w0, hc=hc, hb=hb: e.scalar_tensor_tensor(
                ACC, hb[:, w0 + 1:w0 + 257], TAP[1][:, hc:hc + 1], ACC,
                ALU.mult, ALU.add))
            vec(lambda e, w0=w0, hc=hc, hb=hb: e.scalar_tensor_tensor(
                ACC, hb[:, w0 + 2:w0 + 258], TAP[2][:, hc:hc + 1], ACC,
                ALU.mult, ALU.add))
            act(lambda e, hc=hc: e.activation(SIL, ACC, AF.Silu,
                                              bias=CBB[:, hc:hc + 1], scale=1.0))
            for chh in range(2):
                pb = (hc * 2 + chh) % 2
                ten(lambda e, chh=chh, pb=pb: e.transpose(
                    bank(pb)[:, 0:128], SIL[:, 128 * chh:128 * chh + 128], IDENT))
                act(lambda e, hc=hc, chh=chh, pb=pb: e.activation(
                    HCT[:, HID * chh + 128 * hc:HID * chh + 128 * hc + 128],
                    bank(pb)[:, 0:128], AF.Copy, bias=0.0, scale=1.0))
        for chh in range(2):
            tc = 2 * j + chh
            hsl = slice(HID * chh, HID * chh + HID)
            vec(lambda e, hsl=hsl: e.reduce_max(RMX, HCT[:, hsl], axis=AX.X,
                                                apply_absolute_value=True))
            vec(lambda e: e.tensor_single_scalar(RMX, RMX, 1e-5, ALU.max))
            vec(lambda e: e.reciprocal(RMX, RMX))
            vec(lambda e, tc=tc: e.tensor_single_scalar(QS2[:, tc:tc + 1], RMX,
                                                        127.0, ALU.mult))
            vec(lambda e, hsl=hsl, tc=tc: e.tensor_scalar(
                HCT[:, hsl], HCT[:, hsl], QS2[:, tc:tc + 1], MAGIC, ALU.mult,
                ALU.add))
            vec(lambda e, hsl=hsl: e.tensor_single_scalar(N2B, HCT[:, hsl],
                                                          MAGIC, ALU.subtract))
            dma(n2_h[128 * tc:128 * tc + 128, :], N2B, q="gpsimd")

    for q in range(4 if _KPH >= 3 else 0):
        for hg in range(8):
            for dchunk in range(NCC):
                t1 = T1SL[dchunk % 2]
                dma(t1, t1q_h[512 * hg:512 * hg + 512,
                              128 * dchunk:128 * dchunk + 128], transpose=True)
                for h4 in range(4):
                    ten(lambda e, dchunk=dchunk, h4=h4, q=q, t1=t1: e.matmul(
                        bank(h4), t1[:, 128 * h4:128 * h4 + 128],
                        N1T[:, T * dchunk + 512 * q:T * dchunk + 512 * q + 512],
                        start=(dchunk == 0), stop=(dchunk == NCC - 1)))
            for h4 in range(4):
                hc = 4 * hg + h4
                w0 = 258 * hc
                for e2 in range(2):
                    e8 = 2 * q + e2
                    vec(lambda e, h4=h4, w0=w0, e2=e2, e8=e8: e.tensor_mul(
                        HBUF[e8 % 3][:, w0 + 1:w0 + 257],
                        bank(h4)[:, 256 * e2:256 * e2 + 256],
                        INVS1B[:, ET * e8:ET * e8 + ET]))
        if q >= 1:
            conv_eighth(2 * q - 1)
        conv_eighth(2 * q)
    if _KPH >= 3:
        conv_eighth(NE - 1)

    vec(lambda e: e.reciprocal(SC16, QS2))
    vec(lambda e: e.scalar_tensor_tensor(INVS2, SC16, 0.5,
                                         Q2.broadcast_to([128, 16]), ALU.mult,
                                         ALU.mult))

    fence()
    ffn_stack.close()
    ffn2_stack = contextlib.ExitStack()
    sbg = mk_sb(ffn2_stack)
    N2TS = [sbg("n2ts0", (128, 512), BF16), sbg("n2ts1", (128, 512), BF16)]
    T2R = sbg("t2r", (128, NHC * 1024), BF16)
    XRE = [sbg("xre0", (128, C)), sbg("xre1", (128, C))]
    OST = [sbg("ost0", (128, C)), sbg("ost1", (128, C))]

    for hc in range(NHC):
        dma(T2R[:, 1024 * hc:1024 * hc + 1024],
            t2q_h[:, 128 * hc:128 * hc + 128], transpose=True)
    for tg in range(4 if _KPH >= 4 else 0):
        for hc in range(NHC):
            n2t = N2TS[hc % 2]
            dma(n2t, n2_h[512 * tg:512 * tg + 512, 128 * hc:128 * hc + 128],
                transpose=True)
            for t4 in range(4):
                for half in range(2):
                    ten(lambda e, hc=hc, t4=t4, half=half, n2t=n2t: e.matmul(
                        bank(2 * t4 + half), n2t[:, 128 * t4:128 * t4 + 128],
                        T2R[:, 1024 * hc + 512 * half:1024 * hc + 512 * half + 512],
                        start=(hc == 0), stop=(hc == NHC - 1)))
        for t4 in range(4):
            tc = 4 * tg + t4
            xre, ost = XRE[t4 % 2], OST[t4 % 2]
            dma(xre, xp_h[128 * tc:128 * tc + 128, :])
            for half in range(2):
                vec(lambda e, t4=t4, half=half, tc=tc, xre=xre, ost=ost:
                    e.scalar_tensor_tensor(
                        ost[:, 512 * half:512 * half + 512], bank(2 * t4 + half),
                        INVS2[:, tc:tc + 1], xre[:, 512 * half:512 * half + 512],
                        ALU.mult, ALU.add))
            dma(out_h[128 * tc:128 * tc + 128, :], ost, q="gpsimd")

    ffn2_stack.close()
    perm.close()
    return prog


def _extract_rw(nc, prog):
    with nc.Block() as blk:
        for eng in ENGS:
            ops = [p for p in prog if p[0] == eng]
            if not ops:
                continue
            getattr(blk, eng)(lambda e, ops=ops: [(p[1](e), e.nop(nofuse=True))
                                                  for p in ops])

    def keys_of(args):
        out = set()
        for a in args:
            memref = getattr(a, "memref", None)
            if memref is None:
                continue
            off = getattr(a, "offset", 0) or 0
            ap = getattr(a, "ap", None) or []
            span = 1
            for step, count in ap:
                span += abs(step) * (count - 1)
            if memref == "ps":
                b0, b1 = off // 512, (off + span - 1) // 512
                for bk in range(b0, b1 + 1):
                    out.add("ps:%d" % bk)
            else:
                out.add(memref)
        return out

    groups = {eng: [] for eng in ENGS}
    cur = {eng: ([], []) for eng in ENGS}
    for inst in nc.all_instructions():
        eng = {"PE": "tensor", "Activation": "scalar", "DVE": "vector",
               "Pool": "gpsimd", "SP": "sync"}.get(str(inst.engine).split(".")[-1])
        if eng is None:
            continue
        if type(inst).__name__ == "InstNoOp":
            r, w = cur[eng]
            groups[eng].append((frozenset(r), frozenset(w)))
            cur[eng] = ([], [])
        else:
            r, w = cur[eng]
            r.extend(keys_of(inst.ins))
            w.extend(keys_of(inst.outs))
    idx = {eng: 0 for eng in ENGS}
    rw = []
    for p in prog:
        if p[0] == "FENCE":
            rw.append(None)
            continue
        eng = p[0]
        rw.append(groups[eng][idx[eng]])
        idx[eng] += 1
    return rw


def build_nc():
    nc1 = bass.Bass("TRN2", target_bir_lowering=False)
    prog1 = build_prog(nc1)
    rw = _extract_rw(nc1, prog1)

    nc = bass.Bass("TRN2", target_bir_lowering=False)
    prog = build_prog(nc)
    assert len(prog) == len(prog1)

    KP = 10
    tokens = []
    cnt = {e: 0 for e in ENGS}
    dmai = {e: 0 for e in ENGS}
    for p in prog:
        if p[0] == "FENCE":
            tokens.append(None)
            continue
        eng, fn, inc = p
        if inc == 16:
            j = dmai[eng]
            dmai[eng] += 1
            key = ("dma", eng, j % KP)
            val = 16 * (j // KP + 1)
            selfneed = ((key, val - 16) if j >= KP else None)
        else:
            cnt[eng] += 1
            key = ("eng", eng)
            val = cnt[eng]
            selfneed = None
        tokens.append((key, val, selfneed))

    writer = {}
    readers = {}
    fence_need = {}
    sched = []
    ti = 0
    for p, k in zip(prog, rw):
        if p[0] == "FENCE":
            sched.append(("FENCE", None, 0, None, dict(fence_need)))
            ti += 1
            continue
        eng, fn, inc = p
        tok = tokens[ti]
        ti += 1
        key, val, selfneed = tok
        rds, wrs = k
        need = {}

        def add(tk):
            if tk is None:
                return
            tkey, tval = tk
            if tkey == ("eng", eng) and eng == "tensor":
                return
            if need.get(tkey, 0) < tval:
                need[tkey] = tval
        add(selfneed)
        for rk in rds:
            add(writer.get(rk))
        for rk in wrs:
            add(writer.get(rk))
            for tkey, tval in readers.get(rk, {}).items():
                add((tkey, tval))
        sched.append((eng, fn, inc, (key, val), need))
        for rk in rds:
            readers.setdefault(rk, {})[key] = val
        for rk in wrs:
            writer[rk] = (key, val)
            readers[rk] = {key: val}
        fence_need[key] = val

    finals = dict(fence_need)

    semkeys = sorted(set(fence_need.keys()), key=str)
    with contextlib.ExitStack() as es:
        sems = {sk: es.enter_context(nc.semaphore("s_" + "_".join(map(str, sk))))
                for sk in semkeys}
        with nc.Block() as block:
            def emit(name, handle):
                water = {sk: 0 for sk in semkeys}
                for eng, fn, inc, tok, need in sched:
                    if eng == "FENCE":
                        for sk, v in need.items():
                            if sk == ("eng", name):
                                continue
                            if water[sk] < v:
                                handle.wait_ge(sems[sk], v)
                                water[sk] = v
                        continue
                    if eng != name:
                        continue
                    for sk, v in sorted(need.items(), key=str):
                        if water[sk] < v:
                            handle.wait_ge(sems[sk], v)
                            if sk != ("eng", name):
                                water[sk] = v
                    fn(handle).then_inc(sems[tok[0]], inc)
                if name == "gpsimd":
                    for sk, v in sorted(finals.items(), key=str):
                        if sk != ("eng", name) and water[sk] < v:
                            handle.wait_ge(sems[sk], v)

            block.sync(lambda e: emit("sync", e))
            block.vector(lambda e: emit("vector", e))
            block.scalar(lambda e: emit("scalar", e))
            block.tensor(lambda e: emit("tensor", e))
            block.gpsimd(lambda e: emit("gpsimd", e))

    return nc


_NC_CACHE = []


def kernel(L0_attn_query, L0_attn_key, L0_attn_value, L0_attn_time_maa_x,
           L0_attn_time_maa_w, L0_attn_time_maa_k, L0_attn_time_maa_v,
           L0_attn_time_maa_r, L0_attn_time_maa_w1, L0_attn_time_maa_w2,
           L0_attn_time_decay, L0_attn_time_decay_w1, L0_attn_time_decay_w2,
           L0_attn_time_faaaa, L0_attn_W_r, L0_attn_W_k, L0_attn_W_v, L0_attn_W_o,
           L0_attn_ln_w, L0_attn_ln_b, L0_ffn_w1, L0_ffn_conv_w, L0_ffn_conv_b,
           L0_ffn_w2):
    x = np.ascontiguousarray(np.asarray(L0_attn_query, dtype=np.float32))
    w = {
        "wr": L0_attn_W_r, "wk": L0_attn_W_k, "wv": L0_attn_W_v, "wo": L0_attn_W_o,
        "maa_x": L0_attn_time_maa_x, "maa_w": L0_attn_time_maa_w,
        "maa_k": L0_attn_time_maa_k, "maa_v": L0_attn_time_maa_v,
        "maa_r": L0_attn_time_maa_r, "mw1": L0_attn_time_maa_w1,
        "mw2": L0_attn_time_maa_w2, "dec": L0_attn_time_decay,
        "dw1": L0_attn_time_decay_w1, "dw2": L0_attn_time_decay_w2,
        "faaaa": L0_attn_time_faaaa, "lnw": L0_attn_ln_w, "lnb": L0_attn_ln_b,
        "w1": L0_ffn_w1, "cw": L0_ffn_conv_w, "cb": L0_ffn_conv_b, "w2": L0_ffn_w2,
    }
    w = {k: np.ascontiguousarray(np.asarray(v, dtype=np.float32))
         for k, v in w.items()}
    if not _NC_CACHE:
        _NC_CACHE.append(build_nc())
    nc = _NC_CACHE[0]
    in_maps = [dict(w, x=np.ascontiguousarray(x[i])) for i in range(8)]
    res = run_bass_kernel_spmd(nc, in_maps, core_ids=list(range(8)))
    return np.stack([np.asarray(r["out"], dtype=np.float32)
                     for r in res.results], axis=0)


# version 07
# speedup vs baseline: 2.1931x; 1.0073x over previous
import contextlib
import os

import numpy as np
import concourse.bass as bass
import concourse.mybir as mybir
from concourse.bass_utils import run_bass_kernel_spmd

F32 = mybir.dt.float32
BF16 = mybir.dt.bfloat16
AF = mybir.ActivationFunctionType
ALU = mybir.AluOpType
AX = mybir.AxisListType

MAGIC = 12582912.0
T, C, HID, NH, HD = 2048, 1024, 4096, 16, 64
NCC = 8
NHC = 32
NB, BT, BCH = 4, 512, 4
NE, ET = 8, 256
LN_EPS, GN_EPS = 1e-5, 0.00064
MEAN_W1 = 1.0 / (HID * C)
MEAN_W2 = 1.0 / (C * HID)
_KPH = int(os.environ.get("KPHASE", "99"))
_KBLK = int(os.environ.get("KBLK", "99"))
_KPART = int(os.environ.get("KPART", "99"))
_KWKV = int(os.environ.get("KWKV", "99"))


ENGS = ("sync", "vector", "scalar", "tensor", "gpsimd")


def build_prog(nc):

    def din(name, shape, dt=F32):
        return nc.dram_tensor(name, list(shape), dt, kind="ExternalInput").ap()

    x_h = din("x", (T, C))
    wr_h, wk_h, wv_h, wo_h = (din(n, (C, C)) for n in ("wr", "wk", "wv", "wo"))
    maa_h = {f: din("maa_" + f, (C,)) for f in ("x", "w", "k", "v", "r")}
    mw1_h = din("mw1", (C, 128))
    mw2_h = din("mw2", (4, 32, C))
    dec_h = din("dec", (C,))
    dw1_h = din("dw1", (C, 64))
    dw2_h = din("dw2", (64, C))
    faaaa_h = din("faaaa", (NH, HD))
    lnw_h, lnb_h = din("lnw", (C,)), din("lnb", (C,))
    w1_h = din("w1", (HID, C))
    cw_h = din("cw", (HID, 1, 3))
    cb_h = din("cb", (HID,))
    w2_h = din("w2", (C, HID))

    out_h = nc.dram_tensor("out", [T, C], F32, kind="ExternalOutput").ap()
    n1_h = nc.dram_tensor("n1s", [T, C], BF16, kind="Internal").ap()
    t1q_h = nc.dram_tensor("t1qs", [HID, C], BF16, kind="Internal").ap()
    n2_h = nc.dram_tensor("n2s", [T, HID], BF16, kind="Internal").ap()
    t2q_h = nc.dram_tensor("t2qs", [C, HID], BF16, kind="Internal").ap()
    xp_h = nc.dram_tensor("xps", [T, C], F32, kind="Internal").ap()

    def mk_sb(stack):
        def sb(name, shape, dt=F32):
            return stack.enter_context(nc.sbuf_tensor(name, list(shape), dt)).ap()
        return sb

    perm = contextlib.ExitStack()
    sb = mk_sb(perm)

    IDENT = sb("ident", (128, 128))
    MSK2 = sb("msk2", (128, 256))
    ONEC = sb("onec", (128, 1))
    ONER = sb("oner", (1, 128))
    IND2 = sb("ind2", (128, 2))
    NH5 = sb("nh5", (128, 1))
    PH5 = sb("ph5", (128, 1))
    ZER = sb("zer", (128, 128))
    TAP = [sb("tap%d" % i, (128, NHC)) for i in range(3)]
    CBB = sb("cbpp", (128, NHC))
    S1c = sb("s1c", (128, 1))
    S2c = sb("s2c", (128, 1))
    MUc = sb("muc", (128, 1))
    RSc = sb("rsc", (128, 1))
    NBc = sb("nbc", (128, 1))
    QS1 = sb("qs1", (128, 16))
    QS2 = sb("qs2", (128, 16))
    INVS2 = sb("invs2", (128, 16))
    SC16 = sb("sc16", (128, 16))
    RMX = sb("rmx", (128, 1))
    PL = sb("plx", (128, 128))
    Q1, Q2 = sb("q1", (128, 1)), sb("q2", (128, 1))
    SONE = sb("sone", (1, 1))
    PLv = PL.rearrange("p (a g) -> p a g", a=NCC)

    prog = []

    def fence():
        prog.append(("FENCE", None, 0))

    def op(eng, inc=1):
        def deco(fn):
            prog.append((eng, fn, inc))
        return deco

    def dma(out, in_, transpose=False, q="sync"):
        @op(q, 16)
        def _(e, out=out, in_=in_, tr=transpose):
            if tr:
                return e.dma_start(out=out, in_=in_, transpose=True)
            return e.dma_start(out=out, in_=in_)

    def vec(fn):
        @op("vector")
        def _(e, fn=fn):
            return fn(e)

    def act(fn):
        @op("scalar")
        def _(e, fn=fn):
            return fn(e)

    def ten(fn):
        @op("tensor")
        def _(e, fn=fn):
            return fn(e)

    def gps(fn):
        @op("gpsimd")
        def _(e, fn=fn):
            return fn(e)

    vec(lambda e: e.memset(ZER, 0.0))
    vec(lambda e: e.memset(IDENT, 1.0))
    gps(lambda e: e.affine_select(IDENT, IDENT, [[-1, 128]], ALU.is_equal,
                                  0.0, base=0, channel_multiplier=1))
    gps(lambda e: e.iota(MSK2.rearrange("p (a b) -> p a b", a=2), [[0, 2], [1, 128]],
                         base=0, channel_multiplier=-1,
                         allow_small_or_imprecise_dtypes=True))
    vec(lambda e: e.tensor_single_scalar(MSK2, MSK2, 0.0, ALU.is_gt))
    vec(lambda e: e.memset(ONEC, 1.0))
    vec(lambda e: e.memset(ONER, 1.0))
    vec(lambda e: e.memset(NH5, -0.5))
    vec(lambda e: e.memset(PH5, 0.5))
    vec(lambda e: e.memset(IND2, 0.0))
    vec(lambda e: e.memset(IND2[0:64, 0:1], 1.0))
    vec(lambda e: e.memset(IND2[64:128, 1:2], 1.0))

    PSH = nc.alloc_psum_tensor("ps", [128, 4096], F32)
    PS = PSH.ap()

    def bank(k, w=512):
        return PS[:, 512 * k:512 * k + w]

    cw_stack = contextlib.ExitStack()
    sbc = mk_sb(cw_stack)
    CWS = sbc("cws", (32, 384))
    CBS = sbc("cbs", (32, 128))
    dma(CWS, cw_h.rearrange("(a b) o t -> a (b o t)", a=32))
    dma(CBS, cb_h.rearrange("(a b) -> a b", a=32))
    for tau in range(3):
        ten(lambda e, tau=tau: e.transpose(
            bank(0, 32)[:, 0:32],
            CWS.rearrange("a (b t) -> a b t", t=3)[:, :, tau], IDENT[0:32, 0:32]))
        act(lambda e, tau=tau: e.activation(TAP[tau], bank(0, 32)[:, 0:32], AF.Copy,
                                            bias=0.0, scale=1.0))
    ten(lambda e: e.transpose(bank(0, 32)[:, 0:32], CBS, IDENT[0:32, 0:32]))
    act(lambda e: e.activation(CBB, bank(0, 32)[:, 0:32], AF.Copy, bias=0.0,
                               scale=1.0))
    fence()
    cw_stack.close()

    wq_stack = contextlib.ExitStack()
    sbw = mk_sb(wq_stack)
    WQT = sbw("wqt", (128, C))
    WQA = sbw("wqa", (128, C))
    WQB = sbw("wqb", (128, C))
    WQO = sbw("wqo", (128, C), BF16)
    WSUM = sbw("wsum", (128, 32))
    SW = sbw("sw", (128, 1))

    def wq_p1_job(w_hbm, rt, ct, si):
        def go():
            dma(WQT, w_hbm[128 * rt:128 * rt + 128, 1024 * ct:1024 * ct + 1024])
            act(lambda e, si=si: e.activation(WQA, WQT, AF.Abs, bias=0.0,
                                              scale=1.0,
                                              accum_out=WSUM[:, si:si + 1]))
        return go

    def wq_mid_job(nsub, meanmul, qtile):
        def go():
            vec(lambda e, nsub=nsub: e.reduce_sum(RMX, WSUM[:, 0:nsub], axis=AX.X))
            ten(lambda e: e.matmul(bank(0)[0:1, 0:1], RMX, ONEC, start=True,
                                   stop=True))
            act(lambda e: e.activation(SONE, bank(0)[0:1, 0:1], AF.Copy, bias=0.0,
                                       scale=1.0))
            vec(lambda e, mm=meanmul: e.tensor_scalar(SONE, SONE, mm, 1e-5,
                                                      ALU.mult, ALU.max))
            ten(lambda e: e.matmul(bank(0)[:, 0:1], ONER, SONE, start=True,
                                   stop=True))
            act(lambda e, qtile=qtile: e.activation(qtile, bank(0)[:, 0:1],
                                                    AF.Copy, bias=0.0, scale=1.0))
            vec(lambda e, qtile=qtile: e.reciprocal(SW, qtile))
        return go

    def wq_p2_job(w_hbm, tq_hbm, rt, ct):
        def go():
            dma(WQT, w_hbm[128 * rt:128 * rt + 128, 1024 * ct:1024 * ct + 1024])
            act(lambda e: e.activation(WQA, WQT, AF.Sign, bias=NH5, scale=SW))
            act(lambda e: e.activation(WQB, WQT, AF.Sign, bias=PH5, scale=SW))
            vec(lambda e: e.tensor_add(WQO, WQA, WQB))
            dma(tq_hbm[128 * rt:128 * rt + 128, 1024 * ct:1024 * ct + 1024],
                WQO, q="gpsimd")
        return go

    wq_jobs = []
    for rt in range(32):
        wq_jobs.append(wq_p1_job(w1_h, rt, 0, rt))
    wq_jobs.append(wq_mid_job(32, MEAN_W1, Q1))
    for rt in range(32):
        wq_jobs.append(wq_p2_job(w1_h, t1q_h, rt, 0))
    for rt in range(8):
        for ct in range(4):
            wq_jobs.append(wq_p1_job(w2_h, rt, ct, 4 * rt + ct))
    wq_jobs.append(wq_mid_job(32, MEAN_W2, Q2))
    for rt in range(8):
        for ct in range(4):
            wq_jobs.append(wq_p2_job(w2_h, t2q_h, rt, ct))

    attn_stack = contextlib.ExitStack()
    sba = mk_sb(attn_stack)
    MAA = {f: sba("maa" + f, (128, NCC)) for f in ("x", "w", "k", "v", "r")}
    DEC = sba("decpp", (128, NCC))
    UPP = sba("upp", (128, NCC))
    LNW = sba("lnwpp", (128, NCC))
    LNB = sba("lnbpp", (128, NCC))
    MW1 = sba("mw1t", (128, NCC * 128))
    MW2 = sba("mw2t", (128, NCC * 128))
    DW1 = sba("dw1t", (128, NCC * 64))
    DW2 = sba("dw2t", (128, NCC * 128))
    BIASR = sba("biasr", (1, C))
    XTI = sba("xti", (128, C))
    QA = sba("qa", (128, C))
    N1B = sba("n1b", (128, C), BF16)
    XN_B = sba("xn_b", (128, NCC * BT))
    XX_B = sba("xx_b", (128, NCC * BT))
    XF_B = sba("xf_b", (128, NCC * BT))
    XPC = sba("xpc", (128, NCC))
    XD = sba("xdb", (128, BT))
    CB = sba("cbuf", (128, NCC * BT))
    SCB = sba("scb", (128, NCC * BT))
    KT_B = sba("kt_b", (128, NCC * BT))
    RT_B = sba("rt_b", (128, NCC * BT))
    V_B = sba("v_b", (128, BCH * C))
    KHTM = sba("khtm", (128, NCC * 128))
    DTMB = sba("dtmb", (128, BCH * 16))
    GM = sba("gm", (128, NH * 128))
    SBF = [sba("s0b", (128, NCC * HD)), sba("s1b", (128, NCC * HD))]
    X1 = V_B[:, 0:BT]
    KHF = XF_B[:, 0:NCC * 128]
    YS = sba("ys", (128, C))
    GN1, GN2 = sba("gn1", (128, 16)), sba("gn2", (128, 16))
    GMU, GRS = sba("gmu", (128, 16)), sba("grs", (128, 16))
    YNFB = XX_B
    WST = [sba("wstr0", (128, C)), sba("wstr1", (128, C))]

    VST8 = sba("vst8", (64, 128))
    vec_targets = [(maa_h["x"], MAA["x"]), (maa_h["w"], MAA["w"]),
                   (maa_h["k"], MAA["k"]), (maa_h["v"], MAA["v"]),
                   (maa_h["r"], MAA["r"]), (dec_h, DEC), (lnw_h, LNW),
                   (lnb_h, LNB)]
    for src_h, tgt in vec_targets:
        dma(VST8[0:8, :], src_h.rearrange("(a b) -> a b", a=8))
        ten(lambda e: e.transpose(bank(2)[:, 0:8], VST8[0:8, :], IDENT[0:8, 0:8]))
        act(lambda e, tgt=tgt: e.activation(tgt, bank(2)[:, 0:8], AF.Copy,
                                            bias=0.0, scale=1.0))
    dma(VST8[0:16, 0:64], faaaa_h)
    ten(lambda e: e.transpose(bank(2)[0:64, 0:16], VST8[0:16, 0:64],
                              IDENT[0:16, 0:16]))
    act(lambda e: e.activation(VST8[0:64, 0:16], bank(2)[0:64, 0:16],
                               AF.Copy, bias=0.0, scale=1.0))
    for hp in range(2):
        vec(lambda e, hp=hp: e.tensor_copy(
            UPP[64 * hp:64 * hp + 64, :],
            VST8[0:64, 0:16].rearrange("p (c two) -> p c two", two=2)[:, :, hp]))
    dma(MW1.rearrange("p (a l) -> p a l", a=NCC),
        mw1_h.rearrange("(a b) l -> b a l", b=128))
    for f in range(4):
        dma(MW2[32 * f:32 * f + 32, :], mw2_h[f])
    dma(DW1.rearrange("p (a l) -> p a l", a=NCC),
        dw1_h.rearrange("(a b) l -> b a l", b=128))
    dma(DW2[0:64, :], dw2_h)
    vec(lambda e: e.memset(XPC, 0.0))
    vec(lambda e: e.memset(SBF[0], 0.0))

    for cc in range(NCC):
        wst = WST[cc % 2]
        dma(wst, wo_h[128 * cc:128 * cc + 128, :])
        for half in range(2):
            ten(lambda e, cc=cc, half=half, wst=wst: e.matmul(
                bank(half)[0:1, :], LNB[:, cc:cc + 1],
                wst[:, 512 * half:512 * half + 512],
                start=(cc == 0), stop=(cc == NCC - 1)))
    for half in range(2):
        act(lambda e, half=half: e.activation(
            BIASR[:, 512 * half:512 * half + 512], bank(half)[0:1, :], AF.Copy,
            bias=0.0, scale=1.0))

    def build_xf(f, mf):
        for cc in range(NCC):
            bk = 2 + cc % 2
            ten(lambda e, f=f, cc=cc, bk=bk: e.matmul(
                bank(bk)[:, 0:BT], MW2[32 * f:32 * f + 32, 128 * cc:128 * cc + 128],
                X1[32 * f:32 * f + 32, :], start=True, stop=True,
                tile_position=(32 * f, 0)))
            sl = slice(BT * cc, BT * cc + BT)
            vec(lambda e, cc=cc, sl=sl, mf=mf, bk=bk: e.scalar_tensor_tensor(
                QA[:, 0:BT], bank(bk)[:, 0:BT], MAA[mf][:, cc:cc + 1], XX_B[:, sl],
                ALU.add, ALU.mult))
            vec(lambda e, sl=sl: e.tensor_add(XF_B[:, sl], QA[:, 0:BT],
                                              XN_B[:, sl]))

    for b in range(min(NB if _KPH >= 2 else 0, _KBLK)):
        for tl in range(BCH):
            gt = b * BCH + tl
            dma(XTI, x_h[128 * gt:128 * gt + 128, :])
            vec(lambda e: e.reduce_sum(S1c, XTI, axis=AX.X))
            act(lambda e: e.activation(XF_B[:, C:2 * C], XTI, AF.Square,
                                       bias=0.0, scale=1.0, accum_out=S2c))
            vec(lambda e: e.tensor_single_scalar(MUc, S1c, 1.0 / C, ALU.mult))
            vec(lambda e: e.tensor_single_scalar(S2c, S2c, 1.0 / C, ALU.mult))
            vec(lambda e: e.tensor_mul(RSc, MUc, MUc))
            vec(lambda e: e.tensor_sub(S2c, S2c, RSc))
            vec(lambda e: e.tensor_single_scalar(S2c, S2c, LN_EPS, ALU.add))
            act(lambda e: e.activation(S2c, S2c, AF.Sqrt, bias=0.0, scale=1.0))
            vec(lambda e: e.reciprocal(RSc, S2c))
            vec(lambda e: e.scalar_tensor_tensor(NBc, MUc, -1.0, RSc, ALU.mult,
                                                 ALU.mult))
            vec(lambda e: e.tensor_scalar(XTI, XTI, RSc, NBc, ALU.mult, ALU.add))
            vec(lambda e: e.reduce_max(RMX, XTI, axis=AX.X,
                                       apply_absolute_value=True))
            vec(lambda e: e.tensor_single_scalar(RMX, RMX, 1e-5, ALU.max))
            vec(lambda e: e.reciprocal(RMX, RMX))
            vec(lambda e, gt=gt: e.tensor_single_scalar(QS1[:, gt:gt + 1], RMX,
                                                        127.0, ALU.mult))
            vec(lambda e, gt=gt: e.tensor_scalar(QA, XTI, QS1[:, gt:gt + 1],
                                                 MAGIC, ALU.mult, ALU.add))
            vec(lambda e: e.tensor_single_scalar(N1B, QA, MAGIC, ALU.subtract))
            dma(n1_h[128 * gt:128 * gt + 128, :], N1B, q="gpsimd")
            for cc in range(NCC):
                ten(lambda e, cc=cc: e.transpose(
                    bank(3)[:, 0:128], XTI[:, 128 * cc:128 * cc + 128], IDENT))
                act(lambda e, tl=tl, cc=cc: e.activation(
                    XN_B[:, BT * cc + 128 * tl:BT * cc + 128 * tl + 128],
                    bank(3)[:, 0:128], AF.Copy, bias=0.0, scale=1.0))
        if _KPART < 2:
            continue
        for cc in range(NCC):
            s0 = BT * cc
            vec(lambda e, s0=s0: e.tensor_sub(XX_B[:, s0 + 1:s0 + BT],
                                              XN_B[:, s0:s0 + BT - 1],
                                              XN_B[:, s0 + 1:s0 + BT]))
            vec(lambda e, s0=s0, cc=cc: e.tensor_sub(XX_B[:, s0:s0 + 1],
                                                     XPC[:, cc:cc + 1],
                                                     XN_B[:, s0:s0 + 1]))
        vec(lambda e: e.tensor_copy(
            XPC, XN_B.rearrange("p (a w) -> p a w", a=NCC)[:, :, BT - 1]))
        for cc in range(NCC):
            sl = slice(BT * cc, BT * cc + BT)
            vec(lambda e, cc=cc, sl=sl: e.scalar_tensor_tensor(
                XF_B[:, sl], XX_B[:, sl], MAA["x"][:, cc:cc + 1], XN_B[:, sl],
                ALU.mult, ALU.add))
        for cc in range(NCC):
            ten(lambda e, cc=cc: e.matmul(bank(2)[:, 0:BT],
                                          MW1[:, 128 * cc:128 * cc + 128],
                                          XF_B[:, BT * cc:BT * cc + BT],
                                          start=(cc == 0), stop=(cc == NCC - 1)))
        act(lambda e: e.activation(X1, bank(2)[:, 0:BT], AF.Tanh, bias=0.0,
                                   scale=1.0))
        if _KPART < 3:
            continue
        build_xf(0, "w")
        for cc in range(NCC):
            ten(lambda e, cc=cc: e.matmul(bank(2)[0:64, 0:BT],
                                          DW1[:, 64 * cc:64 * cc + 64],
                                          XF_B[:, BT * cc:BT * cc + BT],
                                          start=(cc == 0), stop=(cc == NCC - 1)))
        act(lambda e: e.activation(XD[0:64, :], bank(2)[0:64, 0:BT], AF.Tanh,
                                   bias=0.0, scale=1.0))
        for cc in range(NCC):
            bk = 2 + cc % 2
            ten(lambda e, cc=cc, bk=bk: e.matmul(bank(bk)[:, 0:BT],
                                                 DW2[0:64, 128 * cc:128 * cc + 128],
                                                 XD[0:64, :], start=True,
                                                 stop=True))
            act(lambda e, cc=cc, bk=bk: e.activation(CB[:, BT * cc:BT * cc + BT],
                                                     bank(bk)[:, 0:BT], AF.Exp,
                                                     bias=DEC[:, cc:cc + 1],
                                                     scale=1.0))
        for cc in range(NCC):
            for ch in range(BCH):
                sl = slice(BT * cc + 128 * ch, BT * cc + 128 * ch + 128)
                vec(lambda e, sl=sl: e.tensor_tensor_scan(SCB[:, sl], CB[:, sl],
                                                          ZER, 0.0, ALU.add,
                                                          ALU.add))
        act(lambda e: e.activation(CB, SCB, AF.Exp, bias=0.0, scale=-1.0))
        vec(lambda e, b=b: e.tensor_copy(
            PLv[:, :, b * BCH:b * BCH + BCH],
            CB.rearrange("p (a c s) -> p a c s", a=NCC, c=BCH)[:, :, :, 127]))
        if _KPART < 4:
            continue
        build_xf(1, "k")
        for dchunk in range(NCC):
            wst = WST[dchunk % 2]
            dma(wst, wk_h[128 * dchunk:128 * dchunk + 128, :])
            for dout in range(NCC):
                ten(lambda e, dchunk=dchunk, dout=dout, wst=wst: e.matmul(
                    bank(dout), wst[:, 128 * dout:128 * dout + 128],
                    XF_B[:, BT * dchunk:BT * dchunk + BT],
                    start=(dchunk == 0), stop=(dchunk == NCC - 1)))
        for dout in range(NCC):
            sl = slice(BT * dout, BT * dout + BT)
            act(lambda e, sl=sl: e.activation(QA[:, 0:BT], SCB[:, sl], AF.Exp,
                                              bias=0.0, scale=1.0))
            vec(lambda e, dout=dout, sl=sl: e.tensor_mul(KT_B[:, sl],
                                                         bank(dout), QA[:, 0:BT]))
            vec(lambda e, dout=dout, sl=sl: e.tensor_scalar(
                SCB[:, sl], bank(dout), UPP[:, dout:dout + 1], None, ALU.mult))
        if _KPART < 5:
            continue
        build_xf(3, "r")
        for dchunk in range(NCC):
            wst = WST[dchunk % 2]
            dma(wst, wr_h[128 * dchunk:128 * dchunk + 128, :])
            for dout in range(NCC):
                ten(lambda e, dchunk=dchunk, dout=dout, wst=wst: e.matmul(
                    bank(dout), wst[:, 128 * dout:128 * dout + 128],
                    XF_B[:, BT * dchunk:BT * dchunk + BT],
                    start=(dchunk == 0), stop=(dchunk == NCC - 1)))
        for dout in range(NCC):
            s0 = BT * dout
            vec(lambda e, dout=dout, s0=s0: e.tensor_mul(
                RT_B[:, s0 + 1:s0 + BT], bank(dout)[:, 1:BT],
                CB[:, s0:s0 + BT - 1]))
            vec(lambda e, dout=dout, s0=s0: e.tensor_copy(
                RT_B[:, s0:s0 + BT].rearrange("p (c s) -> p c s", c=BCH)[:, :, 0],
                bank(dout).rearrange("p (c s) -> p c s", c=BCH)[:, :, 0]))
            vec(lambda e, dout=dout, s0=s0: e.tensor_mul(XD, bank(dout),
                                                         SCB[:, s0:s0 + BT]))
            for chd in range(BCH):
                dsl = slice(512 * dout, 512 * dout + 2)
                ten(lambda e, chd=chd, dsl=dsl: e.matmul(
                    PS[:, dsl], XD[:, 128 * chd:128 * chd + 128], IND2,
                    start=True, stop=True))
                act(lambda e, chd=chd, dout=dout, dsl=dsl: e.activation(
                    DTMB[:, 16 * chd + 2 * dout:16 * chd + 2 * dout + 2],
                    PS[:, dsl], AF.Copy, bias=0.0, scale=1.0))
        if _KPART < 6:
            continue
        build_xf(2, "v")
        for dchunk in range(NCC):
            wst = WST[dchunk % 2]
            dma(wst, wv_h[128 * dchunk:128 * dchunk + 128, :])
            for ch in range(BCH):
                for half in range(2):
                    ten(lambda e, dchunk=dchunk, ch=ch, half=half, wst=wst:
                        e.matmul(
                            bank(2 * ch + half),
                            XF_B[:, BT * dchunk + 128 * ch:BT * dchunk + 128 * ch + 128],
                            wst[:, 512 * half:512 * half + 512],
                            start=(dchunk == 0), stop=(dchunk == NCC - 1)))
        for ch in range(BCH):
            for half in range(2):
                act(lambda e, ch=ch, half=half: e.activation(
                    V_B[:, C * ch + 512 * half:C * ch + 512 * half + 512],
                    bank(2 * ch + half), AF.Copy, bias=0.0, scale=1.0))
        if _KPART < 7:
            continue
        for ch in range(BCH):
            g = b * BCH + ch
            Scur, Snxt = SBF[g % 2], SBF[(g + 1) % 2]
            for cc in range(NCC):
                vec(lambda e, cc=cc, ch=ch, g=g: e.tensor_scalar(
                    KHF[:, 128 * cc:128 * cc + 128],
                    KT_B[:, BT * cc + 128 * ch:BT * cc + 128 * ch + 128],
                    PLv[:, cc, g:g + 1], None, ALU.mult))
            for cc in range(NCC):
                ten(lambda e, cc=cc: e.transpose(
                    bank(3)[:, 0:128], KHF[:, 128 * cc:128 * cc + 128], IDENT))
                act(lambda e, cc=cc: e.activation(
                    KHTM[:, 128 * cc:128 * cc + 128], bank(3)[:, 0:128], AF.Copy,
                    bias=0.0, scale=1.0))
            if _KWKV < 2:
                continue
            for h in [x for x in range(NH) if x % 2 == 0] + \
                     [x for x in range(NH) if x % 2 == 1]:
                cc, base = h // 2, 64 * (h % 2)
                q, s4 = 4 + h // 4, 128 * (h % 4)
                sl = slice(BT * cc + 128 * ch, BT * cc + 128 * ch + 128)
                ten(lambda e, cc=cc, base=base, q=q, s4=s4, sl=sl: e.matmul(
                    bank(q)[:, s4:s4 + 128], KT_B[base:base + 64, sl],
                    RT_B[base:base + 64, sl], start=True, stop=True,
                    tile_position=(base, 0)))
            for q in range(4):
                for hp in range(2):
                    vec(lambda e, q=q, hp=hp: e.tensor_mul(
                        GM[:, 512 * q + 256 * hp:512 * q + 256 * hp + 256],
                        bank(4 + q)[:, 256 * hp:256 * hp + 256], MSK2))
            if _KWKV < 3:
                continue
            for h in range(NH):
                vsl = slice(C * ch + HD * h, C * ch + HD * h + HD)
                ob = bank(h // 8)[:, (64 * h) % 512:(64 * h) % 512 + 64]
                ten(lambda e, h=h, ob=ob, vsl=vsl: e.matmul(
                    ob, GM[:, 128 * h:128 * h + 128], V_B[:, vsl],
                    start=(h % 8 == 0), stop=False))
            for h in [x for x in range(NH) if x % 2 == 0] + \
                     [x for x in range(NH) if x % 2 == 1]:
                cc, base = h // 2, 64 * (h % 2)
                sl = slice(BT * cc + 128 * ch, BT * cc + 128 * ch + 128)
                ob = bank(h // 8)[:, (64 * h) % 512:(64 * h) % 512 + 64]
                ten(lambda e, cc=cc, base=base, sl=sl, ob=ob, h=h, Scur=Scur:
                    e.matmul(ob, RT_B[base:base + 64, sl],
                             Scur[base:base + 64, HD * (h // 2):HD * (h // 2) + HD],
                             start=False, stop=(h in (7, 15)),
                             tile_position=(base, 0)))
            vec(lambda e, ch=ch: e.tensor_mul(
                GM[:, 0:C].rearrange("p (h e) -> p h e", h=NH),
                V_B[:, C * ch:C * ch + C].rearrange("p (h e) -> p h e", h=NH),
                DTMB[:, 16 * ch:16 * ch + 16].unsqueeze(2)
                .broadcast_to([128, NH, HD])))
            vec(lambda e: e.tensor_add(YS[:, 0:512], GM[:, 0:512], bank(0)))
            vec(lambda e: e.tensor_add(YS[:, 512:1024], GM[:, 512:1024], bank(1)))
            if _KWKV < 4:
                continue
            for h in [x for x in range(NH) if x % 2 == 0] + \
                     [x for x in range(NH) if x % 2 == 1]:
                vsl = slice(C * ch + HD * h, C * ch + HD * h + HD)
                ob = bank(2)[64 * (h % 2):64 * (h % 2) + 64,
                             HD * (h // 2):HD * (h // 2) + HD]
                ten(lambda e, h=h, ob=ob, vsl=vsl: e.matmul(
                    ob, KHTM[:, HD * h:HD * h + HD], V_B[:, vsl],
                    start=True, stop=True, tile_position=(0, 64 * (h % 2))))
            for cc in range(NCC):
                vec(lambda e, cc=cc, g=g, Scur=Scur, Snxt=Snxt:
                    e.scalar_tensor_tensor(
                        Snxt[:, HD * cc:HD * cc + HD], Scur[:, HD * cc:HD * cc + HD],
                        PLv[:, cc, g:g + 1], bank(2)[:, HD * cc:HD * cc + HD],
                        ALU.mult, ALU.add))
            if _KWKV < 5:
                continue
            vec(lambda e: e.tensor_mul(GM[:, 0:C], YS, YS))
            vec(lambda e: e.reduce_sum(GN1, YS.rearrange("p (h e) -> p h e", h=NH),
                                       axis=AX.X))
            vec(lambda e: e.reduce_sum(GN2,
                                       GM[:, 0:C].rearrange("p (h e) -> p h e",
                                                            h=NH), axis=AX.X))
            vec(lambda e: e.tensor_single_scalar(GMU, GN1, 1.0 / HD, ALU.mult))
            vec(lambda e: e.tensor_single_scalar(GN2, GN2, 1.0 / HD, ALU.mult))
            vec(lambda e: e.tensor_mul(GRS, GMU, GMU))
            vec(lambda e: e.tensor_sub(GN2, GN2, GRS))
            vec(lambda e: e.tensor_single_scalar(GN2, GN2, GN_EPS, ALU.add))
            act(lambda e: e.activation(GN2, GN2, AF.Sqrt, bias=0.0, scale=1.0))
            vec(lambda e: e.reciprocal(GRS, GN2))
            vec(lambda e: e.tensor_sub(YS.rearrange("p (h e) -> p h e", h=NH),
                                       YS.rearrange("p (h e) -> p h e", h=NH),
                                       GMU.unsqueeze(2).broadcast_to([128, NH, HD])))
            vec(lambda e: e.tensor_mul(YS.rearrange("p (h e) -> p h e", h=NH),
                                       YS.rearrange("p (h e) -> p h e", h=NH),
                                       GRS.unsqueeze(2).broadcast_to([128, NH, HD])))
            if _KWKV < 6:
                continue
            for cc in range(NCC):
                ten(lambda e, cc=cc: e.transpose(
                    bank(3)[:, 0:128], YS[:, 128 * cc:128 * cc + 128], IDENT))
                act(lambda e, cc=cc, ch=ch: e.activation(
                    YNFB[:, BT * cc + 128 * ch:BT * cc + 128 * ch + 128],
                    bank(3)[:, 0:128], AF.Copy, bias=0.0, scale=LNW[:, cc:cc + 1]))
        if _KPART < 8:
            continue
        for ch in range(BCH):
            for half in range(2):
                ten(lambda e, ch=ch, half=half: e.matmul(
                    bank(2 * ch + half), ONER, BIASR[:, 512 * half:512 * half + 512],
                    start=True, stop=False))
        for cc in range(NCC):
            wst = WST[cc % 2]
            dma(wst, wo_h[128 * cc:128 * cc + 128, :])
            for ch in range(BCH):
                for half in range(2):
                    ten(lambda e, cc=cc, ch=ch, half=half, wst=wst: e.matmul(
                        bank(2 * ch + half),
                        YNFB[:, BT * cc + 128 * ch:BT * cc + 128 * ch + 128],
                        wst[:, 512 * half:512 * half + 512],
                        start=False, stop=(cc == NCC - 1)))
        for ch in range(BCH):
            gt = b * BCH + ch
            xrl = V_B[:, C * (ch % 2):C * (ch % 2) + C]
            osl = V_B[:, C * (2 + ch % 2):C * (2 + ch % 2) + C]
            dma(xrl, x_h[128 * gt:128 * gt + 128, :])
            for half in range(2):
                vec(lambda e, ch=ch, half=half, xrl=xrl, osl=osl: e.tensor_add(
                    osl[:, 512 * half:512 * half + 512],
                    xrl[:, 512 * half:512 * half + 512], bank(2 * ch + half)))
            dma(xp_h[128 * gt:128 * gt + 128, :], osl, q="gpsimd")
        lo = b * len(wq_jobs) // NB
        hi = (b + 1) * len(wq_jobs) // NB
        for job in wq_jobs[lo:hi]:
            job()
        wq_jobs[lo:hi] = [None] * (hi - lo)

    for job in wq_jobs:
        if job is not None:
            job()

    fence()
    attn_stack.close()
    wq_stack.close()

    ffn_stack = contextlib.ExitStack()
    sbf = mk_sb(ffn_stack)
    N1T = sbf("n1t", (128, NCC * T), BF16)
    T1SL = [sbf("t1sl0", (128, 512), BF16), sbf("t1sl1", (128, 512), BF16)]
    INVS1B = sbf("invs1b", (128, T))
    HB0 = sbf("hb0", (128, NHC * 258))
    HB1 = sbf("hb1", (128, NHC * 258))
    HB2 = sbf("hb2", (128, NHC * 258))
    GST = sbf("gstash", (128, NHC))
    ACC = sbf("accc", (128, 256))
    SIL = sbf("silc", (128, 256))
    HCT = sbf("hct", (128, 2 * HID))
    N2B = sbf("n2b", (128, HID), BF16)
    IVT = sbf("ivt", (16, 128))
    INVR = sbf("invr", (1, T))
    HBUF = [HB0, HB1, HB2]

    for dchunk in range(NCC):
        dma(N1T[:, T * dchunk:T * dchunk + T],
            n1_h[:, 128 * dchunk:128 * dchunk + 128], transpose=True)
    vec(lambda e: e.reciprocal(SC16, QS1))
    vec(lambda e: e.scalar_tensor_tensor(SC16, SC16, 0.5,
                                         Q1.broadcast_to([128, 16]), ALU.mult,
                                         ALU.mult))
    ten(lambda e: e.transpose(bank(0)[0:16, 0:128], SC16, IDENT))
    act(lambda e: e.activation(IVT, bank(0)[0:16, 0:128], AF.Copy, bias=0.0,
                               scale=1.0))
    dma(INVR.rearrange("p (a w) -> p a w", a=16), IVT)
    for qq in range(4):
        ten(lambda e, qq=qq: e.matmul(bank(qq % 2), ONER,
                                      INVR[:, 512 * qq:512 * qq + 512],
                                      start=True, stop=True))
        act(lambda e, qq=qq: e.activation(INVS1B[:, 512 * qq:512 * qq + 512],
                                          bank(qq % 2), AF.Copy, bias=0.0,
                                          scale=1.0))
    vec(lambda e: e.memset(GST, 0.0))

    def conv_eighth(j):
        hb, hn = HBUF[j % 3], HBUF[(j + 1) % 3]
        hbv = hb.rearrange("p (a w) -> p a w", a=NHC)
        vec(lambda e, hbv=hbv: e.tensor_copy(hbv[:, :, 0], GST))
        if j < NE - 1:
            vec(lambda e, hbv=hbv, hn=hn: e.tensor_copy(
                hbv[:, :, 257], hn.rearrange("p (a w) -> p a w", a=NHC)[:, :, 1]))
        else:
            vec(lambda e, hbv=hbv: e.memset(hbv[:, :, 257], 0.0))
        vec(lambda e, hbv=hbv: e.tensor_copy(GST, hbv[:, :, 256]))
        for hc in range(NHC):
            w0 = 258 * hc
            vec(lambda e, w0=w0, hc=hc, hb=hb: e.tensor_scalar(
                ACC, hb[:, w0:w0 + 256], TAP[0][:, hc:hc + 1], None, ALU.mult))
            vec(lambda e, w0=w0, hc=hc, hb=hb: e.scalar_tensor_tensor(
                ACC, hb[:, w0 + 1:w0 + 257], TAP[1][:, hc:hc + 1], ACC,
                ALU.mult, ALU.add))
            vec(lambda e, w0=w0, hc=hc, hb=hb: e.scalar_tensor_tensor(
                ACC, hb[:, w0 + 2:w0 + 258], TAP[2][:, hc:hc + 1], ACC,
                ALU.mult, ALU.add))
            act(lambda e, hc=hc: e.activation(SIL, ACC, AF.Silu,
                                              bias=CBB[:, hc:hc + 1], scale=1.0))
            for chh in range(2):
                pb = (hc * 2 + chh) % 2
                ten(lambda e, chh=chh, pb=pb: e.transpose(
                    bank(pb)[:, 0:128], SIL[:, 128 * chh:128 * chh + 128], IDENT))
                act(lambda e, hc=hc, chh=chh, pb=pb: e.activation(
                    HCT[:, HID * chh + 128 * hc:HID * chh + 128 * hc + 128],
                    bank(pb)[:, 0:128], AF.Copy, bias=0.0, scale=1.0))
        for chh in range(2):
            tc = 2 * j + chh
            hsl = slice(HID * chh, HID * chh + HID)
            vec(lambda e, hsl=hsl: e.reduce_max(RMX, HCT[:, hsl], axis=AX.X,
                                                apply_absolute_value=True))
            vec(lambda e: e.tensor_single_scalar(RMX, RMX, 1e-5, ALU.max))
            vec(lambda e: e.reciprocal(RMX, RMX))
            vec(lambda e, tc=tc: e.tensor_single_scalar(QS2[:, tc:tc + 1], RMX,
                                                        127.0, ALU.mult))
            vec(lambda e, hsl=hsl, tc=tc: e.tensor_scalar(
                HCT[:, hsl], HCT[:, hsl], QS2[:, tc:tc + 1], MAGIC, ALU.mult,
                ALU.add))
            vec(lambda e, hsl=hsl: e.tensor_single_scalar(N2B, HCT[:, hsl],
                                                          MAGIC, ALU.subtract))
            dma(n2_h[128 * tc:128 * tc + 128, :], N2B, q="gpsimd")

    for q in range(4 if _KPH >= 3 else 0):
        for hg in range(8):
            for dchunk in range(NCC):
                t1 = T1SL[dchunk % 2]
                dma(t1, t1q_h[512 * hg:512 * hg + 512,
                              128 * dchunk:128 * dchunk + 128], transpose=True,
                    q="scalar")
                for h4 in range(4):
                    ten(lambda e, dchunk=dchunk, h4=h4, q=q, t1=t1: e.matmul(
                        bank(h4), t1[:, 128 * h4:128 * h4 + 128],
                        N1T[:, T * dchunk + 512 * q:T * dchunk + 512 * q + 512],
                        start=(dchunk == 0), stop=(dchunk == NCC - 1)))
            for h4 in range(4):
                hc = 4 * hg + h4
                w0 = 258 * hc
                for e2 in range(2):
                    e8 = 2 * q + e2
                    vec(lambda e, h4=h4, w0=w0, e2=e2, e8=e8: e.tensor_mul(
                        HBUF[e8 % 3][:, w0 + 1:w0 + 257],
                        bank(h4)[:, 256 * e2:256 * e2 + 256],
                        INVS1B[:, ET * e8:ET * e8 + ET]))
        if q >= 1:
            conv_eighth(2 * q - 1)
        conv_eighth(2 * q)
    if _KPH >= 3:
        conv_eighth(NE - 1)

    vec(lambda e: e.reciprocal(SC16, QS2))
    vec(lambda e: e.scalar_tensor_tensor(INVS2, SC16, 0.5,
                                         Q2.broadcast_to([128, 16]), ALU.mult,
                                         ALU.mult))

    fence()
    ffn_stack.close()
    ffn2_stack = contextlib.ExitStack()
    sbg = mk_sb(ffn2_stack)
    N2TS = [sbg("n2ts0", (128, 512), BF16), sbg("n2ts1", (128, 512), BF16)]
    T2R = sbg("t2r", (128, NHC * 1024), BF16)
    XRE = [sbg("xre0", (128, C)), sbg("xre1", (128, C))]
    OST = [sbg("ost0", (128, C)), sbg("ost1", (128, C))]

    for hc in range(NHC):
        dma(T2R[:, 1024 * hc:1024 * hc + 1024],
            t2q_h[:, 128 * hc:128 * hc + 128], transpose=True)
    for tg in range(4 if _KPH >= 4 else 0):
        for hc in range(NHC):
            n2t = N2TS[hc % 2]
            dma(n2t, n2_h[512 * tg:512 * tg + 512, 128 * hc:128 * hc + 128],
                transpose=True, q="scalar")
            for t4 in range(4):
                for half in range(2):
                    ten(lambda e, hc=hc, t4=t4, half=half, n2t=n2t: e.matmul(
                        bank(2 * t4 + half), n2t[:, 128 * t4:128 * t4 + 128],
                        T2R[:, 1024 * hc + 512 * half:1024 * hc + 512 * half + 512],
                        start=(hc == 0), stop=(hc == NHC - 1)))
        for t4 in range(4):
            tc = 4 * tg + t4
            xre, ost = XRE[t4 % 2], OST[t4 % 2]
            dma(xre, xp_h[128 * tc:128 * tc + 128, :])
            for half in range(2):
                vec(lambda e, t4=t4, half=half, tc=tc, xre=xre, ost=ost:
                    e.scalar_tensor_tensor(
                        ost[:, 512 * half:512 * half + 512], bank(2 * t4 + half),
                        INVS2[:, tc:tc + 1], xre[:, 512 * half:512 * half + 512],
                        ALU.mult, ALU.add))
            dma(out_h[128 * tc:128 * tc + 128, :], ost, q="gpsimd")

    ffn2_stack.close()
    perm.close()
    return prog


def _extract_rw(nc, prog):
    with nc.Block() as blk:
        for eng in ENGS:
            ops = [p for p in prog if p[0] == eng]
            if not ops:
                continue
            getattr(blk, eng)(lambda e, ops=ops: [(p[1](e), e.nop(nofuse=True))
                                                  for p in ops])

    def keys_of(args):
        out = set()
        for a in args:
            memref = getattr(a, "memref", None)
            if memref is None:
                continue
            off = getattr(a, "offset", 0) or 0
            ap = getattr(a, "ap", None) or []
            span = 1
            for step, count in ap:
                span += abs(step) * (count - 1)
            if memref == "ps":
                b0, b1 = off // 512, (off + span - 1) // 512
                for bk in range(b0, b1 + 1):
                    out.add("ps:%d" % bk)
            else:
                b0, b1 = off // 2048, (off + span - 1) // 2048
                for bk in range(b0, b1 + 1):
                    out.add("%s:%d" % (memref, bk))
        return out

    groups = {eng: [] for eng in ENGS}
    cur = {eng: ([], []) for eng in ENGS}
    for inst in nc.all_instructions():
        eng = {"PE": "tensor", "Activation": "scalar", "DVE": "vector",
               "Pool": "gpsimd", "SP": "sync"}.get(str(inst.engine).split(".")[-1])
        if eng is None:
            continue
        if type(inst).__name__ == "InstNoOp":
            r, w = cur[eng]
            groups[eng].append((frozenset(r), frozenset(w)))
            cur[eng] = ([], [])
        else:
            r, w = cur[eng]
            r.extend(keys_of(inst.ins))
            w.extend(keys_of(inst.outs))
    idx = {eng: 0 for eng in ENGS}
    rw = []
    for p in prog:
        if p[0] == "FENCE":
            rw.append(None)
            continue
        eng = p[0]
        rw.append(groups[eng][idx[eng]])
        idx[eng] += 1
    return rw


def build_nc():
    nc1 = bass.Bass("TRN2", target_bir_lowering=False)
    prog1 = build_prog(nc1)
    rw = _extract_rw(nc1, prog1)

    nc = bass.Bass("TRN2", target_bir_lowering=False)
    prog = build_prog(nc)
    assert len(prog) == len(prog1)

    KP = 10
    tokens = []
    cnt = {e: 0 for e in ENGS}
    dmai = {e: 0 for e in ENGS}
    for p in prog:
        if p[0] == "FENCE":
            tokens.append(None)
            continue
        eng, fn, inc = p
        if inc == 16:
            j = dmai[eng]
            dmai[eng] += 1
            key = ("dma", eng, j % KP)
            val = 16 * (j // KP + 1)
            selfneed = ((key, val - 16) if j >= KP else None)
        else:
            cnt[eng] += 1
            key = ("eng", eng)
            val = cnt[eng]
            selfneed = None
        tokens.append((key, val, selfneed))

    writer = {}
    readers = {}
    fence_need = {}
    sched = []
    ti = 0
    for p, k in zip(prog, rw):
        if p[0] == "FENCE":
            sched.append(("FENCE", None, 0, None, dict(fence_need)))
            ti += 1
            continue
        eng, fn, inc = p
        tok = tokens[ti]
        ti += 1
        key, val, selfneed = tok
        rds, wrs = k
        need = {}

        def add(tk):
            if tk is None:
                return
            tkey, tval = tk
            if tkey == ("eng", eng) and eng == "tensor":
                return
            if need.get(tkey, 0) < tval:
                need[tkey] = tval
        add(selfneed)
        for rk in rds:
            add(writer.get(rk))
        for rk in wrs:
            add(writer.get(rk))
            for tkey, tval in readers.get(rk, {}).items():
                add((tkey, tval))
        sched.append((eng, fn, inc, (key, val), need))
        for rk in rds:
            readers.setdefault(rk, {})[key] = val
        for rk in wrs:
            writer[rk] = (key, val)
            readers[rk] = {key: val}
        fence_need[key] = val

    finals = dict(fence_need)

    semkeys = sorted(set(fence_need.keys()), key=str)
    with contextlib.ExitStack() as es:
        sems = {sk: es.enter_context(nc.semaphore("s_" + "_".join(map(str, sk))))
                for sk in semkeys}
        with nc.Block() as block:
            def emit(name, handle):
                water = {sk: 0 for sk in semkeys}
                for eng, fn, inc, tok, need in sched:
                    if eng == "FENCE":
                        for sk, v in need.items():
                            if sk == ("eng", name):
                                continue
                            if water[sk] < v:
                                handle.wait_ge(sems[sk], v)
                                water[sk] = v
                        continue
                    if eng != name:
                        continue
                    for sk, v in sorted(need.items(), key=str):
                        if water[sk] < v:
                            handle.wait_ge(sems[sk], v)
                            if sk != ("eng", name):
                                water[sk] = v
                    fn(handle).then_inc(sems[tok[0]], inc)
                if name == "gpsimd":
                    for sk, v in sorted(finals.items(), key=str):
                        if sk != ("eng", name) and water[sk] < v:
                            handle.wait_ge(sems[sk], v)

            block.sync(lambda e: emit("sync", e))
            block.vector(lambda e: emit("vector", e))
            block.scalar(lambda e: emit("scalar", e))
            block.tensor(lambda e: emit("tensor", e))
            block.gpsimd(lambda e: emit("gpsimd", e))

    return nc


_NC_CACHE = []


def kernel(L0_attn_query, L0_attn_key, L0_attn_value, L0_attn_time_maa_x,
           L0_attn_time_maa_w, L0_attn_time_maa_k, L0_attn_time_maa_v,
           L0_attn_time_maa_r, L0_attn_time_maa_w1, L0_attn_time_maa_w2,
           L0_attn_time_decay, L0_attn_time_decay_w1, L0_attn_time_decay_w2,
           L0_attn_time_faaaa, L0_attn_W_r, L0_attn_W_k, L0_attn_W_v, L0_attn_W_o,
           L0_attn_ln_w, L0_attn_ln_b, L0_ffn_w1, L0_ffn_conv_w, L0_ffn_conv_b,
           L0_ffn_w2):
    x = np.ascontiguousarray(np.asarray(L0_attn_query, dtype=np.float32))
    w = {
        "wr": L0_attn_W_r, "wk": L0_attn_W_k, "wv": L0_attn_W_v, "wo": L0_attn_W_o,
        "maa_x": L0_attn_time_maa_x, "maa_w": L0_attn_time_maa_w,
        "maa_k": L0_attn_time_maa_k, "maa_v": L0_attn_time_maa_v,
        "maa_r": L0_attn_time_maa_r, "mw1": L0_attn_time_maa_w1,
        "mw2": L0_attn_time_maa_w2, "dec": L0_attn_time_decay,
        "dw1": L0_attn_time_decay_w1, "dw2": L0_attn_time_decay_w2,
        "faaaa": L0_attn_time_faaaa, "lnw": L0_attn_ln_w, "lnb": L0_attn_ln_b,
        "w1": L0_ffn_w1, "cw": L0_ffn_conv_w, "cb": L0_ffn_conv_b, "w2": L0_ffn_w2,
    }
    w = {k: np.ascontiguousarray(np.asarray(v, dtype=np.float32))
         for k, v in w.items()}
    if not _NC_CACHE:
        _NC_CACHE.append(build_nc())
    nc = _NC_CACHE[0]
    in_maps = [dict(w, x=np.ascontiguousarray(x[i])) for i in range(8)]
    res = run_bass_kernel_spmd(nc, in_maps, core_ids=list(range(8)))
    return np.stack([np.asarray(r["out"], dtype=np.float32)
                     for r in res.results], axis=0)


# version 09
# speedup vs baseline: 2.4189x; 1.1030x over previous
import contextlib
import os

import numpy as np
import concourse.bass as bass
import concourse.mybir as mybir
from concourse.bass_utils import run_bass_kernel_spmd

F32 = mybir.dt.float32
BF16 = mybir.dt.bfloat16
AF = mybir.ActivationFunctionType
ALU = mybir.AluOpType
AX = mybir.AxisListType

MAGIC = 12582912.0
T, C, HID, NH, HD = 2048, 1024, 4096, 16, 64
NCC = 8
NHC = 32
NB, BT, BCH = 4, 512, 4
NE, ET = 8, 256
LN_EPS, GN_EPS = 1e-5, 0.00064
MEAN_W1 = 1.0 / (HID * C)
MEAN_W2 = 1.0 / (C * HID)
_KPH = int(os.environ.get("KPHASE", "99"))
_KBLK = int(os.environ.get("KBLK", "99"))
_KPART = int(os.environ.get("KPART", "99"))
_KWKV = int(os.environ.get("KWKV", "99"))


ENGS = ("sync", "vector", "scalar", "tensor", "gpsimd")


def build_prog(nc):

    def din(name, shape, dt=F32):
        return nc.dram_tensor(name, list(shape), dt, kind="ExternalInput").ap()

    x_h = din("x", (T, C))
    wr_h, wk_h, wv_h, wo_h = (din(n, (C, C)) for n in ("wr", "wk", "wv", "wo"))
    maa_h = {f: din("maa_" + f, (C,)) for f in ("x", "w", "k", "v", "r")}
    mw1_h = din("mw1", (C, 128))
    mw2_h = din("mw2", (4, 32, C))
    dec_h = din("dec", (C,))
    dw1_h = din("dw1", (C, 64))
    dw2_h = din("dw2", (64, C))
    faaaa_h = din("faaaa", (NH, HD))
    lnw_h, lnb_h = din("lnw", (C,)), din("lnb", (C,))
    w1_h = din("w1", (HID, C))
    cw_h = din("cw", (HID, 1, 3))
    cb_h = din("cb", (HID,))
    w2_h = din("w2", (C, HID))

    out_h = nc.dram_tensor("out", [T, C], F32, kind="ExternalOutput").ap()
    n1_h = nc.dram_tensor("n1s", [T, C], BF16, kind="Internal").ap()
    t1q_h = nc.dram_tensor("t1qs", [HID, C], BF16, kind="Internal").ap()
    n2_h = nc.dram_tensor("n2s", [T, HID], BF16, kind="Internal").ap()
    t2q_h = nc.dram_tensor("t2qs", [C, HID], BF16, kind="Internal").ap()
    xp_h = nc.dram_tensor("xps", [T, C], F32, kind="Internal").ap()

    def mk_sb(stack):
        def sb(name, shape, dt=F32):
            return stack.enter_context(nc.sbuf_tensor(name, list(shape), dt)).ap()
        return sb

    perm = contextlib.ExitStack()
    sb = mk_sb(perm)

    IDENT = sb("ident", (128, 128))
    MSK2 = sb("msk2", (128, 256))
    ONEC = sb("onec", (128, 1))
    ONER = sb("oner", (1, 128))
    IND2 = sb("ind2", (128, 2))
    NH5 = sb("nh5", (128, 1))
    PH5 = sb("ph5", (128, 1))
    TAP = [sb("tap%d" % i, (128, NHC)) for i in range(3)]
    CBB = sb("cbpp", (128, NHC))
    S1c = sb("s1c", (128, 1))
    S2c = sb("s2c", (128, 1))
    MUc = sb("muc", (128, 1))
    RSc = sb("rsc", (128, 1))
    NBc = sb("nbc", (128, 1))
    QS1 = sb("qs1", (128, 16))
    QS2 = sb("qs2", (128, 16))
    INVS2 = sb("invs2", (128, 16))
    SC16 = sb("sc16", (128, 16))
    RMX = sb("rmx", (128, 1))
    PL = sb("plx", (128, 128))
    Q1, Q2 = sb("q1", (128, 1)), sb("q2", (128, 1))
    SONE = sb("sone", (1, 1))
    PLv = PL.rearrange("p (a g) -> p a g", a=NCC)

    prog = []

    def fence():
        prog.append(("FENCE", None, 0))

    def op(eng, inc=1):
        def deco(fn):
            prog.append((eng, fn, inc))
        return deco

    def dma(out, in_, transpose=False, q="sync"):
        @op(q, 16)
        def _(e, out=out, in_=in_, tr=transpose):
            if tr:
                return e.dma_start(out=out, in_=in_, transpose=True)
            return e.dma_start(out=out, in_=in_)

    def vec(fn):
        @op("vector")
        def _(e, fn=fn):
            return fn(e)

    def act(fn):
        @op("scalar")
        def _(e, fn=fn):
            return fn(e)

    def ten(fn):
        @op("tensor")
        def _(e, fn=fn):
            return fn(e)

    def gps(fn):
        @op("gpsimd")
        def _(e, fn=fn):
            return fn(e)

    vec(lambda e: e.memset(IDENT, 1.0))
    gps(lambda e: e.affine_select(IDENT, IDENT, [[-1, 128]], ALU.is_equal,
                                  0.0, base=0, channel_multiplier=1))
    gps(lambda e: e.iota(MSK2.rearrange("p (a b) -> p a b", a=2), [[0, 2], [1, 128]],
                         base=0, channel_multiplier=-1,
                         allow_small_or_imprecise_dtypes=True))
    vec(lambda e: e.tensor_single_scalar(MSK2, MSK2, 0.0, ALU.is_gt))
    vec(lambda e: e.memset(ONEC, 1.0))
    vec(lambda e: e.memset(ONER, 1.0))
    vec(lambda e: e.memset(NH5, -0.5))
    vec(lambda e: e.memset(PH5, 0.5))
    vec(lambda e: e.memset(IND2, 0.0))
    vec(lambda e: e.memset(IND2[0:64, 0:1], 1.0))
    vec(lambda e: e.memset(IND2[64:128, 1:2], 1.0))

    PSH = nc.alloc_psum_tensor("ps", [128, 4096], F32)
    PS = PSH.ap()

    def bank(k, w=512):
        return PS[:, 512 * k:512 * k + w]

    cw_stack = contextlib.ExitStack()
    sbc = mk_sb(cw_stack)
    CWS = sbc("cws", (32, 384))
    CBS = sbc("cbs", (32, 128))
    dma(CWS, cw_h.rearrange("(a b) o t -> a (b o t)", a=32))
    dma(CBS, cb_h.rearrange("(a b) -> a b", a=32))
    for tau in range(3):
        ten(lambda e, tau=tau: e.transpose(
            bank(0, 32)[:, 0:32],
            CWS.rearrange("a (b t) -> a b t", t=3)[:, :, tau], IDENT[0:32, 0:32]))
        act(lambda e, tau=tau: e.activation(TAP[tau], bank(0, 32)[:, 0:32], AF.Copy,
                                            bias=0.0, scale=1.0))
    ten(lambda e: e.transpose(bank(0, 32)[:, 0:32], CBS, IDENT[0:32, 0:32]))
    act(lambda e: e.activation(CBB, bank(0, 32)[:, 0:32], AF.Copy, bias=0.0,
                               scale=1.0))
    fence()
    cw_stack.close()

    wq_stack = contextlib.ExitStack()
    sbw = mk_sb(wq_stack)
    WQT = [sbw("wqt0", (128, C)), sbw("wqt1", (128, C))]
    WQD = sbw("wqd", (128, C))
    WQA = [sbw("wqa0", (128, C), BF16), sbw("wqa1", (128, C), BF16)]
    WQO = sbw("wqo", (128, C), BF16)
    WSUM = sbw("wsum", (128, 32))
    SW = sbw("sw", (128, 1))

    def wq_p1_job(w_hbm, rt, ct, si):
        def go():
            wt = WQT[si % 2]
            dma(wt, w_hbm[128 * rt:128 * rt + 128, 1024 * ct:1024 * ct + 1024])
            act(lambda e, si=si, wt=wt: e.activation(
                WQD, wt, AF.Abs, bias=0.0, scale=1.0,
                accum_out=WSUM[:, si:si + 1]))
        return go

    def wq_mid_job(nsub, meanmul, qtile):
        def go():
            vec(lambda e, nsub=nsub: e.reduce_sum(RMX, WSUM[:, 0:nsub], axis=AX.X))
            ten(lambda e: e.matmul(bank(0)[0:1, 0:1], RMX, ONEC, start=True,
                                   stop=True))
            act(lambda e: e.activation(SONE, bank(0)[0:1, 0:1], AF.Copy, bias=0.0,
                                       scale=1.0))
            vec(lambda e, mm=meanmul: e.tensor_scalar(SONE, SONE, mm, 1e-5,
                                                      ALU.mult, ALU.max))
            ten(lambda e: e.matmul(bank(0)[:, 0:1], ONER, SONE, start=True,
                                   stop=True))
            act(lambda e, qtile=qtile: e.activation(qtile, bank(0)[:, 0:1],
                                                    AF.Copy, bias=0.0, scale=1.0))
            vec(lambda e, qtile=qtile: e.reciprocal(SW, qtile))
        return go

    def wq_p2_job(w_hbm, tq_hbm, rt, ct):
        def go():
            wt = WQT[(rt + ct) % 2]
            dma(wt, w_hbm[128 * rt:128 * rt + 128, 1024 * ct:1024 * ct + 1024])
            act(lambda e, wt=wt: e.activation(WQA[0], wt, AF.Sign, bias=NH5,
                                              scale=SW))
            act(lambda e, wt=wt: e.activation(WQA[1], wt, AF.Sign, bias=PH5,
                                              scale=SW))
            vec(lambda e: e.tensor_add(WQO, WQA[0], WQA[1]))
            dma(tq_hbm[128 * rt:128 * rt + 128, 1024 * ct:1024 * ct + 1024],
                WQO, q="gpsimd")
        return go

    wq_jobs = []
    for rt in range(32):
        wq_jobs.append(wq_p1_job(w1_h, rt, 0, rt))
    wq_jobs.append(wq_mid_job(32, MEAN_W1, Q1))
    for rt in range(32):
        wq_jobs.append(wq_p2_job(w1_h, t1q_h, rt, 0))
    for rt in range(8):
        for ct in range(4):
            wq_jobs.append(wq_p1_job(w2_h, rt, ct, 4 * rt + ct))
    wq_jobs.append(wq_mid_job(32, MEAN_W2, Q2))
    for rt in range(8):
        for ct in range(4):
            wq_jobs.append(wq_p2_job(w2_h, t2q_h, rt, ct))
    wq_state = [0]

    def wq_drip(n):
        k = wq_state[0]
        for job in wq_jobs[k:k + n]:
            job()
        wq_state[0] = min(k + n, len(wq_jobs))

    attn_stack = contextlib.ExitStack()
    sba = mk_sb(attn_stack)
    MAA = {f: sba("maa" + f, (128, NCC)) for f in ("x", "w", "k", "v", "r")}
    DEC = sba("decpp", (128, NCC))
    UPP = sba("upp", (128, NCC))
    LNW = sba("lnwpp", (128, NCC))
    LNB = sba("lnbpp", (128, NCC))
    MW1 = sba("mw1t", (128, NCC * 128))
    MW2 = sba("mw2t", (128, NCC * 128))
    DW1 = sba("dw1t", (128, NCC * 64))
    DW2 = sba("dw2t", (128, NCC * 128))
    BIASR = sba("biasr", (1, C))
    XTI = sba("xti", (128, C))
    QA = sba("qa", (128, BT))
    N1B = sba("n1b", (128, C), BF16)
    XN_B = sba("xn_b", (128, NCC * BT))
    XX_B = sba("xx_b", (128, NCC * BT))
    XF_B = sba("xf_b", (128, NCC * BT))
    XPC = sba("xpc", (128, NCC))
    XD = sba("xdb", (128, BT))
    CB = sba("cbuf", (128, NCC * BT))
    SCB = sba("scb", (128, NCC * BT))
    KT_B = sba("kt_b", (128, NCC * BT))
    RT_B = sba("rt_b", (128, NCC * BT))
    V_B = sba("v_b", (128, BCH * C))
    KHTM = sba("khtm", (128, NCC * 128))
    DTMB = sba("dtmb", (128, BCH * 16))
    GM = sba("gm", (128, NH * 128))
    SBF = [sba("s0b", (128, NCC * HD)), sba("s1b", (128, NCC * HD))]
    X1 = V_B[:, 0:BT]
    KHF = XF_B[:, 0:NCC * 128]
    YS = sba("ys", (128, C))
    GN1, GN2 = sba("gn1", (128, 16)), sba("gn2", (128, 16))
    GMU, GRS = sba("gmu", (128, 16)), sba("grs", (128, 16))
    YNFB = XX_B
    WST = [sba("wstr0", (128, C)), sba("wstr1", (128, C))]

    VST8 = sba("vst8", (64, 128))
    vec_targets = [(maa_h["x"], MAA["x"]), (maa_h["w"], MAA["w"]),
                   (maa_h["k"], MAA["k"]), (maa_h["v"], MAA["v"]),
                   (maa_h["r"], MAA["r"]), (dec_h, DEC), (lnw_h, LNW),
                   (lnb_h, LNB)]
    for src_h, tgt in vec_targets:
        dma(VST8[0:8, :], src_h.rearrange("(a b) -> a b", a=8))
        ten(lambda e: e.transpose(bank(2)[:, 0:8], VST8[0:8, :], IDENT[0:8, 0:8]))
        act(lambda e, tgt=tgt: e.activation(tgt, bank(2)[:, 0:8], AF.Copy,
                                            bias=0.0, scale=1.0))
    dma(VST8[0:16, 0:64], faaaa_h)
    ten(lambda e: e.transpose(bank(2)[0:64, 0:16], VST8[0:16, 0:64],
                              IDENT[0:16, 0:16]))
    act(lambda e: e.activation(VST8[0:64, 0:16], bank(2)[0:64, 0:16],
                               AF.Copy, bias=0.0, scale=1.0))
    for hp in range(2):
        vec(lambda e, hp=hp: e.tensor_copy(
            UPP[64 * hp:64 * hp + 64, :],
            VST8[0:64, 0:16].rearrange("p (c two) -> p c two", two=2)[:, :, hp]))
    dma(MW1.rearrange("p (a l) -> p a l", a=NCC),
        mw1_h.rearrange("(a b) l -> b a l", b=128))
    for f in range(4):
        dma(MW2[32 * f:32 * f + 32, :], mw2_h[f])
    dma(DW1.rearrange("p (a l) -> p a l", a=NCC),
        dw1_h.rearrange("(a b) l -> b a l", b=128))
    dma(DW2[0:64, :], dw2_h)
    vec(lambda e: e.memset(XPC, 0.0))
    vec(lambda e: e.memset(SBF[0], 0.0))

    for cc in range(NCC):
        wst = WST[cc % 2]
        dma(wst, wo_h[128 * cc:128 * cc + 128, :])
        for half in range(2):
            ten(lambda e, cc=cc, half=half, wst=wst: e.matmul(
                bank(half)[0:1, :], LNB[:, cc:cc + 1],
                wst[:, 512 * half:512 * half + 512],
                start=(cc == 0), stop=(cc == NCC - 1)))
    for half in range(2):
        act(lambda e, half=half: e.activation(
            BIASR[:, 512 * half:512 * half + 512], bank(half)[0:1, :], AF.Copy,
            bias=0.0, scale=1.0))

    def build_xf(f, mf):
        for cc in range(NCC):
            bk = 2 + cc % 2
            ten(lambda e, f=f, cc=cc, bk=bk: e.matmul(
                bank(bk)[:, 0:BT], MW2[32 * f:32 * f + 32, 128 * cc:128 * cc + 128],
                X1[32 * f:32 * f + 32, :], start=True, stop=True,
                tile_position=(32 * f, 0)))
            sl = slice(BT * cc, BT * cc + BT)
            vec(lambda e, cc=cc, sl=sl, mf=mf, bk=bk: e.scalar_tensor_tensor(
                QA[:, 0:BT], bank(bk)[:, 0:BT], MAA[mf][:, cc:cc + 1], XX_B[:, sl],
                ALU.add, ALU.mult))
            vec(lambda e, sl=sl: e.tensor_add(XF_B[:, sl], QA[:, 0:BT],
                                              XN_B[:, sl]))

    for b in range(min(NB if _KPH >= 2 else 0, _KBLK)):
        for tl in range(BCH):
            gt = b * BCH + tl
            dma(XTI, x_h[128 * gt:128 * gt + 128, :])
            vec(lambda e: e.reduce_sum(S1c, XTI, axis=AX.X))
            act(lambda e: e.activation(XF_B[:, C:2 * C], XTI, AF.Square,
                                       bias=0.0, scale=1.0, accum_out=S2c))
            vec(lambda e: e.tensor_single_scalar(MUc, S1c, 1.0 / C, ALU.mult))
            vec(lambda e: e.tensor_single_scalar(S2c, S2c, 1.0 / C, ALU.mult))
            vec(lambda e: e.tensor_mul(RSc, MUc, MUc))
            vec(lambda e: e.tensor_sub(S2c, S2c, RSc))
            vec(lambda e: e.tensor_single_scalar(S2c, S2c, LN_EPS, ALU.add))
            act(lambda e: e.activation(S2c, S2c, AF.Sqrt, bias=0.0, scale=1.0))
            vec(lambda e: e.reciprocal(RSc, S2c))
            vec(lambda e: e.scalar_tensor_tensor(NBc, MUc, -1.0, RSc, ALU.mult,
                                                 ALU.mult))
            vec(lambda e: e.tensor_scalar(XTI, XTI, RSc, NBc, ALU.mult, ALU.add))
            vec(lambda e: e.reduce_max(RMX, XTI, axis=AX.X,
                                       apply_absolute_value=True))
            vec(lambda e: e.tensor_single_scalar(RMX, RMX, 1e-5, ALU.max))
            vec(lambda e: e.reciprocal(RMX, RMX))
            vec(lambda e, gt=gt: e.tensor_single_scalar(QS1[:, gt:gt + 1], RMX,
                                                        127.0, ALU.mult))
            for half in range(2):
                hsl = slice(512 * half, 512 * half + 512)
                vec(lambda e, gt=gt, hsl=hsl: e.tensor_scalar(
                    QA, XTI[:, hsl], QS1[:, gt:gt + 1], MAGIC, ALU.mult, ALU.add))
                vec(lambda e, hsl=hsl: e.tensor_single_scalar(
                    N1B[:, hsl], QA, MAGIC, ALU.subtract))
            dma(n1_h[128 * gt:128 * gt + 128, :], N1B, q="gpsimd")
            for cc in range(NCC):
                ten(lambda e, cc=cc: e.transpose(
                    bank(3)[:, 0:128], XTI[:, 128 * cc:128 * cc + 128], IDENT))
                act(lambda e, tl=tl, cc=cc: e.activation(
                    XN_B[:, BT * cc + 128 * tl:BT * cc + 128 * tl + 128],
                    bank(3)[:, 0:128], AF.Copy, bias=0.0, scale=1.0))
        wq_drip(4)
        if _KPART < 2:
            continue
        for cc in range(NCC):
            s0 = BT * cc
            vec(lambda e, s0=s0: e.tensor_sub(XX_B[:, s0 + 1:s0 + BT],
                                              XN_B[:, s0:s0 + BT - 1],
                                              XN_B[:, s0 + 1:s0 + BT]))
            vec(lambda e, s0=s0, cc=cc: e.tensor_sub(XX_B[:, s0:s0 + 1],
                                                     XPC[:, cc:cc + 1],
                                                     XN_B[:, s0:s0 + 1]))
        vec(lambda e: e.tensor_copy(
            XPC, XN_B.rearrange("p (a w) -> p a w", a=NCC)[:, :, BT - 1]))
        for cc in range(NCC):
            sl = slice(BT * cc, BT * cc + BT)
            vec(lambda e, cc=cc, sl=sl: e.scalar_tensor_tensor(
                XF_B[:, sl], XX_B[:, sl], MAA["x"][:, cc:cc + 1], XN_B[:, sl],
                ALU.mult, ALU.add))
        for cc in range(NCC):
            ten(lambda e, cc=cc: e.matmul(bank(2)[:, 0:BT],
                                          MW1[:, 128 * cc:128 * cc + 128],
                                          XF_B[:, BT * cc:BT * cc + BT],
                                          start=(cc == 0), stop=(cc == NCC - 1)))
        act(lambda e: e.activation(X1, bank(2)[:, 0:BT], AF.Tanh, bias=0.0,
                                   scale=1.0))
        wq_drip(3)
        if _KPART < 3:
            continue
        build_xf(0, "w")
        for cc in range(NCC):
            ten(lambda e, cc=cc: e.matmul(bank(2)[0:64, 0:BT],
                                          DW1[:, 64 * cc:64 * cc + 64],
                                          XF_B[:, BT * cc:BT * cc + BT],
                                          start=(cc == 0), stop=(cc == NCC - 1)))
        act(lambda e: e.activation(XD[0:64, :], bank(2)[0:64, 0:BT], AF.Tanh,
                                   bias=0.0, scale=1.0))
        for cc in range(NCC):
            bk = 2 + cc % 2
            ten(lambda e, cc=cc, bk=bk: e.matmul(bank(bk)[:, 0:BT],
                                                 DW2[0:64, 128 * cc:128 * cc + 128],
                                                 XD[0:64, :], start=True,
                                                 stop=True))
            act(lambda e, cc=cc, bk=bk: e.activation(CB[:, BT * cc:BT * cc + BT],
                                                     bank(bk)[:, 0:BT], AF.Exp,
                                                     bias=DEC[:, cc:cc + 1],
                                                     scale=1.0))
        for cc in range(NCC):
            for ch in range(BCH):
                sl = slice(BT * cc + 128 * ch, BT * cc + 128 * ch + 128)
                vec(lambda e, sl=sl: e.tensor_tensor_scan(SCB[:, sl], CB[:, sl],
                                                          CB[:, sl], 0.0, ALU.add,
                                                          ALU.bypass))
        act(lambda e: e.activation(CB, SCB, AF.Exp, bias=0.0, scale=-1.0))
        vec(lambda e, b=b: e.tensor_copy(
            PLv[:, :, b * BCH:b * BCH + BCH],
            CB.rearrange("p (a c s) -> p a c s", a=NCC, c=BCH)[:, :, :, 127]))
        wq_drip(3)
        if _KPART < 4:
            continue
        build_xf(1, "k")
        for dchunk in range(NCC):
            wst = WST[dchunk % 2]
            dma(wst, wk_h[128 * dchunk:128 * dchunk + 128, :])
            for dout in range(NCC):
                ten(lambda e, dchunk=dchunk, dout=dout, wst=wst: e.matmul(
                    bank(dout), wst[:, 128 * dout:128 * dout + 128],
                    XF_B[:, BT * dchunk:BT * dchunk + BT],
                    start=(dchunk == 0), stop=(dchunk == NCC - 1)))
        for dout in range(NCC):
            sl = slice(BT * dout, BT * dout + BT)
            act(lambda e, sl=sl: e.activation(QA[:, 0:BT], SCB[:, sl], AF.Exp,
                                              bias=0.0, scale=1.0))
            vec(lambda e, dout=dout, sl=sl: e.tensor_mul(KT_B[:, sl],
                                                         bank(dout), QA[:, 0:BT]))
            vec(lambda e, dout=dout, sl=sl: e.tensor_scalar(
                SCB[:, sl], bank(dout), UPP[:, dout:dout + 1], None, ALU.mult))
        wq_drip(3)
        if _KPART < 5:
            continue
        build_xf(3, "r")
        for dchunk in range(NCC):
            wst = WST[dchunk % 2]
            dma(wst, wr_h[128 * dchunk:128 * dchunk + 128, :])
            for dout in range(NCC):
                ten(lambda e, dchunk=dchunk, dout=dout, wst=wst: e.matmul(
                    bank(dout), wst[:, 128 * dout:128 * dout + 128],
                    XF_B[:, BT * dchunk:BT * dchunk + BT],
                    start=(dchunk == 0), stop=(dchunk == NCC - 1)))
        for dout in range(NCC):
            s0 = BT * dout
            vec(lambda e, dout=dout, s0=s0: e.tensor_mul(
                RT_B[:, s0 + 1:s0 + BT], bank(dout)[:, 1:BT],
                CB[:, s0:s0 + BT - 1]))
            vec(lambda e, dout=dout, s0=s0: e.tensor_copy(
                RT_B[:, s0:s0 + BT].rearrange("p (c s) -> p c s", c=BCH)[:, :, 0],
                bank(dout).rearrange("p (c s) -> p c s", c=BCH)[:, :, 0]))
            vec(lambda e, dout=dout, s0=s0: e.tensor_mul(XD, bank(dout),
                                                         SCB[:, s0:s0 + BT]))
            for chd in range(BCH):
                dsl = slice(512 * dout, 512 * dout + 2)
                ten(lambda e, chd=chd, dsl=dsl: e.matmul(
                    PS[:, dsl], XD[:, 128 * chd:128 * chd + 128], IND2,
                    start=True, stop=True))
                act(lambda e, chd=chd, dout=dout, dsl=dsl: e.activation(
                    DTMB[:, 16 * chd + 2 * dout:16 * chd + 2 * dout + 2],
                    PS[:, dsl], AF.Copy, bias=0.0, scale=1.0))
        wq_drip(3)
        if _KPART < 6:
            continue
        build_xf(2, "v")
        for dchunk in range(NCC):
            wst = WST[dchunk % 2]
            dma(wst, wv_h[128 * dchunk:128 * dchunk + 128, :])
            for ch in range(BCH):
                for half in range(2):
                    ten(lambda e, dchunk=dchunk, ch=ch, half=half, wst=wst:
                        e.matmul(
                            bank(2 * ch + half),
                            XF_B[:, BT * dchunk + 128 * ch:BT * dchunk + 128 * ch + 128],
                            wst[:, 512 * half:512 * half + 512],
                            start=(dchunk == 0), stop=(dchunk == NCC - 1)))
        for ch in range(BCH):
            for half in range(2):
                act(lambda e, ch=ch, half=half: e.activation(
                    V_B[:, C * ch + 512 * half:C * ch + 512 * half + 512],
                    bank(2 * ch + half), AF.Copy, bias=0.0, scale=1.0))
        if _KPART < 7:
            continue
        for ch in range(BCH):
            g = b * BCH + ch
            Scur, Snxt = SBF[g % 2], SBF[(g + 1) % 2]
            for cc in range(NCC):
                vec(lambda e, cc=cc, ch=ch, g=g: e.tensor_scalar(
                    KHF[:, 128 * cc:128 * cc + 128],
                    KT_B[:, BT * cc + 128 * ch:BT * cc + 128 * ch + 128],
                    PLv[:, cc, g:g + 1], None, ALU.mult))
            for cc in range(NCC):
                ten(lambda e, cc=cc: e.transpose(
                    bank(3)[:, 0:128], KHF[:, 128 * cc:128 * cc + 128], IDENT))
                act(lambda e, cc=cc: e.activation(
                    KHTM[:, 128 * cc:128 * cc + 128], bank(3)[:, 0:128], AF.Copy,
                    bias=0.0, scale=1.0))
            wq_drip(4)
            if _KWKV < 2:
                continue
            for h in [x for x in range(NH) if x % 2 == 0] + \
                     [x for x in range(NH) if x % 2 == 1]:
                cc, base = h // 2, 64 * (h % 2)
                q, s4 = 4 + h // 4, 128 * (h % 4)
                sl = slice(BT * cc + 128 * ch, BT * cc + 128 * ch + 128)
                ten(lambda e, cc=cc, base=base, q=q, s4=s4, sl=sl: e.matmul(
                    bank(q)[:, s4:s4 + 128], KT_B[base:base + 64, sl],
                    RT_B[base:base + 64, sl], start=True, stop=True,
                    tile_position=(base, 0)))
            for q in range(4):
                for hp in range(2):
                    vec(lambda e, q=q, hp=hp: e.tensor_mul(
                        GM[:, 512 * q + 256 * hp:512 * q + 256 * hp + 256],
                        bank(4 + q)[:, 256 * hp:256 * hp + 256], MSK2))
            if _KWKV < 3:
                continue
            for h in range(NH):
                vsl = slice(C * ch + HD * h, C * ch + HD * h + HD)
                ob = bank(h // 8)[:, (64 * h) % 512:(64 * h) % 512 + 64]
                ten(lambda e, h=h, ob=ob, vsl=vsl: e.matmul(
                    ob, GM[:, 128 * h:128 * h + 128], V_B[:, vsl],
                    start=(h % 8 == 0), stop=False))
            for h in [x for x in range(NH) if x % 2 == 0] + \
                     [x for x in range(NH) if x % 2 == 1]:
                cc, base = h // 2, 64 * (h % 2)
                sl = slice(BT * cc + 128 * ch, BT * cc + 128 * ch + 128)
                ob = bank(h // 8)[:, (64 * h) % 512:(64 * h) % 512 + 64]
                ten(lambda e, cc=cc, base=base, sl=sl, ob=ob, h=h, Scur=Scur:
                    e.matmul(ob, RT_B[base:base + 64, sl],
                             Scur[base:base + 64, HD * (h // 2):HD * (h // 2) + HD],
                             start=False, stop=(h in (7, 15)),
                             tile_position=(base, 0)))
            vec(lambda e, ch=ch: e.tensor_mul(
                GM[:, 0:C].rearrange("p (h e) -> p h e", h=NH),
                V_B[:, C * ch:C * ch + C].rearrange("p (h e) -> p h e", h=NH),
                DTMB[:, 16 * ch:16 * ch + 16].unsqueeze(2)
                .broadcast_to([128, NH, HD])))
            vec(lambda e: e.tensor_add(YS[:, 0:512], GM[:, 0:512], bank(0)))
            vec(lambda e: e.tensor_add(YS[:, 512:1024], GM[:, 512:1024], bank(1)))
            if _KWKV < 4:
                continue
            for h in [x for x in range(NH) if x % 2 == 0] + \
                     [x for x in range(NH) if x % 2 == 1]:
                vsl = slice(C * ch + HD * h, C * ch + HD * h + HD)
                ob = bank(2)[64 * (h % 2):64 * (h % 2) + 64,
                             HD * (h // 2):HD * (h // 2) + HD]
                ten(lambda e, h=h, ob=ob, vsl=vsl: e.matmul(
                    ob, KHTM[:, HD * h:HD * h + HD], V_B[:, vsl],
                    start=True, stop=True, tile_position=(0, 64 * (h % 2))))
            for cc in range(NCC):
                vec(lambda e, cc=cc, g=g, Scur=Scur, Snxt=Snxt:
                    e.scalar_tensor_tensor(
                        Snxt[:, HD * cc:HD * cc + HD], Scur[:, HD * cc:HD * cc + HD],
                        PLv[:, cc, g:g + 1], bank(2)[:, HD * cc:HD * cc + HD],
                        ALU.mult, ALU.add))
            if _KWKV < 5:
                continue
            vec(lambda e: e.tensor_mul(GM[:, 0:C], YS, YS))
            vec(lambda e: e.reduce_sum(GN1, YS.rearrange("p (h e) -> p h e", h=NH),
                                       axis=AX.X))
            vec(lambda e: e.reduce_sum(GN2,
                                       GM[:, 0:C].rearrange("p (h e) -> p h e",
                                                            h=NH), axis=AX.X))
            vec(lambda e: e.tensor_single_scalar(GMU, GN1, 1.0 / HD, ALU.mult))
            vec(lambda e: e.tensor_single_scalar(GN2, GN2, 1.0 / HD, ALU.mult))
            vec(lambda e: e.tensor_mul(GRS, GMU, GMU))
            vec(lambda e: e.tensor_sub(GN2, GN2, GRS))
            vec(lambda e: e.tensor_single_scalar(GN2, GN2, GN_EPS, ALU.add))
            act(lambda e: e.activation(GN2, GN2, AF.Sqrt, bias=0.0, scale=1.0))
            vec(lambda e: e.reciprocal(GRS, GN2))
            vec(lambda e: e.tensor_sub(YS.rearrange("p (h e) -> p h e", h=NH),
                                       YS.rearrange("p (h e) -> p h e", h=NH),
                                       GMU.unsqueeze(2).broadcast_to([128, NH, HD])))
            vec(lambda e: e.tensor_mul(YS.rearrange("p (h e) -> p h e", h=NH),
                                       YS.rearrange("p (h e) -> p h e", h=NH),
                                       GRS.unsqueeze(2).broadcast_to([128, NH, HD])))
            if _KWKV < 6:
                continue
            for cc in range(NCC):
                ten(lambda e, cc=cc: e.transpose(
                    bank(3)[:, 0:128], YS[:, 128 * cc:128 * cc + 128], IDENT))
                act(lambda e, cc=cc, ch=ch: e.activation(
                    YNFB[:, BT * cc + 128 * ch:BT * cc + 128 * ch + 128],
                    bank(3)[:, 0:128], AF.Copy, bias=0.0, scale=LNW[:, cc:cc + 1]))
        if _KPART < 8:
            continue
        for ch in range(BCH):
            for half in range(2):
                ten(lambda e, ch=ch, half=half: e.matmul(
                    bank(2 * ch + half), ONER, BIASR[:, 512 * half:512 * half + 512],
                    start=True, stop=False))
        for cc in range(NCC):
            wst = WST[cc % 2]
            dma(wst, wo_h[128 * cc:128 * cc + 128, :])
            for ch in range(BCH):
                for half in range(2):
                    ten(lambda e, cc=cc, ch=ch, half=half, wst=wst: e.matmul(
                        bank(2 * ch + half),
                        YNFB[:, BT * cc + 128 * ch:BT * cc + 128 * ch + 128],
                        wst[:, 512 * half:512 * half + 512],
                        start=False, stop=(cc == NCC - 1)))
        for ch in range(BCH):
            gt = b * BCH + ch
            xrl = V_B[:, C * (ch % 2):C * (ch % 2) + C]
            osl = V_B[:, C * (2 + ch % 2):C * (2 + ch % 2) + C]
            dma(xrl, x_h[128 * gt:128 * gt + 128, :])
            for half in range(2):
                vec(lambda e, ch=ch, half=half, xrl=xrl, osl=osl: e.tensor_add(
                    osl[:, 512 * half:512 * half + 512],
                    xrl[:, 512 * half:512 * half + 512], bank(2 * ch + half)))
            dma(xp_h[128 * gt:128 * gt + 128, :], osl, q="gpsimd")
    wq_drip(len(wq_jobs))

    fence()
    attn_stack.close()
    wq_stack.close()

    ffn_stack = contextlib.ExitStack()
    sbf = mk_sb(ffn_stack)
    N1T = sbf("n1t", (128, NCC * T), BF16)
    T1SL = [sbf("t1sl0", (128, 512), BF16), sbf("t1sl1", (128, 512), BF16)]
    INVS1B = sbf("invs1b", (128, T))
    HB0 = sbf("hb0", (128, NHC * 258))
    HB1 = sbf("hb1", (128, NHC * 258))
    HB2 = sbf("hb2", (128, NHC * 258))
    GST = sbf("gstash", (128, NHC))
    ACC = sbf("accc", (128, 256))
    SIL = sbf("silc", (128, 256))
    HCT = sbf("hct", (128, 2 * HID))
    N2B = sbf("n2b", (128, HID), BF16)
    IVT = sbf("ivt", (16, 128))
    INVR = sbf("invr", (1, T))
    HBUF = [HB0, HB1, HB2]

    for dchunk in range(NCC):
        dma(N1T[:, T * dchunk:T * dchunk + T],
            n1_h[:, 128 * dchunk:128 * dchunk + 128], transpose=True)
    vec(lambda e: e.reciprocal(SC16, QS1))
    vec(lambda e: e.scalar_tensor_tensor(SC16, SC16, 0.5,
                                         Q1.broadcast_to([128, 16]), ALU.mult,
                                         ALU.mult))
    ten(lambda e: e.transpose(bank(0)[0:16, 0:128], SC16, IDENT))
    act(lambda e: e.activation(IVT, bank(0)[0:16, 0:128], AF.Copy, bias=0.0,
                               scale=1.0))
    dma(INVR.rearrange("p (a w) -> p a w", a=16), IVT)
    for qq in range(4):
        ten(lambda e, qq=qq: e.matmul(bank(qq % 2), ONER,
                                      INVR[:, 512 * qq:512 * qq + 512],
                                      start=True, stop=True))
        act(lambda e, qq=qq: e.activation(INVS1B[:, 512 * qq:512 * qq + 512],
                                          bank(qq % 2), AF.Copy, bias=0.0,
                                          scale=1.0))
    vec(lambda e: e.memset(GST, 0.0))

    def conv_eighth(j):
        hb, hn = HBUF[j % 3], HBUF[(j + 1) % 3]
        hbv = hb.rearrange("p (a w) -> p a w", a=NHC)
        vec(lambda e, hbv=hbv: e.tensor_copy(hbv[:, :, 0], GST))
        if j < NE - 1:
            vec(lambda e, hbv=hbv, hn=hn: e.tensor_copy(
                hbv[:, :, 257], hn.rearrange("p (a w) -> p a w", a=NHC)[:, :, 1]))
        else:
            vec(lambda e, hbv=hbv: e.memset(hbv[:, :, 257], 0.0))
        vec(lambda e, hbv=hbv: e.tensor_copy(GST, hbv[:, :, 256]))
        for hc in range(NHC):
            w0 = 258 * hc
            vec(lambda e, w0=w0, hc=hc, hb=hb: e.tensor_scalar(
                ACC, hb[:, w0:w0 + 256], TAP[0][:, hc:hc + 1], None, ALU.mult))
            vec(lambda e, w0=w0, hc=hc, hb=hb: e.scalar_tensor_tensor(
                ACC, hb[:, w0 + 1:w0 + 257], TAP[1][:, hc:hc + 1], ACC,
                ALU.mult, ALU.add))
            vec(lambda e, w0=w0, hc=hc, hb=hb: e.scalar_tensor_tensor(
                ACC, hb[:, w0 + 2:w0 + 258], TAP[2][:, hc:hc + 1], ACC,
                ALU.mult, ALU.add))
            act(lambda e, hc=hc: e.activation(SIL, ACC, AF.Silu,
                                              bias=CBB[:, hc:hc + 1], scale=1.0))
            for chh in range(2):
                pb = (hc * 2 + chh) % 2
                ten(lambda e, chh=chh, pb=pb: e.transpose(
                    bank(pb)[:, 0:128], SIL[:, 128 * chh:128 * chh + 128], IDENT))
                act(lambda e, hc=hc, chh=chh, pb=pb: e.activation(
                    HCT[:, HID * chh + 128 * hc:HID * chh + 128 * hc + 128],
                    bank(pb)[:, 0:128], AF.Copy, bias=0.0, scale=1.0))
        for chh in range(2):
            tc = 2 * j + chh
            hsl = slice(HID * chh, HID * chh + HID)
            vec(lambda e, hsl=hsl: e.reduce_max(RMX, HCT[:, hsl], axis=AX.X,
                                                apply_absolute_value=True))
            vec(lambda e: e.tensor_single_scalar(RMX, RMX, 1e-5, ALU.max))
            vec(lambda e: e.reciprocal(RMX, RMX))
            vec(lambda e, tc=tc: e.tensor_single_scalar(QS2[:, tc:tc + 1], RMX,
                                                        127.0, ALU.mult))
            vec(lambda e, hsl=hsl, tc=tc: e.tensor_scalar(
                HCT[:, hsl], HCT[:, hsl], QS2[:, tc:tc + 1], MAGIC, ALU.mult,
                ALU.add))
            vec(lambda e, hsl=hsl: e.tensor_single_scalar(N2B, HCT[:, hsl],
                                                          MAGIC, ALU.subtract))
            dma(n2_h[128 * tc:128 * tc + 128, :], N2B, q="gpsimd")

    for q in range(4 if _KPH >= 3 else 0):
        for hg in range(8):
            for dchunk in range(NCC):
                t1 = T1SL[dchunk % 2]
                dma(t1, t1q_h[512 * hg:512 * hg + 512,
                              128 * dchunk:128 * dchunk + 128], transpose=True,
                    q="scalar")
                for h4 in range(4):
                    ten(lambda e, dchunk=dchunk, h4=h4, q=q, t1=t1: e.matmul(
                        bank(h4), t1[:, 128 * h4:128 * h4 + 128],
                        N1T[:, T * dchunk + 512 * q:T * dchunk + 512 * q + 512],
                        start=(dchunk == 0), stop=(dchunk == NCC - 1)))
            for h4 in range(4):
                hc = 4 * hg + h4
                w0 = 258 * hc
                for e2 in range(2):
                    e8 = 2 * q + e2
                    vec(lambda e, h4=h4, w0=w0, e2=e2, e8=e8: e.tensor_mul(
                        HBUF[e8 % 3][:, w0 + 1:w0 + 257],
                        bank(h4)[:, 256 * e2:256 * e2 + 256],
                        INVS1B[:, ET * e8:ET * e8 + ET]))
        if q >= 1:
            conv_eighth(2 * q - 1)
        conv_eighth(2 * q)
    if _KPH >= 3:
        conv_eighth(NE - 1)

    vec(lambda e: e.reciprocal(SC16, QS2))
    vec(lambda e: e.scalar_tensor_tensor(INVS2, SC16, 0.5,
                                         Q2.broadcast_to([128, 16]), ALU.mult,
                                         ALU.mult))

    fence()
    ffn_stack.close()
    ffn2_stack = contextlib.ExitStack()
    sbg = mk_sb(ffn2_stack)
    N2TS = [sbg("n2ts0", (128, 512), BF16), sbg("n2ts1", (128, 512), BF16)]
    T2R = sbg("t2r", (128, NHC * 1024), BF16)
    XRE = [sbg("xre0", (128, C)), sbg("xre1", (128, C))]
    OST = [sbg("ost0", (128, C)), sbg("ost1", (128, C))]

    for hc in range(NHC):
        dma(T2R[:, 1024 * hc:1024 * hc + 1024],
            t2q_h[:, 128 * hc:128 * hc + 128], transpose=True)
    for tg in range(4 if _KPH >= 4 else 0):
        for hc in range(NHC):
            n2t = N2TS[hc % 2]
            dma(n2t, n2_h[512 * tg:512 * tg + 512, 128 * hc:128 * hc + 128],
                transpose=True, q="scalar")
            for t4 in range(4):
                for half in range(2):
                    ten(lambda e, hc=hc, t4=t4, half=half, n2t=n2t: e.matmul(
                        bank(2 * t4 + half), n2t[:, 128 * t4:128 * t4 + 128],
                        T2R[:, 1024 * hc + 512 * half:1024 * hc + 512 * half + 512],
                        start=(hc == 0), stop=(hc == NHC - 1)))
        for t4 in range(4):
            tc = 4 * tg + t4
            xre, ost = XRE[t4 % 2], OST[t4 % 2]
            dma(xre, xp_h[128 * tc:128 * tc + 128, :])
            for half in range(2):
                vec(lambda e, t4=t4, half=half, tc=tc, xre=xre, ost=ost:
                    e.scalar_tensor_tensor(
                        ost[:, 512 * half:512 * half + 512], bank(2 * t4 + half),
                        INVS2[:, tc:tc + 1], xre[:, 512 * half:512 * half + 512],
                        ALU.mult, ALU.add))
            dma(out_h[128 * tc:128 * tc + 128, :], ost, q="gpsimd")

    ffn2_stack.close()
    perm.close()
    return prog


def _extract_rw(nc, prog):
    with nc.Block() as blk:
        for eng in ENGS:
            ops = [p for p in prog if p[0] == eng]
            if not ops:
                continue
            getattr(blk, eng)(lambda e, ops=ops: [(p[1](e), e.nop(nofuse=True))
                                                  for p in ops])

    def keys_of(args):
        out = set()
        for a in args:
            memref = getattr(a, "memref", None)
            if memref is None:
                continue
            off = getattr(a, "offset", 0) or 0
            ap = getattr(a, "ap", None) or []
            span = 1
            for step, count in ap:
                span += abs(step) * (count - 1)
            if memref == "ps":
                b0, b1 = off // 512, (off + span - 1) // 512
                for bk in range(b0, b1 + 1):
                    out.add("ps:%d" % bk)
            else:
                b0, b1 = off // 2048, (off + span - 1) // 2048
                for bk in range(b0, b1 + 1):
                    out.add("%s:%d" % (memref, bk))
        return out

    groups = {eng: [] for eng in ENGS}
    cur = {eng: ([], []) for eng in ENGS}
    for inst in nc.all_instructions():
        eng = {"PE": "tensor", "Activation": "scalar", "DVE": "vector",
               "Pool": "gpsimd", "SP": "sync"}.get(str(inst.engine).split(".")[-1])
        if eng is None:
            continue
        if type(inst).__name__ == "InstNoOp":
            r, w = cur[eng]
            groups[eng].append((frozenset(r), frozenset(w)))
            cur[eng] = ([], [])
        else:
            r, w = cur[eng]
            r.extend(keys_of(inst.ins))
            w.extend(keys_of(inst.outs))
    idx = {eng: 0 for eng in ENGS}
    rw = []
    for p in prog:
        if p[0] == "FENCE":
            rw.append(None)
            continue
        eng = p[0]
        rw.append(groups[eng][idx[eng]])
        idx[eng] += 1
    return rw


def build_nc():
    nc1 = bass.Bass("TRN2", target_bir_lowering=False)
    prog1 = build_prog(nc1)
    rw = _extract_rw(nc1, prog1)

    nc = bass.Bass("TRN2", target_bir_lowering=False)
    prog = build_prog(nc)
    assert len(prog) == len(prog1)

    KP = 10
    tokens = []
    cnt = {e: 0 for e in ENGS}
    dmai = {e: 0 for e in ENGS}
    for p in prog:
        if p[0] == "FENCE":
            tokens.append(None)
            continue
        eng, fn, inc = p
        if inc == 16:
            j = dmai[eng]
            dmai[eng] += 1
            key = ("dma", eng, j % KP)
            val = 16 * (j // KP + 1)
            selfneed = ((key, val - 16) if j >= KP else None)
        else:
            cnt[eng] += 1
            key = ("eng", eng)
            val = cnt[eng]
            selfneed = None
        tokens.append((key, val, selfneed))

    writer = {}
    readers = {}
    fence_need = {}
    sched = []
    ti = 0
    for p, k in zip(prog, rw):
        if p[0] == "FENCE":
            sched.append(("FENCE", None, 0, None, dict(fence_need)))
            ti += 1
            continue
        eng, fn, inc = p
        tok = tokens[ti]
        ti += 1
        key, val, selfneed = tok
        rds, wrs = k
        need = {}

        def add(tk):
            if tk is None:
                return
            tkey, tval = tk
            if tkey == ("eng", eng) and eng == "tensor":
                return
            if need.get(tkey, 0) < tval:
                need[tkey] = tval
        add(selfneed)
        for rk in rds:
            add(writer.get(rk))
        for rk in wrs:
            add(writer.get(rk))
            for tkey, tval in readers.get(rk, {}).items():
                add((tkey, tval))
        sched.append((eng, fn, inc, (key, val), need))
        for rk in rds:
            readers.setdefault(rk, {})[key] = val
        for rk in wrs:
            writer[rk] = (key, val)
            readers[rk] = {key: val}
        fence_need[key] = val

    finals = dict(fence_need)

    semkeys = sorted(set(fence_need.keys()), key=str)
    with contextlib.ExitStack() as es:
        sems = {sk: es.enter_context(nc.semaphore("s_" + "_".join(map(str, sk))))
                for sk in semkeys}
        with nc.Block() as block:
            def emit(name, handle):
                water = {sk: 0 for sk in semkeys}
                for eng, fn, inc, tok, need in sched:
                    if eng == "FENCE":
                        for sk, v in need.items():
                            if sk == ("eng", name):
                                continue
                            if water[sk] < v:
                                handle.wait_ge(sems[sk], v)
                                water[sk] = v
                        continue
                    if eng != name:
                        continue
                    for sk, v in sorted(need.items(), key=str):
                        if water[sk] < v:
                            handle.wait_ge(sems[sk], v)
                            if sk != ("eng", name):
                                water[sk] = v
                    fn(handle).then_inc(sems[tok[0]], inc)
                if name == "gpsimd":
                    for sk, v in sorted(finals.items(), key=str):
                        if sk != ("eng", name) and water[sk] < v:
                            handle.wait_ge(sems[sk], v)

            block.sync(lambda e: emit("sync", e))
            block.vector(lambda e: emit("vector", e))
            block.scalar(lambda e: emit("scalar", e))
            block.tensor(lambda e: emit("tensor", e))
            block.gpsimd(lambda e: emit("gpsimd", e))

    return nc


_NC_CACHE = []


def kernel(L0_attn_query, L0_attn_key, L0_attn_value, L0_attn_time_maa_x,
           L0_attn_time_maa_w, L0_attn_time_maa_k, L0_attn_time_maa_v,
           L0_attn_time_maa_r, L0_attn_time_maa_w1, L0_attn_time_maa_w2,
           L0_attn_time_decay, L0_attn_time_decay_w1, L0_attn_time_decay_w2,
           L0_attn_time_faaaa, L0_attn_W_r, L0_attn_W_k, L0_attn_W_v, L0_attn_W_o,
           L0_attn_ln_w, L0_attn_ln_b, L0_ffn_w1, L0_ffn_conv_w, L0_ffn_conv_b,
           L0_ffn_w2):
    x = np.ascontiguousarray(np.asarray(L0_attn_query, dtype=np.float32))
    w = {
        "wr": L0_attn_W_r, "wk": L0_attn_W_k, "wv": L0_attn_W_v, "wo": L0_attn_W_o,
        "maa_x": L0_attn_time_maa_x, "maa_w": L0_attn_time_maa_w,
        "maa_k": L0_attn_time_maa_k, "maa_v": L0_attn_time_maa_v,
        "maa_r": L0_attn_time_maa_r, "mw1": L0_attn_time_maa_w1,
        "mw2": L0_attn_time_maa_w2, "dec": L0_attn_time_decay,
        "dw1": L0_attn_time_decay_w1, "dw2": L0_attn_time_decay_w2,
        "faaaa": L0_attn_time_faaaa, "lnw": L0_attn_ln_w, "lnb": L0_attn_ln_b,
        "w1": L0_ffn_w1, "cw": L0_ffn_conv_w, "cb": L0_ffn_conv_b, "w2": L0_ffn_w2,
    }
    w = {k: np.ascontiguousarray(np.asarray(v, dtype=np.float32))
         for k, v in w.items()}
    if not _NC_CACHE:
        _NC_CACHE.append(build_nc())
    nc = _NC_CACHE[0]
    in_maps = [dict(w, x=np.ascontiguousarray(x[i])) for i in range(8)]
    res = run_bass_kernel_spmd(nc, in_maps, core_ids=list(range(8)))
    return np.stack([np.asarray(r["out"], dtype=np.float32)
                     for r in res.results], axis=0)


# version 10
# speedup vs baseline: 2.4972x; 1.0323x over previous
import contextlib
import os

import numpy as np
import concourse.bass as bass
import concourse.mybir as mybir
from concourse.bass_utils import run_bass_kernel_spmd

F32 = mybir.dt.float32
BF16 = mybir.dt.bfloat16
AF = mybir.ActivationFunctionType
ALU = mybir.AluOpType
AX = mybir.AxisListType

MAGIC = 12582912.0
T, C, HID, NH, HD = 2048, 1024, 4096, 16, 64
NCC = 8
NHC = 32
NB, BT, BCH = 4, 512, 4
NE, ET = 8, 256
LN_EPS, GN_EPS = 1e-5, 0.00064
MEAN_W1 = 1.0 / (HID * C)
MEAN_W2 = 1.0 / (C * HID)
_KPH = int(os.environ.get("KPHASE", "99"))
_KBLK = int(os.environ.get("KBLK", "99"))
_KPART = int(os.environ.get("KPART", "99"))
_KWKV = int(os.environ.get("KWKV", "99"))


ENGS = ("sync", "vector", "scalar", "tensor", "gpsimd")


def build_prog(nc):

    def din(name, shape, dt=F32):
        return nc.dram_tensor(name, list(shape), dt, kind="ExternalInput").ap()

    x_h = din("x", (T, C))
    wr_h, wk_h, wv_h, wo_h = (din(n, (C, C)) for n in ("wr", "wk", "wv", "wo"))
    maa_h = {f: din("maa_" + f, (C,)) for f in ("x", "w", "k", "v", "r")}
    mw1_h = din("mw1", (C, 128))
    mw2_h = din("mw2", (4, 32, C))
    dec_h = din("dec", (C,))
    dw1_h = din("dw1", (C, 64))
    dw2_h = din("dw2", (64, C))
    faaaa_h = din("faaaa", (NH, HD))
    lnw_h, lnb_h = din("lnw", (C,)), din("lnb", (C,))
    w1_h = din("w1", (HID, C))
    cw_h = din("cw", (HID, 1, 3))
    cb_h = din("cb", (HID,))
    w2_h = din("w2", (C, HID))

    out_h = nc.dram_tensor("out", [T, C], F32, kind="ExternalOutput").ap()
    n1_h = nc.dram_tensor("n1s", [T, C], BF16, kind="Internal").ap()
    t1q_h = nc.dram_tensor("t1qs", [HID, C], BF16, kind="Internal").ap()
    n2_h = nc.dram_tensor("n2s", [T, HID], BF16, kind="Internal").ap()
    t2q_h = nc.dram_tensor("t2qs", [C, HID], BF16, kind="Internal").ap()
    xp_h = nc.dram_tensor("xps", [T, C], F32, kind="Internal").ap()

    def mk_sb(stack):
        def sb(name, shape, dt=F32):
            return stack.enter_context(nc.sbuf_tensor(name, list(shape), dt)).ap()
        return sb

    perm = contextlib.ExitStack()
    sb = mk_sb(perm)

    IDENT = sb("ident", (128, 128))
    MSK2 = sb("msk2", (128, 256))
    ONEC = sb("onec", (128, 1))
    ONER = sb("oner", (1, 128))
    IND2 = sb("ind2", (128, 2))
    NH5 = sb("nh5", (128, 1))
    PH5 = sb("ph5", (128, 1))
    TAP = [sb("tap%d" % i, (128, NHC)) for i in range(3)]
    CBB = sb("cbpp", (128, NHC))
    S1c = sb("s1c", (128, 1))
    S2c = sb("s2c", (128, 1))
    MUc = sb("muc", (128, 1))
    RSc = sb("rsc", (128, 1))
    NBc = sb("nbc", (128, 1))
    QS1 = sb("qs1", (128, 16))
    QS2 = sb("qs2", (128, 16))
    INVS2 = sb("invs2", (128, 16))
    SC16 = sb("sc16", (128, 16))
    RMX = sb("rmx", (128, 1))
    PL = sb("plx", (128, 128))
    Q1, Q2 = sb("q1", (128, 1)), sb("q2", (128, 1))
    SONE = sb("sone", (1, 1))
    PLv = PL.rearrange("p (a g) -> p a g", a=NCC)

    prog = []

    def fence():
        prog.append(("FENCE", None, 0))

    def op(eng, inc=1):
        def deco(fn):
            prog.append((eng, fn, inc))
        return deco

    def dma(out, in_, transpose=False, q="sync"):
        @op(q, 16)
        def _(e, out=out, in_=in_, tr=transpose):
            if tr:
                return e.dma_start(out=out, in_=in_, transpose=True)
            return e.dma_start(out=out, in_=in_)

    def vec(fn):
        @op("vector")
        def _(e, fn=fn):
            return fn(e)

    def act(fn):
        @op("scalar")
        def _(e, fn=fn):
            return fn(e)

    def ten(fn):
        @op("tensor")
        def _(e, fn=fn):
            return fn(e)

    def gps(fn):
        @op("gpsimd")
        def _(e, fn=fn):
            return fn(e)

    vec(lambda e: e.memset(IDENT, 1.0))
    gps(lambda e: e.affine_select(IDENT, IDENT, [[-1, 128]], ALU.is_equal,
                                  0.0, base=0, channel_multiplier=1))
    gps(lambda e: e.iota(MSK2.rearrange("p (a b) -> p a b", a=2), [[0, 2], [1, 128]],
                         base=0, channel_multiplier=-1,
                         allow_small_or_imprecise_dtypes=True))
    vec(lambda e: e.tensor_single_scalar(MSK2, MSK2, 0.0, ALU.is_gt))
    vec(lambda e: e.memset(ONEC, 1.0))
    vec(lambda e: e.memset(ONER, 1.0))
    vec(lambda e: e.memset(NH5, -0.5))
    vec(lambda e: e.memset(PH5, 0.5))
    vec(lambda e: e.memset(IND2, 0.0))
    vec(lambda e: e.memset(IND2[0:64, 0:1], 1.0))
    vec(lambda e: e.memset(IND2[64:128, 1:2], 1.0))

    PSH = nc.alloc_psum_tensor("ps", [128, 4096], F32)
    PS = PSH.ap()

    def bank(k, w=512):
        return PS[:, 512 * k:512 * k + w]

    cw_stack = contextlib.ExitStack()
    sbc = mk_sb(cw_stack)
    CWS = sbc("cws", (32, 384))
    CBS = sbc("cbs", (32, 128))
    dma(CWS, cw_h.rearrange("(a b) o t -> a (b o t)", a=32))
    dma(CBS, cb_h.rearrange("(a b) -> a b", a=32))
    for tau in range(3):
        ten(lambda e, tau=tau: e.transpose(
            bank(0, 32)[:, 0:32],
            CWS.rearrange("a (b t) -> a b t", t=3)[:, :, tau], IDENT[0:32, 0:32]))
        act(lambda e, tau=tau: e.activation(TAP[tau], bank(0, 32)[:, 0:32], AF.Copy,
                                            bias=0.0, scale=1.0))
    ten(lambda e: e.transpose(bank(0, 32)[:, 0:32], CBS, IDENT[0:32, 0:32]))
    act(lambda e: e.activation(CBB, bank(0, 32)[:, 0:32], AF.Copy, bias=0.0,
                               scale=1.0))
    fence()
    cw_stack.close()

    wq_stack = contextlib.ExitStack()
    sbw = mk_sb(wq_stack)
    WQT = [sbw("wqt0", (128, C)), sbw("wqt1", (128, C))]
    WQD = sbw("wqd", (128, C))
    WQA = [sbw("wqa0", (128, C), BF16), sbw("wqa1", (128, C), BF16)]
    WQO = sbw("wqo", (128, C), BF16)
    WSUM = sbw("wsum", (128, 32))
    SW = sbw("sw", (128, 1))

    def wq_p1_job(w_hbm, rt, ct, si):
        def go():
            wt = WQT[si % 2]
            dma(wt, w_hbm[128 * rt:128 * rt + 128, 1024 * ct:1024 * ct + 1024])
            act(lambda e, si=si, wt=wt: e.activation(
                WQD, wt, AF.Abs, bias=0.0, scale=1.0,
                accum_out=WSUM[:, si:si + 1]))
        return go

    def wq_mid_job(nsub, meanmul, qtile):
        def go():
            vec(lambda e, nsub=nsub: e.reduce_sum(RMX, WSUM[:, 0:nsub], axis=AX.X))
            ten(lambda e: e.matmul(bank(0)[0:1, 0:1], RMX, ONEC, start=True,
                                   stop=True))
            act(lambda e: e.activation(SONE, bank(0)[0:1, 0:1], AF.Copy, bias=0.0,
                                       scale=1.0))
            vec(lambda e, mm=meanmul: e.tensor_scalar(SONE, SONE, mm, 1e-5,
                                                      ALU.mult, ALU.max))
            ten(lambda e: e.matmul(bank(0)[:, 0:1], ONER, SONE, start=True,
                                   stop=True))
            act(lambda e, qtile=qtile: e.activation(qtile, bank(0)[:, 0:1],
                                                    AF.Copy, bias=0.0, scale=1.0))
            vec(lambda e, qtile=qtile: e.reciprocal(SW, qtile))
        return go

    def wq_p2_job(w_hbm, tq_hbm, rt, ct):
        def go():
            wt = WQT[(rt + ct) % 2]
            dma(wt, w_hbm[128 * rt:128 * rt + 128, 1024 * ct:1024 * ct + 1024])
            act(lambda e, wt=wt: e.activation(WQA[0], wt, AF.Sign, bias=NH5,
                                              scale=SW))
            act(lambda e, wt=wt: e.activation(WQA[1], wt, AF.Sign, bias=PH5,
                                              scale=SW))
            vec(lambda e: e.tensor_add(WQO, WQA[0], WQA[1]))
            dma(tq_hbm[128 * rt:128 * rt + 128, 1024 * ct:1024 * ct + 1024],
                WQO, q="gpsimd")
        return go

    wq_jobs = []
    for rt in range(32):
        wq_jobs.append(wq_p1_job(w1_h, rt, 0, rt))
    wq_jobs.append(wq_mid_job(32, MEAN_W1, Q1))
    for rt in range(32):
        wq_jobs.append(wq_p2_job(w1_h, t1q_h, rt, 0))
    for rt in range(8):
        for ct in range(4):
            wq_jobs.append(wq_p1_job(w2_h, rt, ct, 4 * rt + ct))
    wq_jobs.append(wq_mid_job(32, MEAN_W2, Q2))
    for rt in range(8):
        for ct in range(4):
            wq_jobs.append(wq_p2_job(w2_h, t2q_h, rt, ct))
    wq_state = [0]

    def wq_drip(n):
        k = wq_state[0]
        for job in wq_jobs[k:k + n]:
            job()
        wq_state[0] = min(k + n, len(wq_jobs))

    attn_stack = contextlib.ExitStack()
    sba = mk_sb(attn_stack)
    MAA = {f: sba("maa" + f, (128, NCC)) for f in ("x", "w", "k", "v", "r")}
    DEC = sba("decpp", (128, NCC))
    UPP = sba("upp", (128, NCC))
    LNW = sba("lnwpp", (128, NCC))
    LNB = sba("lnbpp", (128, NCC))
    MW1 = sba("mw1t", (128, NCC * 128))
    MW2 = sba("mw2t", (128, NCC * 128))
    DW1 = sba("dw1t", (128, NCC * 64))
    DW2 = sba("dw2t", (128, NCC * 128))
    BIASR = sba("biasr", (1, C))
    XTI = sba("xti", (128, C))
    QA = sba("qa", (128, BT))
    N1B = sba("n1b", (128, C), BF16)
    XN_B = sba("xn_b", (128, NCC * BT))
    XX_B = sba("xx_b", (128, NCC * BT))
    XF_B = sba("xf_b", (128, NCC * BT))
    XPC = sba("xpc", (128, NCC))
    XD = sba("xdb", (128, BT))
    CB = sba("cbuf", (128, NCC * BT))
    SCB = sba("scb", (128, NCC * BT))
    KT_B = sba("kt_b", (128, NCC * BT))
    RT_B = sba("rt_b", (128, NCC * BT))
    V_B = sba("v_b", (128, BCH * C))
    KHTM = sba("khtm", (128, NCC * 128))
    DTMB = sba("dtmb", (128, BCH * 16))
    GM = sba("gm", (128, NH * 128))
    SBF = [sba("s0b", (128, NCC * HD)), sba("s1b", (128, NCC * HD))]
    X1 = V_B[:, 0:BT]
    KHF = XF_B[:, 0:NCC * 128]
    YS = sba("ys", (128, C))
    GN1, GN2 = sba("gn1", (128, 16)), sba("gn2", (128, 16))
    GMU, GRS = sba("gmu", (128, 16)), sba("grs", (128, 16))
    YNFB = XX_B
    WST = [sba("wstr0", (128, C)), sba("wstr1", (128, C))]

    VST8 = sba("vst8", (64, 128))
    vec_targets = [(maa_h["x"], MAA["x"]), (maa_h["w"], MAA["w"]),
                   (maa_h["k"], MAA["k"]), (maa_h["v"], MAA["v"]),
                   (maa_h["r"], MAA["r"]), (dec_h, DEC), (lnw_h, LNW),
                   (lnb_h, LNB)]
    for src_h, tgt in vec_targets:
        dma(VST8[0:8, :], src_h.rearrange("(a b) -> a b", a=8))
        ten(lambda e: e.transpose(bank(2)[:, 0:8], VST8[0:8, :], IDENT[0:8, 0:8]))
        act(lambda e, tgt=tgt: e.activation(tgt, bank(2)[:, 0:8], AF.Copy,
                                            bias=0.0, scale=1.0))
    dma(VST8[0:16, 0:64], faaaa_h)
    ten(lambda e: e.transpose(bank(2)[0:64, 0:16], VST8[0:16, 0:64],
                              IDENT[0:16, 0:16]))
    act(lambda e: e.activation(VST8[0:64, 0:16], bank(2)[0:64, 0:16],
                               AF.Copy, bias=0.0, scale=1.0))
    for hp in range(2):
        vec(lambda e, hp=hp: e.tensor_copy(
            UPP[64 * hp:64 * hp + 64, :],
            VST8[0:64, 0:16].rearrange("p (c two) -> p c two", two=2)[:, :, hp]))
    dma(MW1.rearrange("p (a l) -> p a l", a=NCC),
        mw1_h.rearrange("(a b) l -> b a l", b=128))
    for f in range(4):
        dma(MW2[32 * f:32 * f + 32, :], mw2_h[f])
    dma(DW1.rearrange("p (a l) -> p a l", a=NCC),
        dw1_h.rearrange("(a b) l -> b a l", b=128))
    dma(DW2[0:64, :], dw2_h)
    vec(lambda e: e.memset(XPC, 0.0))
    vec(lambda e: e.memset(SBF[0], 0.0))

    for cc in range(NCC):
        wst = WST[cc % 2]
        dma(wst, wo_h[128 * cc:128 * cc + 128, :])
        for half in range(2):
            ten(lambda e, cc=cc, half=half, wst=wst: e.matmul(
                bank(half)[0:1, :], LNB[:, cc:cc + 1],
                wst[:, 512 * half:512 * half + 512],
                start=(cc == 0), stop=(cc == NCC - 1)))
    for half in range(2):
        act(lambda e, half=half: e.activation(
            BIASR[:, 512 * half:512 * half + 512], bank(half)[0:1, :], AF.Copy,
            bias=0.0, scale=1.0))

    def build_xf(f, mf):
        for cc in range(NCC):
            bk = 2 + cc % 2
            ten(lambda e, f=f, cc=cc, bk=bk: e.matmul(
                bank(bk)[:, 0:BT], MW2[32 * f:32 * f + 32, 128 * cc:128 * cc + 128],
                X1[32 * f:32 * f + 32, :], start=True, stop=True,
                tile_position=(32 * f, 0)))
            sl = slice(BT * cc, BT * cc + BT)
            vec(lambda e, cc=cc, sl=sl, mf=mf, bk=bk: e.scalar_tensor_tensor(
                QA[:, 0:BT], bank(bk)[:, 0:BT], MAA[mf][:, cc:cc + 1], XX_B[:, sl],
                ALU.add, ALU.mult))
            vec(lambda e, sl=sl: e.tensor_add(XF_B[:, sl], QA[:, 0:BT],
                                              XN_B[:, sl]))

    for b in range(min(NB if _KPH >= 2 else 0, _KBLK)):
        for tl in range(BCH):
            gt = b * BCH + tl
            dma(XTI, x_h[128 * gt:128 * gt + 128, :])
            vec(lambda e: e.reduce_sum(S1c, XTI, axis=AX.X))
            act(lambda e: e.activation(XF_B[:, C:2 * C], XTI, AF.Square,
                                       bias=0.0, scale=1.0, accum_out=S2c))
            vec(lambda e: e.tensor_single_scalar(MUc, S1c, 1.0 / C, ALU.mult))
            vec(lambda e: e.tensor_single_scalar(S2c, S2c, 1.0 / C, ALU.mult))
            vec(lambda e: e.tensor_mul(RSc, MUc, MUc))
            vec(lambda e: e.tensor_sub(S2c, S2c, RSc))
            vec(lambda e: e.tensor_single_scalar(S2c, S2c, LN_EPS, ALU.add))
            act(lambda e: e.activation(S2c, S2c, AF.Sqrt, bias=0.0, scale=1.0))
            vec(lambda e: e.reciprocal(RSc, S2c))
            vec(lambda e: e.scalar_tensor_tensor(NBc, MUc, -1.0, RSc, ALU.mult,
                                                 ALU.mult))
            vec(lambda e: e.tensor_scalar(XTI, XTI, RSc, NBc, ALU.mult, ALU.add))
            vec(lambda e: e.reduce_max(RMX, XTI, axis=AX.X,
                                       apply_absolute_value=True))
            vec(lambda e: e.tensor_single_scalar(RMX, RMX, 1e-5, ALU.max))
            vec(lambda e: e.reciprocal(RMX, RMX))
            vec(lambda e, gt=gt: e.tensor_single_scalar(QS1[:, gt:gt + 1], RMX,
                                                        127.0, ALU.mult))
            for half in range(2):
                hsl = slice(512 * half, 512 * half + 512)
                vec(lambda e, gt=gt, hsl=hsl: e.tensor_scalar(
                    QA, XTI[:, hsl], QS1[:, gt:gt + 1], MAGIC, ALU.mult, ALU.add))
                vec(lambda e, hsl=hsl: e.tensor_single_scalar(
                    N1B[:, hsl], QA, MAGIC, ALU.subtract))
            dma(n1_h[128 * gt:128 * gt + 128, :], N1B, q="gpsimd")
            for cc in range(NCC):
                ten(lambda e, cc=cc: e.transpose(
                    bank(3)[:, 0:128], XTI[:, 128 * cc:128 * cc + 128], IDENT))
                act(lambda e, tl=tl, cc=cc: e.activation(
                    XN_B[:, BT * cc + 128 * tl:BT * cc + 128 * tl + 128],
                    bank(3)[:, 0:128], AF.Copy, bias=0.0, scale=1.0))
        wq_drip(4)
        if _KPART < 2:
            continue
        for cc in range(NCC):
            s0 = BT * cc
            vec(lambda e, s0=s0: e.tensor_sub(XX_B[:, s0 + 1:s0 + BT],
                                              XN_B[:, s0:s0 + BT - 1],
                                              XN_B[:, s0 + 1:s0 + BT]))
            vec(lambda e, s0=s0, cc=cc: e.tensor_sub(XX_B[:, s0:s0 + 1],
                                                     XPC[:, cc:cc + 1],
                                                     XN_B[:, s0:s0 + 1]))
        vec(lambda e: e.tensor_copy(
            XPC, XN_B.rearrange("p (a w) -> p a w", a=NCC)[:, :, BT - 1]))
        for cc in range(NCC):
            sl = slice(BT * cc, BT * cc + BT)
            vec(lambda e, cc=cc, sl=sl: e.scalar_tensor_tensor(
                XF_B[:, sl], XX_B[:, sl], MAA["x"][:, cc:cc + 1], XN_B[:, sl],
                ALU.mult, ALU.add))
        for cc in range(NCC):
            ten(lambda e, cc=cc: e.matmul(bank(2)[:, 0:BT],
                                          MW1[:, 128 * cc:128 * cc + 128],
                                          XF_B[:, BT * cc:BT * cc + BT],
                                          start=(cc == 0), stop=(cc == NCC - 1)))
        act(lambda e: e.activation(X1, bank(2)[:, 0:BT], AF.Tanh, bias=0.0,
                                   scale=1.0))
        wq_drip(3)
        if _KPART < 3:
            continue
        build_xf(0, "w")
        for cc in range(NCC):
            ten(lambda e, cc=cc: e.matmul(bank(2)[0:64, 0:BT],
                                          DW1[:, 64 * cc:64 * cc + 64],
                                          XF_B[:, BT * cc:BT * cc + BT],
                                          start=(cc == 0), stop=(cc == NCC - 1)))
        act(lambda e: e.activation(XD[0:64, :], bank(2)[0:64, 0:BT], AF.Tanh,
                                   bias=0.0, scale=1.0))
        for cc in range(NCC):
            bk = 2 + cc % 2
            ten(lambda e, cc=cc, bk=bk: e.matmul(bank(bk)[:, 0:BT],
                                                 DW2[0:64, 128 * cc:128 * cc + 128],
                                                 XD[0:64, :], start=True,
                                                 stop=True))
            act(lambda e, cc=cc, bk=bk: e.activation(CB[:, BT * cc:BT * cc + BT],
                                                     bank(bk)[:, 0:BT], AF.Exp,
                                                     bias=DEC[:, cc:cc + 1],
                                                     scale=1.0))
        for cc in range(NCC):
            for ch in range(BCH):
                sl = slice(BT * cc + 128 * ch, BT * cc + 128 * ch + 128)
                vec(lambda e, sl=sl: e.tensor_tensor_scan(SCB[:, sl], CB[:, sl],
                                                          CB[:, sl], 0.0, ALU.add,
                                                          ALU.bypass))
        act(lambda e: e.activation(CB, SCB, AF.Exp, bias=0.0, scale=-1.0))
        vec(lambda e, b=b: e.tensor_copy(
            PLv[:, :, b * BCH:b * BCH + BCH],
            CB.rearrange("p (a c s) -> p a c s", a=NCC, c=BCH)[:, :, :, 127]))
        wq_drip(3)
        if _KPART < 4:
            continue
        build_xf(1, "k")
        for dchunk in range(NCC):
            wst = WST[dchunk % 2]
            dma(wst, wk_h[128 * dchunk:128 * dchunk + 128, :])
            for dout in range(NCC):
                ten(lambda e, dchunk=dchunk, dout=dout, wst=wst: e.matmul(
                    bank(dout), wst[:, 128 * dout:128 * dout + 128],
                    XF_B[:, BT * dchunk:BT * dchunk + BT],
                    start=(dchunk == 0), stop=(dchunk == NCC - 1)))
        for dout in range(NCC):
            sl = slice(BT * dout, BT * dout + BT)
            act(lambda e, sl=sl: e.activation(QA[:, 0:BT], SCB[:, sl], AF.Exp,
                                              bias=0.0, scale=1.0))
            vec(lambda e, dout=dout, sl=sl: e.tensor_mul(KT_B[:, sl],
                                                         bank(dout), QA[:, 0:BT]))
            vec(lambda e, dout=dout, sl=sl: e.tensor_scalar(
                SCB[:, sl], bank(dout), UPP[:, dout:dout + 1], None, ALU.mult))
        wq_drip(3)
        if _KPART < 5:
            continue
        build_xf(3, "r")
        for dchunk in range(NCC):
            wst = WST[dchunk % 2]
            dma(wst, wr_h[128 * dchunk:128 * dchunk + 128, :])
            for dout in range(NCC):
                ten(lambda e, dchunk=dchunk, dout=dout, wst=wst: e.matmul(
                    bank(dout), wst[:, 128 * dout:128 * dout + 128],
                    XF_B[:, BT * dchunk:BT * dchunk + BT],
                    start=(dchunk == 0), stop=(dchunk == NCC - 1)))
        for dout in range(NCC):
            s0 = BT * dout
            vec(lambda e, dout=dout, s0=s0: e.tensor_mul(
                RT_B[:, s0 + 1:s0 + BT], bank(dout)[:, 1:BT],
                CB[:, s0:s0 + BT - 1]))
            vec(lambda e, dout=dout, s0=s0: e.tensor_copy(
                RT_B[:, s0:s0 + BT].rearrange("p (c s) -> p c s", c=BCH)[:, :, 0],
                bank(dout).rearrange("p (c s) -> p c s", c=BCH)[:, :, 0]))
            vec(lambda e, dout=dout, s0=s0: e.tensor_mul(XD, bank(dout),
                                                         SCB[:, s0:s0 + BT]))
            for chd in range(BCH):
                dsl = slice(512 * dout, 512 * dout + 2)
                ten(lambda e, chd=chd, dsl=dsl: e.matmul(
                    PS[:, dsl], XD[:, 128 * chd:128 * chd + 128], IND2,
                    start=True, stop=True))
                act(lambda e, chd=chd, dout=dout, dsl=dsl: e.activation(
                    DTMB[:, 16 * chd + 2 * dout:16 * chd + 2 * dout + 2],
                    PS[:, dsl], AF.Copy, bias=0.0, scale=1.0))
        wq_drip(3)
        if _KPART < 6:
            continue
        build_xf(2, "v")
        for dchunk in range(NCC):
            wst = WST[dchunk % 2]
            dma(wst, wv_h[128 * dchunk:128 * dchunk + 128, :])
            for ch in range(BCH):
                for half in range(2):
                    ten(lambda e, dchunk=dchunk, ch=ch, half=half, wst=wst:
                        e.matmul(
                            bank(2 * ch + half),
                            XF_B[:, BT * dchunk + 128 * ch:BT * dchunk + 128 * ch + 128],
                            wst[:, 512 * half:512 * half + 512],
                            start=(dchunk == 0), stop=(dchunk == NCC - 1)))
        for ch in range(BCH):
            for half in range(2):
                act(lambda e, ch=ch, half=half: e.activation(
                    V_B[:, C * ch + 512 * half:C * ch + 512 * half + 512],
                    bank(2 * ch + half), AF.Copy, bias=0.0, scale=1.0))
        if _KPART < 7:
            continue
        for ch in range(BCH):
            g = b * BCH + ch
            Scur, Snxt = SBF[g % 2], SBF[(g + 1) % 2]
            for cc in range(NCC):
                vec(lambda e, cc=cc, ch=ch, g=g: e.tensor_scalar(
                    KHF[:, 128 * cc:128 * cc + 128],
                    KT_B[:, BT * cc + 128 * ch:BT * cc + 128 * ch + 128],
                    PLv[:, cc, g:g + 1], None, ALU.mult))
            for cc in range(NCC):
                ten(lambda e, cc=cc: e.transpose(
                    bank(3)[:, 0:128], KHF[:, 128 * cc:128 * cc + 128], IDENT))
                act(lambda e, cc=cc: e.activation(
                    KHTM[:, 128 * cc:128 * cc + 128], bank(3)[:, 0:128], AF.Copy,
                    bias=0.0, scale=1.0))
            wq_drip(4)
            if _KWKV < 2:
                continue
            for h in [x for x in range(NH) if x % 2 == 0] + \
                     [x for x in range(NH) if x % 2 == 1]:
                cc, base = h // 2, 64 * (h % 2)
                q, s4 = 4 + h // 4, 128 * (h % 4)
                sl = slice(BT * cc + 128 * ch, BT * cc + 128 * ch + 128)
                ten(lambda e, cc=cc, base=base, q=q, s4=s4, sl=sl: e.matmul(
                    bank(q)[:, s4:s4 + 128], KT_B[base:base + 64, sl],
                    RT_B[base:base + 64, sl], start=True, stop=True,
                    tile_position=(base, 0)))
            for q in range(4):
                for hp in range(2):
                    vec(lambda e, q=q, hp=hp: e.tensor_mul(
                        GM[:, 512 * q + 256 * hp:512 * q + 256 * hp + 256],
                        bank(4 + q)[:, 256 * hp:256 * hp + 256], MSK2))
            if _KWKV < 3:
                continue
            for h in range(NH):
                vsl = slice(C * ch + HD * h, C * ch + HD * h + HD)
                ob = bank(h // 8)[:, (64 * h) % 512:(64 * h) % 512 + 64]
                ten(lambda e, h=h, ob=ob, vsl=vsl: e.matmul(
                    ob, GM[:, 128 * h:128 * h + 128], V_B[:, vsl],
                    start=(h % 8 == 0), stop=False))
            for h in [x for x in range(NH) if x % 2 == 0] + \
                     [x for x in range(NH) if x % 2 == 1]:
                cc, base = h // 2, 64 * (h % 2)
                sl = slice(BT * cc + 128 * ch, BT * cc + 128 * ch + 128)
                ob = bank(h // 8)[:, (64 * h) % 512:(64 * h) % 512 + 64]
                ten(lambda e, cc=cc, base=base, sl=sl, ob=ob, h=h, Scur=Scur:
                    e.matmul(ob, RT_B[base:base + 64, sl],
                             Scur[base:base + 64, HD * (h // 2):HD * (h // 2) + HD],
                             start=False, stop=(h in (7, 15)),
                             tile_position=(base, 0)))
            vec(lambda e, ch=ch: e.tensor_mul(
                GM[:, 0:C].rearrange("p (h e) -> p h e", h=NH),
                V_B[:, C * ch:C * ch + C].rearrange("p (h e) -> p h e", h=NH),
                DTMB[:, 16 * ch:16 * ch + 16].unsqueeze(2)
                .broadcast_to([128, NH, HD])))
            vec(lambda e: e.tensor_add(YS[:, 0:512], GM[:, 0:512], bank(0)))
            vec(lambda e: e.tensor_add(YS[:, 512:1024], GM[:, 512:1024], bank(1)))
            if _KWKV < 4:
                continue
            for h in [x for x in range(NH) if x % 2 == 0] + \
                     [x for x in range(NH) if x % 2 == 1]:
                vsl = slice(C * ch + HD * h, C * ch + HD * h + HD)
                ob = bank(2)[64 * (h % 2):64 * (h % 2) + 64,
                             HD * (h // 2):HD * (h // 2) + HD]
                ten(lambda e, h=h, ob=ob, vsl=vsl: e.matmul(
                    ob, KHTM[:, HD * h:HD * h + HD], V_B[:, vsl],
                    start=True, stop=True, tile_position=(0, 64 * (h % 2))))
            for cc in range(NCC):
                vec(lambda e, cc=cc, g=g, Scur=Scur, Snxt=Snxt:
                    e.scalar_tensor_tensor(
                        Snxt[:, HD * cc:HD * cc + HD], Scur[:, HD * cc:HD * cc + HD],
                        PLv[:, cc, g:g + 1], bank(2)[:, HD * cc:HD * cc + HD],
                        ALU.mult, ALU.add))
            if _KWKV < 5:
                continue
            vec(lambda e: e.tensor_mul(GM[:, 0:C], YS, YS))
            vec(lambda e: e.reduce_sum(GN1, YS.rearrange("p (h e) -> p h e", h=NH),
                                       axis=AX.X))
            vec(lambda e: e.reduce_sum(GN2,
                                       GM[:, 0:C].rearrange("p (h e) -> p h e",
                                                            h=NH), axis=AX.X))
            vec(lambda e: e.tensor_single_scalar(GMU, GN1, 1.0 / HD, ALU.mult))
            vec(lambda e: e.tensor_single_scalar(GN2, GN2, 1.0 / HD, ALU.mult))
            vec(lambda e: e.tensor_mul(GRS, GMU, GMU))
            vec(lambda e: e.tensor_sub(GN2, GN2, GRS))
            vec(lambda e: e.tensor_single_scalar(GN2, GN2, GN_EPS, ALU.add))
            act(lambda e: e.activation(GN2, GN2, AF.Sqrt, bias=0.0, scale=1.0))
            vec(lambda e: e.reciprocal(GRS, GN2))
            vec(lambda e: e.tensor_sub(YS.rearrange("p (h e) -> p h e", h=NH),
                                       YS.rearrange("p (h e) -> p h e", h=NH),
                                       GMU.unsqueeze(2).broadcast_to([128, NH, HD])))
            vec(lambda e: e.tensor_mul(YS.rearrange("p (h e) -> p h e", h=NH),
                                       YS.rearrange("p (h e) -> p h e", h=NH),
                                       GRS.unsqueeze(2).broadcast_to([128, NH, HD])))
            if _KWKV < 6:
                continue
            for cc in range(NCC):
                ten(lambda e, cc=cc: e.transpose(
                    bank(3)[:, 0:128], YS[:, 128 * cc:128 * cc + 128], IDENT))
                act(lambda e, cc=cc, ch=ch: e.activation(
                    YNFB[:, BT * cc + 128 * ch:BT * cc + 128 * ch + 128],
                    bank(3)[:, 0:128], AF.Copy, bias=0.0, scale=LNW[:, cc:cc + 1]))
        if _KPART < 8:
            continue
        for ch in range(BCH):
            for half in range(2):
                ten(lambda e, ch=ch, half=half: e.matmul(
                    bank(2 * ch + half), ONER, BIASR[:, 512 * half:512 * half + 512],
                    start=True, stop=False))
        for cc in range(NCC):
            wst = WST[cc % 2]
            dma(wst, wo_h[128 * cc:128 * cc + 128, :])
            for ch in range(BCH):
                for half in range(2):
                    ten(lambda e, cc=cc, ch=ch, half=half, wst=wst: e.matmul(
                        bank(2 * ch + half),
                        YNFB[:, BT * cc + 128 * ch:BT * cc + 128 * ch + 128],
                        wst[:, 512 * half:512 * half + 512],
                        start=False, stop=(cc == NCC - 1)))
        for ch in range(BCH):
            gt = b * BCH + ch
            xrl = V_B[:, C * (ch % 2):C * (ch % 2) + C]
            osl = V_B[:, C * (2 + ch % 2):C * (2 + ch % 2) + C]
            dma(xrl, x_h[128 * gt:128 * gt + 128, :])
            for half in range(2):
                vec(lambda e, ch=ch, half=half, xrl=xrl, osl=osl: e.tensor_add(
                    osl[:, 512 * half:512 * half + 512],
                    xrl[:, 512 * half:512 * half + 512], bank(2 * ch + half)))
            dma(xp_h[128 * gt:128 * gt + 128, :], osl, q="gpsimd")
    wq_drip(len(wq_jobs))

    fence()
    attn_stack.close()
    wq_stack.close()

    ffn_stack = contextlib.ExitStack()
    sbf = mk_sb(ffn_stack)
    N1T = sbf("n1t", (128, NCC * T), BF16)
    T1SL = [sbf("t1sl0", (128, 1024), BF16), sbf("t1sl1", (128, 1024), BF16)]
    INVS1B = sbf("invs1b", (128, T))
    HB0 = sbf("hb0", (128, NHC * 258))
    HB1 = sbf("hb1", (128, NHC * 258))
    HB2 = sbf("hb2", (128, NHC * 258))
    GST = sbf("gstash", (128, NHC))
    ACC = sbf("accc", (128, 256))
    T0C = sbf("t0c", (128, 256))
    T2C = sbf("t2c", (128, 256))
    SIL = sbf("silc", (128, 256))
    HCT = sbf("hct", (128, 2 * HID))
    N2B = sbf("n2b", (128, HID), BF16)
    IVT = sbf("ivt", (16, 128))
    INVR = sbf("invr", (1, T))
    HBUF = [HB0, HB1, HB2]

    for dchunk in range(NCC):
        dma(N1T[:, T * dchunk:T * dchunk + T],
            n1_h[:, 128 * dchunk:128 * dchunk + 128], transpose=True)
    vec(lambda e: e.reciprocal(SC16, QS1))
    vec(lambda e: e.scalar_tensor_tensor(SC16, SC16, 0.5,
                                         Q1.broadcast_to([128, 16]), ALU.mult,
                                         ALU.mult))
    ten(lambda e: e.transpose(bank(0)[0:16, 0:128], SC16, IDENT))
    act(lambda e: e.activation(IVT, bank(0)[0:16, 0:128], AF.Copy, bias=0.0,
                               scale=1.0))
    dma(INVR.rearrange("p (a w) -> p a w", a=16), IVT)
    for qq in range(4):
        ten(lambda e, qq=qq: e.matmul(bank(qq % 2), ONER,
                                      INVR[:, 512 * qq:512 * qq + 512],
                                      start=True, stop=True))
        act(lambda e, qq=qq: e.activation(INVS1B[:, 512 * qq:512 * qq + 512],
                                          bank(qq % 2), AF.Copy, bias=0.0,
                                          scale=1.0))
    vec(lambda e: e.memset(GST, 0.0))

    def conv_eighth(j):
        hb, hn = HBUF[j % 3], HBUF[(j + 1) % 3]
        hbv = hb.rearrange("p (a w) -> p a w", a=NHC)
        vec(lambda e, hbv=hbv: e.tensor_copy(hbv[:, :, 0], GST))
        if j < NE - 1:
            vec(lambda e, hbv=hbv, hn=hn: e.tensor_copy(
                hbv[:, :, 257], hn.rearrange("p (a w) -> p a w", a=NHC)[:, :, 1]))
        else:
            vec(lambda e, hbv=hbv: e.memset(hbv[:, :, 257], 0.0))
        vec(lambda e, hbv=hbv: e.tensor_copy(GST, hbv[:, :, 256]))
        for hc in range(NHC):
            w0 = 258 * hc
            gps(lambda e, w0=w0, hc=hc, hb=hb: e.tensor_tensor(
                T0C, hb[:, w0:w0 + 256],
                TAP[0][:, hc:hc + 1].broadcast_to([128, 256]), ALU.mult))
            gps(lambda e, w0=w0, hc=hc, hb=hb: e.tensor_tensor(
                T2C, hb[:, w0 + 2:w0 + 258],
                TAP[2][:, hc:hc + 1].broadcast_to([128, 256]), ALU.mult))
            vec(lambda e, w0=w0, hc=hc, hb=hb: e.scalar_tensor_tensor(
                ACC, hb[:, w0 + 1:w0 + 257], TAP[1][:, hc:hc + 1], T0C,
                ALU.mult, ALU.add))
            vec(lambda e: e.tensor_add(ACC, ACC, T2C))
            act(lambda e, hc=hc: e.activation(SIL, ACC, AF.Silu,
                                              bias=CBB[:, hc:hc + 1], scale=1.0))
            for chh in range(2):
                pb = (hc * 2 + chh) % 2
                ten(lambda e, chh=chh, pb=pb: e.transpose(
                    bank(pb)[:, 0:128], SIL[:, 128 * chh:128 * chh + 128], IDENT))
                act(lambda e, hc=hc, chh=chh, pb=pb: e.activation(
                    HCT[:, HID * chh + 128 * hc:HID * chh + 128 * hc + 128],
                    bank(pb)[:, 0:128], AF.Copy, bias=0.0, scale=1.0))
        for chh in range(2):
            tc = 2 * j + chh
            hsl = slice(HID * chh, HID * chh + HID)
            vec(lambda e, hsl=hsl: e.reduce_max(RMX, HCT[:, hsl], axis=AX.X,
                                                apply_absolute_value=True))
            vec(lambda e: e.tensor_single_scalar(RMX, RMX, 1e-5, ALU.max))
            vec(lambda e: e.reciprocal(RMX, RMX))
            vec(lambda e, tc=tc: e.tensor_single_scalar(QS2[:, tc:tc + 1], RMX,
                                                        127.0, ALU.mult))
            vec(lambda e, hsl=hsl, tc=tc: e.tensor_scalar(
                HCT[:, hsl], HCT[:, hsl], QS2[:, tc:tc + 1], MAGIC, ALU.mult,
                ALU.add))
            vec(lambda e, hsl=hsl: e.tensor_single_scalar(N2B, HCT[:, hsl],
                                                          MAGIC, ALU.subtract))
            dma(n2_h[128 * tc:128 * tc + 128, :], N2B, q="gpsimd")

    for q in range(4 if _KPH >= 3 else 0):
        for hg2 in range(4):
            for dchunk in range(NCC):
                t1 = T1SL[dchunk % 2]
                dma(t1, t1q_h[1024 * hg2:1024 * hg2 + 1024,
                              128 * dchunk:128 * dchunk + 128], transpose=True,
                    q="scalar")
                for h8 in range(8):
                    bk = h8 % 4
                    ten(lambda e, dchunk=dchunk, h8=h8, bk=bk, q=q, t1=t1:
                        e.matmul(
                            bank(4 + bk) if h8 >= 4 else bank(bk),
                            t1[:, 128 * h8:128 * h8 + 128],
                            N1T[:, T * dchunk + 512 * q:T * dchunk + 512 * q + 512],
                            start=(dchunk == 0), stop=(dchunk == NCC - 1)))
            for h8 in range(8):
                hc = 8 * hg2 + h8
                w0 = 258 * hc
                bkk = (4 + h8 % 4) if h8 >= 4 else (h8 % 4)
                for e2 in range(2):
                    e8 = 2 * q + e2
                    vec(lambda e, bkk=bkk, w0=w0, e2=e2, e8=e8: e.tensor_mul(
                        HBUF[e8 % 3][:, w0 + 1:w0 + 257],
                        bank(bkk)[:, 256 * e2:256 * e2 + 256],
                        INVS1B[:, ET * e8:ET * e8 + ET]))
        if q >= 1:
            conv_eighth(2 * q - 1)
        conv_eighth(2 * q)
    if _KPH >= 3:
        conv_eighth(NE - 1)

    vec(lambda e: e.reciprocal(SC16, QS2))
    vec(lambda e: e.scalar_tensor_tensor(INVS2, SC16, 0.5,
                                         Q2.broadcast_to([128, 16]), ALU.mult,
                                         ALU.mult))

    fence()
    ffn_stack.close()
    ffn2_stack = contextlib.ExitStack()
    sbg = mk_sb(ffn2_stack)
    N2TS = [sbg("n2ts0", (128, 512), BF16), sbg("n2ts1", (128, 512), BF16)]
    T2R = sbg("t2r", (128, NHC * 1024), BF16)
    XRE = [sbg("xre0", (128, C)), sbg("xre1", (128, C))]
    OST = [sbg("ost0", (128, C)), sbg("ost1", (128, C))]

    for hc in range(NHC):
        dma(T2R[:, 1024 * hc:1024 * hc + 1024],
            t2q_h[:, 128 * hc:128 * hc + 128], transpose=True)
    for tg in range(4 if _KPH >= 4 else 0):
        for hc in range(NHC):
            n2t = N2TS[hc % 2]
            dma(n2t, n2_h[512 * tg:512 * tg + 512, 128 * hc:128 * hc + 128],
                transpose=True, q="scalar")
            for t4 in range(4):
                for half in range(2):
                    ten(lambda e, hc=hc, t4=t4, half=half, n2t=n2t: e.matmul(
                        bank(2 * t4 + half), n2t[:, 128 * t4:128 * t4 + 128],
                        T2R[:, 1024 * hc + 512 * half:1024 * hc + 512 * half + 512],
                        start=(hc == 0), stop=(hc == NHC - 1)))
        for t4 in range(4):
            tc = 4 * tg + t4
            xre, ost = XRE[t4 % 2], OST[t4 % 2]
            dma(xre, xp_h[128 * tc:128 * tc + 128, :])
            for half in range(2):
                vec(lambda e, t4=t4, half=half, tc=tc, xre=xre, ost=ost:
                    e.scalar_tensor_tensor(
                        ost[:, 512 * half:512 * half + 512], bank(2 * t4 + half),
                        INVS2[:, tc:tc + 1], xre[:, 512 * half:512 * half + 512],
                        ALU.mult, ALU.add))
            dma(out_h[128 * tc:128 * tc + 128, :], ost, q="gpsimd")

    ffn2_stack.close()
    perm.close()
    return prog


def _extract_rw(nc, prog):
    with nc.Block() as blk:
        for eng in ENGS:
            ops = [p for p in prog if p[0] == eng]
            if not ops:
                continue
            getattr(blk, eng)(lambda e, ops=ops: [(p[1](e), e.nop(nofuse=True))
                                                  for p in ops])

    def keys_of(args):
        out = set()
        for a in args:
            memref = getattr(a, "memref", None)
            if memref is None:
                continue
            off = getattr(a, "offset", 0) or 0
            ap = getattr(a, "ap", None) or []
            span = 1
            for step, count in ap:
                span += abs(step) * (count - 1)
            if memref == "ps":
                b0, b1 = off // 512, (off + span - 1) // 512
                for bk in range(b0, b1 + 1):
                    out.add("ps:%d" % bk)
            else:
                b0, b1 = off // 2048, (off + span - 1) // 2048
                for bk in range(b0, b1 + 1):
                    out.add("%s:%d" % (memref, bk))
        return out

    groups = {eng: [] for eng in ENGS}
    cur = {eng: ([], []) for eng in ENGS}
    for inst in nc.all_instructions():
        eng = {"PE": "tensor", "Activation": "scalar", "DVE": "vector",
               "Pool": "gpsimd", "SP": "sync"}.get(str(inst.engine).split(".")[-1])
        if eng is None:
            continue
        if type(inst).__name__ == "InstNoOp":
            r, w = cur[eng]
            groups[eng].append((frozenset(r), frozenset(w)))
            cur[eng] = ([], [])
        else:
            r, w = cur[eng]
            r.extend(keys_of(inst.ins))
            w.extend(keys_of(inst.outs))
    idx = {eng: 0 for eng in ENGS}
    rw = []
    for p in prog:
        if p[0] == "FENCE":
            rw.append(None)
            continue
        eng = p[0]
        rw.append(groups[eng][idx[eng]])
        idx[eng] += 1
    return rw


def build_nc():
    nc1 = bass.Bass("TRN2", target_bir_lowering=False)
    prog1 = build_prog(nc1)
    rw = _extract_rw(nc1, prog1)

    nc = bass.Bass("TRN2", target_bir_lowering=False)
    prog = build_prog(nc)
    assert len(prog) == len(prog1)

    KP = 10
    tokens = []
    cnt = {e: 0 for e in ENGS}
    dmai = {e: 0 for e in ENGS}
    for p in prog:
        if p[0] == "FENCE":
            tokens.append(None)
            continue
        eng, fn, inc = p
        if inc == 16:
            j = dmai[eng]
            dmai[eng] += 1
            key = ("dma", eng, j % KP)
            val = 16 * (j // KP + 1)
            selfneed = ((key, val - 16) if j >= KP else None)
        else:
            cnt[eng] += 1
            key = ("eng", eng)
            val = cnt[eng]
            selfneed = None
        tokens.append((key, val, selfneed))

    writer = {}
    readers = {}
    fence_need = {}
    sched = []
    ti = 0
    for p, k in zip(prog, rw):
        if p[0] == "FENCE":
            sched.append(("FENCE", None, 0, None, dict(fence_need)))
            ti += 1
            continue
        eng, fn, inc = p
        tok = tokens[ti]
        ti += 1
        key, val, selfneed = tok
        rds, wrs = k
        need = {}

        def add(tk):
            if tk is None:
                return
            tkey, tval = tk
            if tkey == ("eng", eng) and eng == "tensor":
                return
            if need.get(tkey, 0) < tval:
                need[tkey] = tval
        add(selfneed)
        for rk in rds:
            add(writer.get(rk))
        for rk in wrs:
            add(writer.get(rk))
            for tkey, tval in readers.get(rk, {}).items():
                add((tkey, tval))
        sched.append((eng, fn, inc, (key, val), need))
        for rk in rds:
            readers.setdefault(rk, {})[key] = val
        for rk in wrs:
            writer[rk] = (key, val)
            readers[rk] = {key: val}
        fence_need[key] = val

    finals = dict(fence_need)

    semkeys = sorted(set(fence_need.keys()), key=str)
    with contextlib.ExitStack() as es:
        sems = {sk: es.enter_context(nc.semaphore("s_" + "_".join(map(str, sk))))
                for sk in semkeys}
        with nc.Block() as block:
            def emit(name, handle):
                water = {sk: 0 for sk in semkeys}
                for eng, fn, inc, tok, need in sched:
                    if eng == "FENCE":
                        for sk, v in need.items():
                            if sk == ("eng", name):
                                continue
                            if water[sk] < v:
                                handle.wait_ge(sems[sk], v)
                                water[sk] = v
                        continue
                    if eng != name:
                        continue
                    for sk, v in sorted(need.items(), key=str):
                        if water[sk] < v:
                            handle.wait_ge(sems[sk], v)
                            if sk != ("eng", name):
                                water[sk] = v
                    fn(handle).then_inc(sems[tok[0]], inc)
                if name == "gpsimd":
                    for sk, v in sorted(finals.items(), key=str):
                        if sk != ("eng", name) and water[sk] < v:
                            handle.wait_ge(sems[sk], v)

            block.sync(lambda e: emit("sync", e))
            block.vector(lambda e: emit("vector", e))
            block.scalar(lambda e: emit("scalar", e))
            block.tensor(lambda e: emit("tensor", e))
            block.gpsimd(lambda e: emit("gpsimd", e))

    return nc


_NC_CACHE = []


def kernel(L0_attn_query, L0_attn_key, L0_attn_value, L0_attn_time_maa_x,
           L0_attn_time_maa_w, L0_attn_time_maa_k, L0_attn_time_maa_v,
           L0_attn_time_maa_r, L0_attn_time_maa_w1, L0_attn_time_maa_w2,
           L0_attn_time_decay, L0_attn_time_decay_w1, L0_attn_time_decay_w2,
           L0_attn_time_faaaa, L0_attn_W_r, L0_attn_W_k, L0_attn_W_v, L0_attn_W_o,
           L0_attn_ln_w, L0_attn_ln_b, L0_ffn_w1, L0_ffn_conv_w, L0_ffn_conv_b,
           L0_ffn_w2):
    x = np.ascontiguousarray(np.asarray(L0_attn_query, dtype=np.float32))
    w = {
        "wr": L0_attn_W_r, "wk": L0_attn_W_k, "wv": L0_attn_W_v, "wo": L0_attn_W_o,
        "maa_x": L0_attn_time_maa_x, "maa_w": L0_attn_time_maa_w,
        "maa_k": L0_attn_time_maa_k, "maa_v": L0_attn_time_maa_v,
        "maa_r": L0_attn_time_maa_r, "mw1": L0_attn_time_maa_w1,
        "mw2": L0_attn_time_maa_w2, "dec": L0_attn_time_decay,
        "dw1": L0_attn_time_decay_w1, "dw2": L0_attn_time_decay_w2,
        "faaaa": L0_attn_time_faaaa, "lnw": L0_attn_ln_w, "lnb": L0_attn_ln_b,
        "w1": L0_ffn_w1, "cw": L0_ffn_conv_w, "cb": L0_ffn_conv_b, "w2": L0_ffn_w2,
    }
    w = {k: np.ascontiguousarray(np.asarray(v, dtype=np.float32))
         for k, v in w.items()}
    if not _NC_CACHE:
        _NC_CACHE.append(build_nc())
    nc = _NC_CACHE[0]
    in_maps = [dict(w, x=np.ascontiguousarray(x[i])) for i in range(8)]
    res = run_bass_kernel_spmd(nc, in_maps, core_ids=list(range(8)))
    return np.stack([np.asarray(r["out"], dtype=np.float32)
                     for r in res.results], axis=0)
